# version 06
# baseline (speedup 1.0000x reference)
import numpy as np
import ml_dtypes
from contextlib import ExitStack

import concourse.bass as bass
import concourse.mybir as mybir
from concourse.bass_utils import run_bass_kernel_spmd

F32 = mybir.dt.float32
BF16 = mybir.dt.bfloat16
AF = mybir.ActivationFunctionType
ALU = mybir.AluOpType
AX = mybir.AxisListType
BF = ml_dtypes.bfloat16

NCORE = 8
T = 1024
NTB = 8
D = 2048
DI = 4096
NH, HD, DSTATE, CH = 64, 64, 128, 64
DPROJ = 8512
NCH = T // CH
WCOL = 256
EPS = 1e-5


class Prog:
    COMPUTE = {"tensor": "pe", "vector": "dve", "scalar": "act", "gpsimd": "pool"}
    DMAQ = {"gpsimd": "qg", "sync": "qs", "scalar": "qa"}
    ENGINES = ("tensor", "vector", "scalar", "gpsimd", "sync")

    def __init__(self, nc):
        self.nc = nc
        self.q = {e: [] for e in self.ENGINES}
        self.semnames = ["pe", "dve", "act", "pool", "qg", "qs", "qa"]
        self.sem = {}
        self.cnt = {s: 0 for s in self.semnames}
        self.waited = {e: {s: 0 for s in self.semnames} for e in self.ENGINES}

    @staticmethod
    def _flat(waits):
        out = []
        for ev in waits:
            if ev is None:
                continue
            if isinstance(ev, list):
                out.extend(Prog._flat(ev))
            else:
                out.append(ev)
        return out

    def _push(self, eng, waits, build, incsem, incn):
        w = []
        for (s, c) in self._flat(waits):
            if c > self.waited[eng][s]:
                w.append((s, c))
                self.waited[eng][s] = c
        target = None
        if incsem is not None:
            self.cnt[incsem] += incn
            target = self.cnt[incsem]
        sem = self.sem

        def run(e, w=tuple(w), build=build, incsem=incsem, target=target, incn=incn):
            for (s, c) in w:
                e.wait_ge(sem[s], c)
            inst = build(e)
            if incsem is not None and inst is not None:
                inst.then_inc(sem[incsem], incn)

        self.q[eng].append(run)
        return (incsem, target) if incsem else None

    def op(self, eng, build, waits=()):
        return self._push(eng, waits, build, self.COMPUTE[eng], 1)

    def mmg(self, builds, waits=()):
        for b in builds[:-1]:
            self._push("tensor", waits, b, None, 0)
            waits = ()
        return self._push("tensor", waits, builds[-1], "pe", 1)

    def dma(self, eng, build, waits=()):
        return self._push(eng, waits, build, self.DMAQ[eng], 16)

    def ev(self, semname):
        return (semname, self.cnt[semname])

    def barrier(self):
        evs = [(s, self.cnt[s]) for s in self.semnames if self.cnt[s] > 0]
        for e in self.ENGINES:
            self._push(e, evs, lambda _e: None, None, 0)

    def finalize(self, block):
        for ename in self.ENGINES:
            q = self.q[ename]
            deco = getattr(block, ename)

            def body(e, q=q):
                for f in q:
                    f(e)

            deco(body)


PHASES = 99
HALO_F = True
G_PARTS = 9
SERIAL_A = False
DEBUG = False
_LAST_RES = None


def build_nc():
    nc = bass.Bass("TRN2", target_bir_lowering=False)
    dt = nc.dram_tensor

    u2 = dt("u2", [2 * T, D], F32, kind="ExternalInput").ap()
    uT = dt("uT", [D, T], F32, kind="ExternalInput").ap()
    wiT = dt("wiT", [D, DPROJ], F32, kind="ExternalInput").ap()
    woT = dt("woT", [DI, D], F32, kind="ExternalInput").ap()
    wb = dt("wb", [D, D], F32, kind="ExternalInput").ap()
    as_ = dt("as_", [D, 32], F32, kind="ExternalInput").ap()
    bs = dt("bs", [16, D], F32, kind="ExternalInput").ap()
    arT = dt("arT", [D, 96], F32, kind="ExternalInput").ap()
    br6 = dt("br6", [96, D], F32, kind="ExternalInput").ap()
    wr = dt("wr", [D, 6], F32, kind="ExternalInput").ap()
    vrows = dt("vrows", [1, 8192], F32, kind="ExternalInput").ap()
    c_ones = dt("c_ones", [1, 128], F32, kind="ExternalInput").ap()
    c_idb = dt("c_idb", [128, 128], F32, kind="ExternalInput").ap()
    c_ut2 = dt("c_ut2", [128, 128], F32, kind="ExternalInput").ap()
    c_ut1 = dt("c_ut1", [128, 128], F32, kind="ExternalInput").ap()
    c_pick = dt("c_pick", [128, 128], F32, kind="ExternalInput").ap()
    c_picke = dt("c_picke", [128, 128], F32, kind="ExternalInput").ap()
    c_picko = dt("c_picko", [128, 128], F32, kind="ExternalInput").ap()
    c_tril = dt("c_tril", [128, 64], F32, kind="ExternalInput").ap()
    c_exp6 = dt("c_exp6", [6, 96], F32, kind="ExternalInput").ap()

    outT = dt("outT", [D, T], F32, kind="ExternalOutput").ap()

    KI = "ExternalOutput" if DEBUG else "Internal"
    d_un = dt("d_un", [T, D], BF16, kind=KI).ap()
    d_unh = dt("d_unh", [T, D], BF16, kind=KI).ap()
    d_xn = dt("d_xn", [T, D], BF16, kind=KI).ap()
    d_bc = dt("d_bc", [T, 2 * DSTATE], BF16, kind=KI).ap()
    d_svz = dt("d_svz", [T, DI], BF16, kind=KI).ap()
    d_yg = dt("d_yg", [T, DI], BF16, kind=KI).ap()
    d_w6 = dt("d_w6", [T, 128], BF16, kind=KI).ap()
    d_nss = dt("d_nss", [128, 4096], F32, kind=KI).ap() if DEBUG else None
    d_sc = dt("d_sc", [128, 4096], F32, kind=KI).ap() if DEBUG else None

    RO = {"lnw": 0, "lnb": 2048, "bb": 4096, "bnw": 6144, "cnw": 6272,
          "d": 6400, "dtb": 6464, "alog": 6528, "brt": 6592, "tie": 6598,
          "flag": 6634}

    with ExitStack() as ctx:
        E_ = ctx.enter_context
        A = E_(nc.sbuf_tensor("A", [128, 16384], BF16))
        B = E_(nc.sbuf_tensor("B", [128, 32768], BF16))
        C = E_(nc.sbuf_tensor("C", [128, 16384], BF16))
        NSS = E_(nc.sbuf_tensor("NSS", [128, 4096], F32))
        E = E_(nc.sbuf_tensor("E", [128, 4096], F32))
        W = E_(nc.sbuf_tensor("W", [128, 8192], BF16))
        Bn = E_(nc.sbuf_tensor("Bn", [128, 1024], BF16))
        Cn = E_(nc.sbuf_tensor("Cn", [128, 1024], BF16))
        Bh = E_(nc.sbuf_tensor("Bh", [128, 1024], BF16))
        DSC = E_(nc.sbuf_tensor("DSC", [128, 512], BF16))
        PST = E_(nc.sbuf_tensor("PST", [128, 512], BF16))
        GDS = E_(nc.sbuf_tensor("GDS", [128, 512], BF16))
        TREP = E_(nc.sbuf_tensor("TREP", [128, 1024], BF16))
        BFM = E_(nc.sbuf_tensor("BFM", [128, 1024], BF16))
        CFM = E_(nc.sbuf_tensor("CFM", [128, 1024], BF16))
        CBT = E_(nc.sbuf_tensor("CBT", [128, 512], BF16))
        IDB = E_(nc.sbuf_tensor("IDB", [128, 128], BF16))
        UT2 = E_(nc.sbuf_tensor("UT2", [128, 128], F32))
        UT1 = E_(nc.sbuf_tensor("UT1", [128, 128], F32))
        PICK = E_(nc.sbuf_tensor("PICK", [128, 128], F32))
        PICKE = E_(nc.sbuf_tensor("PICKE", [128, 128], F32))
        PICKO = E_(nc.sbuf_tensor("PICKO", [128, 128], F32))
        TRILM = E_(nc.sbuf_tensor("TRILM", [128, 64], F32))
        ONES = E_(nc.sbuf_tensor("ONES", [1, 128], F32))
        ROWB = E_(nc.sbuf_tensor("ROWB", [1, 512], F32))
        EXP6 = E_(nc.sbuf_tensor("EXP6", [6, 96], BF16))
        DD = E_(nc.sbuf_tensor("DD", [128, 64], BF16))
        DTB = E_(nc.sbuf_tensor("DTB", [128, 64], F32))
        ANEG = E_(nc.sbuf_tensor("ANEG", [128, 64], F32))
        BNWB = E_(nc.sbuf_tensor("BNWB", [128, 128], BF16))
        CNWB = E_(nc.sbuf_tensor("CNWB", [128, 128], BF16))
        STT = E_(nc.sbuf_tensor("STT", [128, 40], F32))
        TMP = E_(nc.sbuf_tensor("TMP", [128, 256], F32))
        dAm = E[:, 0:512]
        dAh = E[:, 512:1024]
        GAS = E[:, 1024:1536]
        PS = [E_(nc.psum_tensor(f"PS{i}", [128, 512], F32)) for i in range(4)]
        PSB = E_(nc.psum_tensor("PSB", [128, 2048], F32))

        p = Prog(nc)
        for s in p.semnames:
            p.sem[s] = E_(nc.semaphore(s))

        rowst = {"busy": None}

        def ldrow(name, w, off2=0):
            o = RO[name] + off2
            ev = p.dma("sync", lambda e, o=o, w=w: e.dma_start(
                out=ROWB[0:1, 0:w], in_=vrows[0:1, o:o + w]), waits=[rowst["busy"]])
            return ev

        p.dma("sync", lambda e: e.dma_start(out=ONES[:], in_=c_ones))
        p.dma("sync", lambda e: e.dma_start(out=UT2[:], in_=c_ut2))
        p.dma("sync", lambda e: e.dma_start(out=UT1[:], in_=c_ut1))
        p.dma("sync", lambda e: e.dma_start(out=PICK[:], in_=c_pick))
        p.dma("sync", lambda e: e.dma_start(out=PICKE[:], in_=c_picke))
        p.dma("sync", lambda e: e.dma_start(out=PICKO[:], in_=c_picko))
        p.dma("sync", lambda e: e.dma_start(out=TRILM[:], in_=c_tril))
        p.dma("gpsimd", lambda e: e.dma_start(out=IDB[:], in_=c_idb))
        p.dma("gpsimd", lambda e: e.dma_start(out=EXP6[:], in_=c_exp6))
        ev_qs0 = p.ev("qs")

        bc_prev = [None]

        def bcast_name(name, width, dst, off2=0, prep=None):
            evl = ldrow(name, width, off2)
            evp_ = evl
            if prep is not None:
                evp_ = prep(evl)
            evm = p.mmg([lambda e, width=width: e.matmul(
                PS[3][:, 0:width], ONES[0:1, :], ROWB[0:1, 0:width],
                start=True, stop=True)], waits=[ev_qs0, evp_, bc_prev[0]])
            rowst["busy"] = evm
            evc = p.op("vector", lambda e, dst=dst, width=width: e.tensor_copy(
                dst, PS[3][:, 0:width]), waits=[evm])
            bc_prev[0] = evc
            return evc

        def _alog_prep(evl):
            ea = p.op("scalar", lambda e: e.activation(
                out=ROWB[0:1, 0:NH], in_=ROWB[0:1, 0:NH], func=AF.Exp), waits=[evl])
            return p.op("vector", lambda e: e.tensor_scalar_mul(
                ROWB[0:1, 0:NH], ROWB[0:1, 0:NH], -1.0), waits=[ea])

        ev_dd = bcast_name("d", NH, DD[:, :])
        ev_dtb = bcast_name("dtb", NH, DTB[:, :])
        ev_aneg = bcast_name("alog", NH, ANEG[:, :], prep=_alog_prep)
        ev_bnw = bcast_name("bnw", DSTATE, BNWB[:, :])
        ev_cnw = bcast_name("cnw", DSTATE, CNWB[:, :])
        LNW = B[:, 0:2048]
        LNB = B[:, 2048:4096]
        for j in range(4):
            bcast_name("lnw", 512, LNW[:, j * 512:(j + 1) * 512], off2=j * 512)
        for j in range(4):
            ev_lnwb = bcast_name("lnb", 512, LNB[:, j * 512:(j + 1) * 512], off2=j * 512)


        if PHASES <= 1:
            with nc.Block() as block:
                p.finalize(block)
            return nc
        p.barrier()

        Ust = [E[:, 0:2048], E[:, 2048:4096]]
        XNS = [W[:, 0:2048], W[:, 2048:4096]]
        UNS = [W[:, 4096:6144], W[:, 6144:8192]]
        SQS = B[:, 4096:8192]
        ev_ub_free = [None, None]
        for i in range(16):
            halo = i < 8
            tb = i % 8
            par = i % 2
            ub, xb, nb = Ust[par], XNS[par], UNS[par]
            sq = SQS[:, par * 2048:(par + 1) * 2048]
            off = par * 16
            cc = {k: STT[:, off + j:off + j + 1] for j, k in enumerate(
                ["sq", "sum", "m", "msq", "v", "a", "r1", "w", "b", "r12", "negm"])}
            ld = p.dma("sync", lambda e, i=i, ub=ub: e.dma_start(
                out=ub, in_=u2[i * 128:(i + 1) * 128, :]),
                waits=[ev_ub_free[par]] + ([p.ev("dve"), p.ev("act"), p.ev("pool"),
                                            p.ev("qs")] if SERIAL_A else []))
            e1 = p.op("vector", lambda e, ub=ub, sq=sq, c=cc["sq"]: e.scalar_tensor_tensor(
                out=sq, in0=ub, scalar=1.0, in1=ub, op0=ALU.mult, op1=ALU.mult,
                accum_out=c), waits=[ld, ev_ub_free[par]])
            e2 = p.op("vector", lambda e, ub=ub, c=cc["sum"]: e.reduce_sum(
                c, ub, axis=AX.X), waits=[ld])
            e3 = p.op("vector", lambda e, cc=cc: e.tensor_scalar_mul(cc["m"], cc["sum"], 1.0 / D),
                      waits=[e2])
            e4 = p.op("vector", lambda e, cc=cc: e.tensor_scalar_mul(cc["msq"], cc["sq"], 1.0 / D),
                      waits=[e1])
            e5 = p.op("vector", lambda e, cc=cc: e.tensor_scalar(
                out=cc["v"], in0=cc["m"], scalar1=cc["m"], scalar2=None, op0=ALU.mult),
                waits=[e3])
            e6 = p.op("vector", lambda e, cc=cc: e.tensor_sub(cc["v"], cc["msq"], cc["v"]),
                      waits=[e4, e5])
            e7 = p.op("vector", lambda e, cc=cc: e.tensor_scalar_add(cc["a"], cc["v"], EPS),
                      waits=[e6])
            e8 = p.op("scalar", lambda e, cc=cc: e.activation(
                out=cc["r1"], in_=cc["a"], func=AF.Ln), waits=[e7])
            e8b = p.op("vector", lambda e, cc=cc: e.tensor_scalar_mul(
                cc["r1"], cc["r1"], -0.5), waits=[e8])
            e9 = p.op("scalar", lambda e, cc=cc: e.activation(
                out=cc["r1"], in_=cc["r1"], func=AF.Exp), waits=[e8b])
            e10 = p.op("vector", lambda e, cc=cc: e.reciprocal(cc["w"], cc["a"]), waits=[e7])
            e11 = p.op("vector", lambda e, cc=cc: e.tensor_scalar(
                out=cc["w"], in0=cc["w"], scalar1=cc["v"], scalar2=EPS, op0=ALU.mult,
                op1=ALU.add), waits=[e10])
            e12 = p.op("scalar", lambda e, cc=cc: e.activation(
                out=cc["b"], in_=cc["w"], func=AF.Ln), waits=[e11])
            e12b = p.op("vector", lambda e, cc=cc: e.tensor_scalar_mul(
                cc["b"], cc["b"], -0.5), waits=[e12])
            e13 = p.op("scalar", lambda e, cc=cc: e.activation(
                out=cc["b"], in_=cc["b"], func=AF.Exp), waits=[e12b])
            e15 = p.op("vector", lambda e, cc=cc: e.tensor_scalar_mul(cc["negm"], cc["m"], -1.0),
                      waits=[e3])
            exn = p.op("vector", lambda e, ub=ub, xb=xb, cc=cc: e.tensor_scalar(
                out=xb, in0=ub, scalar1=cc["negm"], scalar2=cc["r1"], op0=ALU.add,
                op1=ALU.mult), waits=[e15, e9])
            eun1 = p.op("vector", lambda e, xb=xb, nb=nb, cc=cc: e.scalar_tensor_tensor(
                out=nb, in0=xb, scalar=cc["b"], in1=LNW, op0=ALU.mult, op1=ALU.mult),
                waits=[exn, e13, ev_lnwb])
            eun2 = p.op("gpsimd", lambda e, nb=nb: e.tensor_add(nb, nb, LNB), waits=[eun1])
            frees = [e1, e2, exn, eun2, e13, e9]
            if DEBUG == 2 and i == 15:
                p.dma("sync", lambda e, ub=ub: e.dma_start(
                    out=d_sc[:, 0:2048], in_=ub), waits=[exn])
                p.dma("sync", lambda e: e.dma_start(
                    out=d_sc[:, 2048:2088], in_=STT[:, :]), waits=[exn, eun2])
            if not halo:
                exs = p.dma("sync", lambda e, tb=tb, xb=xb: e.dma_start(
                    out=d_xn[tb * 128:(tb + 1) * 128, :], in_=xb), waits=[exn])
                frees.append(exs)
            dun = d_unh if halo else d_un
            eus = p.dma("sync", lambda e, tb=tb, nb=nb, dun=dun: e.dma_start(
                out=dun[tb * 128:(tb + 1) * 128, :], in_=nb), waits=[eun2])
            frees.append(eus)
            ev_ub_free[par] = frees


        if PHASES <= 2:
            with nc.Block() as block:
                p.finalize(block)
            return nc
        p.barrier()
        for k in range(16):
            p.dma("scalar", lambda e, k=k: e.dma_start(
                out=A[:, k * 1024:(k + 1) * 1024], in_=d_un[:, k * 128:(k + 1) * 128],
                transpose=True))
            p.dma("scalar", lambda e, k=k: e.dma_start(
                out=C[:, k * 1024:(k + 1) * 1024], in_=d_unh[:, k * 128:(k + 1) * 128],
                transpose=True))
        p.barrier()

        def lhs_un(i, k):
            reg = C if i < 8 else A
            tb = i % 8
            return reg[:, k * 1024 + tb * 128: k * 1024 + (tb + 1) * 128]


        if PHASES <= 3:
            with nc.Block() as block:
                p.finalize(block)
            return nc
        NB = 320
        ev_wb = None
        for k in range(16):
            ev_wb = p.dma("gpsimd", lambda e, k=k: e.dma_start(
                out=W[:, k * NB:(k + 1) * NB],
                in_=wiT[k * 128:(k + 1) * 128, 8192:8512]))
        ev_evac3 = [None, None, None]
        for i in range(16):
            halo = i < 8
            tb = i % 8
            ps = PS[i % 3]
            builds = [lambda e, i=i, k=k, ps=ps: e.matmul(
                ps[:, 0:NB], lhs_un(i, k), W[:, k * NB:(k + 1) * NB],
                start=(k == 0), stop=(k == 15)) for k in range(16)]
            evm = p.mmg(builds, waits=[ev_wb, ev_evac3[i % 3]])
            readers = []
            bn_dst = Bh if halo else Bn
            c_bs, c_br = STT[:, 32:33], STT[:, 33:34]
            eb1 = p.op("scalar", lambda e, ps=ps, c=c_bs: e.activation(
                out=TMP[:, 0:128], in_=ps[:, 0:128], func=AF.Square, accum_out=c),
                waits=[evm, ev_evac3[i % 3]])
            eb1b = p.op("vector", lambda e: e.tensor_scalar(
                out=c_br, in0=c_bs, scalar1=1.0 / DSTATE, scalar2=EPS, op0=ALU.mult,
                op1=ALU.add), waits=[eb1])
            eb2 = p.op("scalar", lambda e: e.activation(
                out=c_br, in_=c_br, func=AF.Ln), waits=[eb1b])
            eb2b = p.op("vector", lambda e: e.tensor_scalar_mul(c_br, c_br, -0.5),
                        waits=[eb2])
            eb3 = p.op("scalar", lambda e: e.activation(
                out=c_br, in_=c_br, func=AF.Exp), waits=[eb2b])
            eb4 = p.op("vector", lambda e, ps=ps, bn_dst=bn_dst, tb=tb: e.scalar_tensor_tensor(
                out=bn_dst[:, tb * 128:(tb + 1) * 128], in0=ps[:, 0:128], scalar=c_br,
                in1=BNWB[:, :], op0=ALU.mult, op1=ALU.mult), waits=[eb3, ev_bnw])
            readers += [eb1, eb4]
            if not halo:
                c_cs, c_cr = STT[:, 34:35], STT[:, 35:36]
                ec1 = p.op("scalar", lambda e, ps=ps, c=c_cs: e.activation(
                    out=TMP[:, 0:128], in_=ps[:, 128:256], func=AF.Square, accum_out=c),
                    waits=[eb1])
                ec1b = p.op("vector", lambda e: e.tensor_scalar(
                    out=c_cr, in0=c_cs, scalar1=1.0 / DSTATE, scalar2=EPS, op0=ALU.mult,
                    op1=ALU.add), waits=[ec1])
                ec2 = p.op("scalar", lambda e: e.activation(
                    out=c_cr, in_=c_cr, func=AF.Ln), waits=[ec1b])
                ec2b = p.op("vector", lambda e: e.tensor_scalar_mul(c_cr, c_cr, -0.5),
                            waits=[ec2])
                ec3 = p.op("scalar", lambda e: e.activation(
                    out=c_cr, in_=c_cr, func=AF.Exp), waits=[ec2b])
                ec4 = p.op("vector", lambda e, ps=ps, tb=tb: e.scalar_tensor_tensor(
                    out=Cn[:, tb * 128:(tb + 1) * 128], in0=ps[:, 128:256], scalar=c_cr,
                    in1=CNWB[:, :], op0=ALU.mult, op1=ALU.mult), waits=[ec3, ev_cnw])
                readers += [ec1, ec4]
            da_dst = dAh if halo else dAm
            ed1 = p.op("vector", lambda e, ps=ps: e.tensor_add(
                TMP[:, 128:192], ps[:, 256:320], DTB[:, :]), waits=[evm, ev_dtb, eb4])
            ed2 = p.op("scalar", lambda e: e.activation(
                out=TMP[:, 192:256], in_=TMP[:, 128:192], func=AF.Abs), waits=[ed1])
            ed3 = p.op("scalar", lambda e: e.activation(
                out=TMP[:, 192:256], in_=TMP[:, 192:256], func=AF.Exp, scale=-1.0),
                waits=[ed2])
            ed4 = p.op("scalar", lambda e: e.activation(
                out=TMP[:, 192:256], in_=TMP[:, 192:256], func=AF.Ln, bias=1.0), waits=[ed3])
            ed5 = p.op("vector", lambda e: e.tensor_scalar_max(
                TMP[:, 128:192], TMP[:, 128:192], 0.0), waits=[ed2])
            ed6 = p.op("vector", lambda e: e.tensor_add(
                TMP[:, 128:192], TMP[:, 128:192], TMP[:, 192:256]), waits=[ed4, ed5])
            ed7 = p.op("vector", lambda e, da_dst=da_dst, tb=tb: e.tensor_mul(
                da_dst[:, tb * 64:(tb + 1) * 64], TMP[:, 128:192], ANEG[:, :]),
                waits=[ed6, ev_aneg])
            readers.append(ed1)
            ev_evac3[i % 3] = readers + [ed7]

        p.barrier()


        if PHASES <= 4:
            with nc.Block() as block:
                p.finalize(block)
            return nc
        prev_ed = prev_ds = prev_te = prev_to = None
        for tb in range(NTB):
            evm = p.mmg([lambda e, tb=tb: e.matmul(
                PS[0][:, 0:64], UT2[:], dAm[:, tb * 64:(tb + 1) * 64],
                start=True, stop=True)], waits=[prev_ed])
            eac = p.op("vector", lambda e: e.tensor_copy(TMP[:, 0:64], PS[0][:, 0:64]),
                       waits=[evm])
            evp = p.mmg([lambda e: e.matmul(
                PS[1][:, 0:64], PICK[:, 0:128], TMP[:, 0:64], start=True, stop=True)],
                waits=[eac])
            ed = p.op("vector", lambda e: e.tensor_sub(
                TMP[:, 64:128], PS[1][:, 0:64], TMP[:, 0:64]), waits=[evp, prev_ds])
            ds1 = p.op("scalar", lambda e, tb=tb: e.activation(
                out=DSC[:, tb * 64:(tb + 1) * 64], in_=TMP[:, 64:128], func=AF.Exp),
                waits=[ed])
            ds2 = p.op("scalar", lambda e, tb=tb: e.activation(
                out=PST[:, tb * 64:(tb + 1) * 64], in_=TMP[:, 64:128], func=AF.Exp,
                scale=-1.0), waits=[ed])
            prev_ds = [ds1, ds2]
            prev_ed = ed
            ev_e = p.mmg([lambda e: e.matmul(
                PS[2][:, 0:64], PICKE[:], TMP[:, 0:64], start=True, stop=True)],
                waits=[eac, prev_te])
            prev_te = p.op("scalar", lambda e, tb=tb: e.activation(
                out=TREP[:, (2 * tb) * 64:(2 * tb + 1) * 64], in_=PS[2][:, 0:64],
                func=AF.Exp), waits=[ev_e])
            ev_o = p.mmg([lambda e: e.matmul(
                PS[2][:, 64:128], PICKO[:], TMP[:, 0:64], start=True, stop=True)],
                waits=[eac, prev_to])
            prev_to = p.op("scalar", lambda e, tb=tb: e.activation(
                out=TREP[:, (2 * tb + 1) * 64:(2 * tb + 2) * 64], in_=PS[2][:, 64:128],
                func=AF.Exp), waits=[ev_o])

        OFFR = TMP[0:1, 128:192]
        prev = p.op("vector", lambda e: e.memset(OFFR[:, 0:64], 0.0),
                    waits=[p.ev("act")])
        for tb in range(NTB):
            evm = p.mmg([lambda e, tb=tb: e.matmul(
                PS[0][:, 0:64], UT1[:], dAh[:, tb * 64:(tb + 1) * 64],
                start=True, stop=True)], waits=[p.ev("dve"), p.ev("act")])
            evb = p.mmg([lambda e: e.matmul(
                PS[1][:, 0:64], ONES[0:1, :], OFFR[:, 0:64], start=True, stop=True)],
                waits=[prev])
            ecb = p.op("vector", lambda e: e.tensor_copy(TMP[:, 64:128], PS[1][:, 0:64]),
                       waits=[evb])
            ega = p.op("vector", lambda e, tb=tb: e.tensor_add(
                GAS[:, tb * 64:(tb + 1) * 64], PS[0][:, 0:64], TMP[:, 64:128]),
                waits=[evm, ecb])
            evr = p.mmg([lambda e, tb=tb: e.matmul(
                PS[2][0:1, 0:64], PICK[:, 64:65], GAS[:, tb * 64:(tb + 1) * 64],
                start=True, stop=True)], waits=[ega])
            prev = p.op("vector", lambda e: e.tensor_copy(OFFR[:, 0:64], PS[2][0:1, 0:64]),
                        waits=[evr])
        evb = p.mmg([lambda e: e.matmul(
            PS[1][:, 0:64], ONES[0:1, :], OFFR[:, 0:64], start=True, stop=True)],
            waits=[prev])
        egb = p.op("vector", lambda e: e.tensor_copy(TMP[:, 0:64], PS[1][:, 0:64]),
                   waits=[evb])
        ev_gds = []
        for tb in range(NTB):
            eg1 = p.op("vector", lambda e, tb=tb: e.tensor_sub(
                GAS[:, tb * 64:(tb + 1) * 64], TMP[:, 0:64], GAS[:, tb * 64:(tb + 1) * 64]),
                waits=[egb])
            ev_gds.append(p.op("scalar", lambda e, tb=tb: e.activation(
                out=GDS[:, tb * 64:(tb + 1) * 64], in_=GAS[:, tb * 64:(tb + 1) * 64],
                func=AF.Exp), waits=[eg1]))
        ev_fl = ldrow("flag", 1)
        evf = p.mmg([lambda e: e.matmul(
            PS[1][:, 0:1], ONES[0:1, :], ROWB[0:1, 0:1], start=True, stop=True)],
            waits=[egb, ev_fl])
        rowst["busy"] = evf
        evf = p.op("vector", lambda e: e.tensor_copy(STT[:, 36:37], PS[1][:, 0:1]),
                   waits=[evf])
        p.op("vector", lambda e: e.tensor_scalar(
            out=GDS[:], in0=GDS[:], scalar1=STT[:, 36:37], scalar2=None, op0=ALU.mult),
            waits=[evf] + ev_gds)

        p.barrier()


        if PHASES <= 5:
            with nc.Block() as block:
                p.finalize(block)
            return nc
        WT = [W[:, 0:4096], W[:, 4096:8192]]
        HST = [E[:, i * 128:(i + 1) * 128].bitcast(BF16) for i in range(8)]
        ev_wt = [None, None]
        ev_wfree = [None, None]

        def load_tile(col0, buf):
            evs = []
            for k in range(16):
                evs.append(p.dma("gpsimd", lambda e, k=k, col0=col0, buf=buf: e.dma_start(
                    out=WT[buf][:, k * WCOL:(k + 1) * WCOL],
                    in_=wiT[k * 128:(k + 1) * 128, col0:col0 + WCOL]),
                    waits=[ev_wfree[buf]]))
            return evs[-1]

        ev_wt[0] = load_tile(4096, 0)
        ev_evac3 = [None, None, None]
        ev_hst_free = [None, None]
        ev_fchain = None
        for j in range(16):
            buf = j % 2
            if j < 15:
                ev_wt[(j + 1) % 2] = load_tile(4096 + (j + 1) * WCOL, (j + 1) % 2)
            last_mm = None
            for i in range(16):
                halo = i < 8
                tb = i % 8
                ps = PS[i % 3]
                builds = [lambda e, i=i, k=k, ps=ps, buf=buf: e.matmul(
                    ps[:, 0:WCOL], lhs_un(i, k), WT[buf][:, k * WCOL:(k + 1) * WCOL],
                    start=(k == 0), stop=(k == 15)) for k in range(16)]
                evm = p.mmg(builds, waits=[ev_wt[buf], ev_evac3[i % 3]])
                last_mm = evm
                if halo and not HALO_F:
                    ev_evac3[i % 3] = evm
                elif halo:
                    hb = HST[tb]
                    exx = p.op("vector", lambda e, ps=ps, hb=hb, tb=tb, j=j: e.tensor_tensor(
                        out=hb[:].rearrange("p (h q) -> p h q", h=4),
                        in0=ps[:, 0:WCOL].rearrange("p (h q) -> p h q", h=4),
                        in1=GDS[:, tb * 64 + j * 4: tb * 64 + (j + 1) * 4, None]
                            .broadcast_to([128, 4, 64]),
                        op=ALU.mult), waits=[evm, ev_hst_free[0]])
                    ev_exx = exx
                    ev_evac3[i % 3] = exx
                    if tb == 7 and HALO_F != "noF":
                        fb = []
                        for tb2 in range(8):
                            fb.append(lambda e, tb2=tb2: e.matmul(
                                PSB[:, 0:256],
                                Bh[:, tb2 * 128:(tb2 + 1) * 128],
                                HST[tb2][:, :],
                                start=(tb2 == 0), stop=(tb2 == 7)))
                        ev_hst_free[0] = p.mmg(fb, waits=[p.ev("dve"), ev_fchain])
                else:
                    ecp = p.op("vector", lambda e, ps=ps, tb=tb, j=j: e.tensor_copy(
                        B[:, tb * 4096 + j * WCOL: tb * 4096 + (j + 1) * WCOL],
                        ps[:, 0:WCOL]), waits=[evm])
                    ev_evac3[i % 3] = ecp
            ev_wfree[buf] = last_mm
            if HALO_F is True:
                ev_fchain = p.op("vector", lambda e, j=j: e.tensor_copy(
                    NSS[:, j * 256:(j + 1) * 256], PSB[:, 0:256]),
                    waits=[ev_hst_free[0]])

        p.barrier()
        if DEBUG == 1:
            p.dma("sync", lambda e: e.dma_start(out=d_nss, in_=NSS[:]))
            p.op("vector", lambda e: e.tensor_copy(E[:, 0:256], DSC[:].bitcast(F32)),
                 waits=[p.ev("qs")])
            p.op("vector", lambda e: e.tensor_copy(E[:, 256:512], PST[:].bitcast(F32)))
            p.op("vector", lambda e: e.tensor_copy(E[:, 512:1024], TREP[:].bitcast(F32)))
            p.op("vector", lambda e: e.tensor_copy(E[:, 1024:1280], GDS[:].bitcast(F32)))
            p.dma("sync", lambda e: e.dma_start(out=d_sc, in_=E[:]), waits=[p.ev("dve")])
            p.barrier()

        if PHASES <= 6:
            with nc.Block() as block:
                p.finalize(block)
            return nc
        SVS = [E[:, 0:128].bitcast(BF16), E[:, 128:256].bitcast(BF16)]
        ev_wt = [None, None]
        ev_wfree = [None, None]
        ev_wt[0] = load_tile(0, 0)
        ev_sv = [None, None]
        ev_evac3 = [None, None, None]
        for j in range(16):
            buf = j % 2
            if j < 15:
                ev_wt[(j + 1) % 2] = load_tile((j + 1) * WCOL, (j + 1) % 2)
            last_mm = None
            for tb in range(8):
                i = 8 + tb
                ps = PS[tb % 3]
                sv = SVS[tb % 2]
                builds = [lambda e, i=i, k=k, ps=ps, buf=buf: e.matmul(
                    ps[:, 0:WCOL], lhs_un(i, k), WT[buf][:, k * WCOL:(k + 1) * WCOL],
                    start=(k == 0), stop=(k == 15)) for k in range(16)]
                evm = p.mmg(builds, waits=[ev_wt[buf], ev_evac3[tb % 3]])
                last_mm = evm
                esi = p.op("scalar", lambda e, ps=ps, sv=sv: e.activation(
                    out=sv, in_=ps[:, 0:WCOL], func=AF.Silu), waits=[evm, ev_sv[tb % 2]])
                ev_evac3[tb % 3] = esi
                ev_sv[tb % 2] = p.dma("sync", lambda e, tb=tb, j=j, sv=sv: e.dma_start(
                    out=d_svz[tb * 128:(tb + 1) * 128, j * WCOL:(j + 1) * WCOL], in_=sv),
                    waits=[esi])
            ev_wfree[buf] = last_mm

        for tb in range(NTB):
            p.dma("sync", lambda e, tb=tb: e.dma_start(
                out=d_bc[tb * 128:(tb + 1) * 128, 0:128], in_=Bn[:, tb * 128:(tb + 1) * 128]))
            p.dma("sync", lambda e, tb=tb: e.dma_start(
                out=d_bc[tb * 128:(tb + 1) * 128, 128:256], in_=Cn[:, tb * 128:(tb + 1) * 128]))
        p.barrier()
        p.dma("scalar", lambda e: e.dma_start(out=BFM[:], in_=d_bc[:, 0:128], transpose=True))
        p.dma("scalar", lambda e: e.dma_start(out=CFM[:], in_=d_bc[:, 128:256], transpose=True))
        p.barrier()

        for z in range(NCH):
            par = z % 2
            evm = p.mmg([lambda e, z=z, par=par: e.matmul(
                PS[0][par * 64:(par + 1) * 64, 0:64], BFM[:, z * 64:(z + 1) * 64],
                CFM[:, z * 64:(z + 1) * 64], start=True, stop=True,
                tile_position=(0, par * 64))], waits=[p.ev("dve")])
            p.op("vector", lambda e, z=z, par=par: e.tensor_mul(
                CBT[par * 64:(par + 1) * 64, (z // 2) * 64:(z // 2 + 1) * 64],
                PS[0][par * 64:(par + 1) * 64, 0:64],
                TRILM[par * 64:(par + 1) * 64, :]), waits=[evm])
        for k in range(16):
            p.dma("scalar", lambda e, k=k: e.dma_start(
                out=A[:, k * 1024:(k + 1) * 1024], in_=d_xn[:, k * 128:(k + 1) * 128],
                transpose=True))
        p.barrier()


        if PHASES <= 7:
            with nc.Block() as block:
                p.finalize(block)
            return nc
        XP = C[:, 0:4096]
        XPD = C[:, 4096:8192]
        SPZ = C[:, 8192:12288]
        NSB = C[:, 12288:16384]
        YG = W[:, 0:4096]
        SVZ = [E[:, 0:2048].bitcast(BF16), E[:, 2048:4096].bitcast(BF16)]
        ev_svld = [None, None]
        ev_svld[0] = p.dma("sync", lambda e: e.dma_start(out=SVZ[0][:], in_=d_svz[0:128, :]))
        ev_pair = [None, None, None]
        ev_sadd = None
        ev_smm = None
        ev_ymm = None
        ev_yev = [None, None]
        ev_ygdma = [None, None]
        for z in range(NCH):
            tb = z // 2
            par = z % 2
            if par == 0:
                if tb < 7:
                    ev_svld[(tb + 1) % 2] = p.dma("sync", lambda e, tb=tb: e.dma_start(
                        out=SVZ[(tb + 1) % 2][:], in_=d_svz[(tb + 1) * 128:(tb + 2) * 128, :]),
                        waits=[ev_pair[2]])
                exp_ = p.op("gpsimd", lambda e, tb=tb: e.tensor_tensor(
                    out=XP[:].rearrange("p (h q) -> p h q", h=NH),
                    in0=B[:, tb * 4096:(tb + 1) * 4096].rearrange("p (h q) -> p h q", h=NH),
                    in1=DSC[:, tb * 64:(tb + 1) * 64, None].broadcast_to([128, NH, HD]),
                    op=ALU.mult), waits=[ev_ymm, ev_smm])
                exd = p.op("gpsimd", lambda e: e.tensor_tensor(
                    out=XPD[:].rearrange("p (h q) -> p h q", h=NH),
                    in0=XP[:].rearrange("p (h q) -> p h q", h=NH),
                    in1=DD[:, :, None].broadcast_to([128, NH, HD]),
                    op=ALU.mult), waits=[exp_, ev_dd, ev_ymm])
                esp = p.op("gpsimd", lambda e, tb=tb: e.tensor_tensor(
                    out=SPZ[:].rearrange("p (h q) -> p h q", h=NH),
                    in0=SVZ[tb % 2][:].rearrange("p (h q) -> p h q", h=NH),
                    in1=PST[:, tb * 64:(tb + 1) * 64, None].broadcast_to([128, NH, HD]),
                    op=ALU.mult), waits=[ev_svld[tb % 2], ev_yev[0], ev_yev[1]])
                ev_pair = [exp_, exd, esp]
            emul = p.op("gpsimd", lambda e, z=z: e.tensor_tensor(
                out=NSS[:].rearrange("p (h q) -> p h q", h=NH),
                in0=NSS[:].rearrange("p (h q) -> p h q", h=NH),
                in1=TREP[:, z * 64:(z + 1) * 64, None].broadcast_to([128, NH, HD]),
                op=ALU.mult), waits=[ev_sadd, ev_ymm])
            ev_nsb = p.op("vector", lambda e: e.tensor_copy(NSB[:], NSS[:]), waits=[emul])
            for hh in range(2):
                builds = []
                for q in range(4):
                    cs = hh * 2048 + q * 512
                    builds.append(lambda e, par=par, z=z, cs=cs, q=q: e.matmul(
                        PSB[par * 64:(par + 1) * 64, q * 512:(q + 1) * 512],
                        CBT[par * 64:(par + 1) * 64, (z // 2) * 64:(z // 2 + 1) * 64],
                        XP[par * 64:(par + 1) * 64, cs:cs + 512],
                        start=True, stop=False, tile_position=(par * 64, par * 64)))
                for q in range(4):
                    cs = hh * 2048 + q * 512
                    builds.append(lambda e, par=par, z=z, cs=cs, q=q: e.matmul(
                        PSB[par * 64:(par + 1) * 64, q * 512:(q + 1) * 512],
                        CFM[:, z * 64:(z + 1) * 64], NSB[:, cs:cs + 512],
                        start=False, stop=False, tile_position=(0, par * 64)))
                for q in range(4):
                    cs = hh * 2048 + q * 512
                    builds.append(lambda e, par=par, cs=cs, q=q: e.matmul(
                        PSB[par * 64:(par + 1) * 64, q * 512:(q + 1) * 512],
                        IDB[par * 64:(par + 1) * 64, par * 64:(par + 1) * 64],
                        XPD[par * 64:(par + 1) * 64, cs:cs + 512],
                        start=False, stop=True, tile_position=(par * 64, par * 64)))
                ev_ymm = p.mmg(builds, waits=[ev_nsb, ev_pair[0], ev_pair[1],
                                              ev_yev[0], ev_yev[1]])
                ev_yev[hh] = p.op("vector", lambda e, par=par, hh=hh: e.tensor_mul(
                    YG[par * 64:(par + 1) * 64, hh * 2048:(hh + 1) * 2048],
                    PSB[par * 64:(par + 1) * 64, :],
                    SPZ[par * 64:(par + 1) * 64, hh * 2048:(hh + 1) * 2048]),
                    waits=[ev_ymm, ev_pair[2], ev_ygdma[par]])
            ev_ygdma[par] = p.dma("sync", lambda e, z=z, par=par: e.dma_start(
                out=d_yg[z * 64:(z + 1) * 64, :], in_=YG[par * 64:(par + 1) * 64, :]),
                waits=[ev_yev[0], ev_yev[1]])
            for wave in range(2):
                smms = []
                for q in range(4):
                    cq = wave * 2048 + q * 512
                    smms.append(lambda e, par=par, tb=tb, cq=cq, q=q: e.matmul(
                        PS[q][:, :], Bn[par * 64:(par + 1) * 64, tb * 128:(tb + 1) * 128],
                        XP[par * 64:(par + 1) * 64, cq:cq + 512],
                        start=True, stop=True, tile_position=(par * 64, 0)))
                ev_smm = p.mmg(smms, waits=[ev_sadd, ev_pair[0]])
                for q in range(4):
                    cq = wave * 2048 + q * 512
                    ev_sadd = p.op("vector", lambda e, cq=cq, q=q: e.scalar_tensor_tensor(
                        out=NSS[:, cq:cq + 512], in0=PS[q][:, :], scalar=1.0,
                        in1=NSS[:, cq:cq + 512], op0=ALU.mult, op1=ALU.add),
                        waits=[ev_smm])

        p.barrier()


        if PHASES <= 8:
            with nc.Block() as block:
                p.finalize(block)
            return nc
        W6 = E[:, 0:512].bitcast(BF16)
        HE = E[:, 512:1024].bitcast(BF16)
        HW = E[:, 1024:1536].bitcast(BF16)
        H1 = E[:, 1536:2048].bitcast(BF16)
        BS6 = E[:, 2048:3072].bitcast(BF16)
        BR6s = E[:, 3072:4096].bitcast(BF16)
        G6 = TMP[:, 0:48]
        GX = W[:, 4096:4608].bitcast(F32)
        WRT = W[:, 4608:4704]
        AST = W[:, 4704:5216]
        ART = W[:, 5216:6752]
        ev_wr = p.dma("gpsimd", lambda e: e.dma_start(
            out=WRT[:].rearrange("p (k n) -> p k n", k=16),
            in_=wr.rearrange("(k p) n -> p k n", k=16)))
        ev_as = p.dma("gpsimd", lambda e: e.dma_start(
            out=AST[:].rearrange("p (k n) -> p k n", k=16),
            in_=as_.rearrange("(k p) n -> p k n", k=16)))
        ev_ar = p.dma("gpsimd", lambda e: e.dma_start(
            out=ART[:].rearrange("p (k n) -> p k n", k=16),
            in_=arT.rearrange("(k p) n -> p k n", k=16)))
        ev_bs6 = p.dma("gpsimd", lambda e: e.dma_start(out=BS6[0:16, :], in_=bs))
        ev_br6 = p.dma("gpsimd", lambda e: e.dma_start(out=BR6s[0:96, :], in_=br6))
        ev_l1 = ldrow("brt", 6)
        evb1 = p.mmg([lambda e: e.matmul(PS[3][:, 0:6], ONES[0:1, :], ROWB[0:1, 0:6],
                                         start=True, stop=True)], waits=[ev_l1])
        rowst["busy"] = evb1
        ev_brtb = p.op("vector", lambda e: e.tensor_copy(TMP[:, 48:54], PS[3][:, 0:6]),
                       waits=[evb1])
        ev_l2 = ldrow("tie", 36)
        evb2 = p.mmg([lambda e: e.matmul(PS[3][:, 64:100], ONES[0:1, :], ROWB[0:1, 0:36],
                                         start=True, stop=True)], waits=[ev_brtb, ev_l2])
        rowst["busy"] = evb2
        ev_tieb = p.op("vector", lambda e: e.tensor_copy(TMP[:, 64:100], PS[3][:, 64:100]),
                       waits=[evb2])
        if G_PARTS <= 1:
            with nc.Block() as block:
                p.finalize(block)
            return nc
        ev_w6w = None
        for tb in range(NTB):
            builds = [lambda e, tb=tb, k=k: e.matmul(
                PS[0][:, 0:6], A[:, k * 1024 + tb * 128:k * 1024 + (tb + 1) * 128],
                WRT[:, k * 6:(k + 1) * 6], start=(k == 0), stop=(k == 15))
                for k in range(16)]
            evm = p.mmg(builds, waits=[ev_wr, ev_w6w])
            el = p.op("vector", lambda e: e.tensor_add(G6[:, 0:6], PS[0][:, 0:6],
                                                       TMP[:, 48:54]), waits=[evm, ev_brtb])
            em = p.op("vector", lambda e: e.reduce_max(STT[:, 0:1], G6[:, 0:6], axis=AX.X),
                      waits=[el])
            em2 = p.op("vector", lambda e: e.tensor_scalar_mul(STT[:, 1:2], STT[:, 0:1], -1.0),
                       waits=[em])
            ee = p.op("scalar", lambda e: e.activation(
                out=G6[:, 8:14], in_=G6[:, 0:6], func=AF.Exp, bias=STT[:, 1:2]),
                waits=[em2])
            es = p.op("vector", lambda e: e.reduce_sum(STT[:, 2:3], G6[:, 8:14], axis=AX.X),
                      waits=[ee])
            er = p.op("vector", lambda e: e.reciprocal(STT[:, 3:4], STT[:, 2:3]), waits=[es])
            eg = p.op("vector", lambda e: e.tensor_scalar(
                out=G6[:, 16:22], in0=G6[:, 8:14], scalar1=STT[:, 3:4], scalar2=None,
                op0=ALU.mult), waits=[er])
            e1 = p.op("vector", lambda e: e.tensor_tensor(
                out=GX[:, 0:36].rearrange("p (a b) -> p a b", a=6),
                in0=G6[:, None, 16:22].broadcast_to([128, 6, 6]),
                in1=G6[:, 16:22, None].broadcast_to([128, 6, 6]),
                op=ALU.is_gt), waits=[eg])
            e2 = p.op("vector", lambda e: e.tensor_tensor(
                out=GX[:, 36:72].rearrange("p (a b) -> p a b", a=6),
                in0=G6[:, None, 16:22].broadcast_to([128, 6, 6]),
                in1=G6[:, 16:22, None].broadcast_to([128, 6, 6]),
                op=ALU.is_equal), waits=[e1])
            e3 = p.op("vector", lambda e: e.tensor_mul(
                GX[:, 36:72], GX[:, 36:72], TMP[:, 64:100]), waits=[e2, ev_tieb])
            e4 = p.op("vector", lambda e: e.tensor_add(
                GX[:, 0:36], GX[:, 0:36], GX[:, 36:72]), waits=[e3])
            e5 = p.op("vector", lambda e: e.tensor_reduce(
                out=GX[:, 72:78, None], in_=GX[:, 0:36].rearrange("p (a b) -> p a b", a=6),
                axis=AX.X, op=ALU.add), waits=[e4])
            e6 = p.op("vector", lambda e: e.tensor_single_scalar(
                GX[:, 78:84], GX[:, 72:78], 2.0, ALU.is_lt), waits=[e5])
            e7 = p.op("vector", lambda e: e.tensor_mul(
                GX[:, 84:90], G6[:, 16:22], GX[:, 78:84]), waits=[e6])
            e8 = p.op("vector", lambda e: e.tensor_copy(W6[0:128, 0:6], GX[:, 84:90]),
                      waits=[e7, ev_w6w])
            ev_w6w = p.dma("sync", lambda e, tb=tb: e.dma_start(
                out=d_w6[tb * 128:(tb + 1) * 128, 0:6], in_=W6[0:128, 0:6]), waits=[e8])
        p.barrier()
        if G_PARTS <= 2:
            with nc.Block() as block:
                p.finalize(block)
            return nc
        ev_w6t = p.dma("scalar", lambda e: e.dma_start(out=W6[:], in_=d_w6, transpose=True))
        if G_PARTS <= 3:
            with nc.Block() as block:
                p.finalize(block)
            return nc
        for h2 in range(2):
            builds = [lambda e, k=k, h2=h2: e.matmul(
                PSB[0:96, h2 * 512:(h2 + 1) * 512], ART[:, k * 96:(k + 1) * 96],
                A[:, k * 1024 + h2 * 512:k * 1024 + (h2 + 1) * 512],
                start=(k == 0), stop=(k == 15)) for k in range(16)]
            evh = p.mmg(builds, waits=[ev_ar])
            p.op("vector", lambda e, h2=h2: e.tensor_copy(
                HE[0:96, h2 * 512:(h2 + 1) * 512], PSB[0:96, h2 * 512:(h2 + 1) * 512]),
                waits=[evh])
        if G_PARTS <= 4:
            with nc.Block() as block:
                p.finalize(block)
            return nc
        prev_h1 = None
        for h2 in range(2):
            builds = [lambda e, k=k, h2=h2: e.matmul(
                PS[1][0:32, 0:512], AST[:, k * 32:(k + 1) * 32],
                A[:, k * 1024 + h2 * 512:k * 1024 + (h2 + 1) * 512],
                start=(k == 0), stop=(k == 15)) for k in range(16)]
            evh = p.mmg(builds, waits=[ev_as, prev_h1])
            prev_h1 = p.op("vector", lambda e, h2=h2: e.tensor_copy(
                H1[0:16, h2 * 512:(h2 + 1) * 512], PS[1][0:16, 0:512]), waits=[evh])
        if G_PARTS <= 5:
            with nc.Block() as block:
                p.finalize(block)
            return nc
        prev_hw = None
        for h2 in range(2):
            evw = p.mmg([lambda e, h2=h2: e.matmul(
                PS[2][0:96, 0:512], EXP6[0:6, :], W6[0:6, h2 * 512:(h2 + 1) * 512],
                start=True, stop=True)], waits=[ev_w6t, prev_hw])
            prev_hw = p.op("vector", lambda e, h2=h2: e.tensor_mul(
                HW[0:96, h2 * 512:(h2 + 1) * 512], HE[0:96, h2 * 512:(h2 + 1) * 512],
                PS[2][0:96, 0:512]), waits=[evw, p.ev("dve")])
        p.barrier()


        if PHASES <= 9:
            with nc.Block() as block:
                p.finalize(block)
            return nc
        for ec in range(32):
            p.dma("scalar", lambda e, ec=ec: e.dma_start(
                out=B[:, ec * 1024:(ec + 1) * 1024], in_=d_yg[:, ec * 128:(ec + 1) * 128],
                transpose=True))
        ev_bbT = []
        prevb = None
        for f in range(16):
            evl = ldrow("bb", 128, off2=f * 128)
            evm = p.mmg([lambda e, f=f: e.matmul(
                PS[3][:, 160 + f:161 + f], ROWB[0:1, 0:128],
                ONES[0:1, 0:1], start=True, stop=True)], waits=[prevb, evl])
            rowst["busy"] = evm
            prevb = p.op("vector", lambda e, f=f: e.tensor_copy(
                TMP[:, 100 + f:101 + f], PS[3][:, 160 + f:161 + f]), waits=[evm])
            ev_bbT.append(prevb)
        ev_wt = [None, None]
        ev_wfree = [None, None]

        def load_wb(j, buf):
            evs = []
            for k in range(16):
                evs.append(p.dma("gpsimd", lambda e, k=k, j=j, buf=buf: e.dma_start(
                    out=WT[buf][:, k * WCOL:(k + 1) * WCOL],
                    in_=wb[k * 128:(k + 1) * 128, j * WCOL:(j + 1) * WCOL]),
                    waits=[ev_wfree[buf]]))
            return evs[-1]

        ev_wt[0] = load_wb(0, 0)
        ev_gev = [None, None, None]
        for j in range(8):
            buf = j % 2
            if j < 7:
                ev_wt[(j + 1) % 2] = load_wb(j + 1, (j + 1) % 2)
            last = None
            for fc2 in range(2):
                f = j * 2 + fc2
                for tt in range(2):
                    ps = PS[(f * 2 + tt) % 3]
                    builds = [lambda e, k=k, fc2=fc2, tt=tt, ps=ps, buf=buf: e.matmul(
                        ps[:, :], WT[buf][:, k * WCOL + fc2 * 128:k * WCOL + (fc2 + 1) * 128],
                        A[:, k * 1024 + tt * 512:k * 1024 + (tt + 1) * 512],
                        start=(k == 0), stop=False) for k in range(16)]
                    builds.append(lambda e, f=f, tt=tt, ps=ps: e.matmul(
                        ps[:, :], BS6[0:16, f * 128:(f + 1) * 128],
                        H1[0:16, tt * 512:(tt + 1) * 512], start=False, stop=False))
                    builds.append(lambda e, f=f, tt=tt, ps=ps: e.matmul(
                        ps[:, :], BR6s[0:96, f * 128:(f + 1) * 128],
                        HW[0:96, tt * 512:(tt + 1) * 512], start=False, stop=True))
                    evm = p.mmg(builds, waits=[ev_wt[buf], ev_bs6, ev_br6,
                                               ev_gev[(f * 2 + tt) % 3]])
                    last = evm
                    ev_gev[(f * 2 + tt) % 3] = p.op(
                        "scalar", lambda e, f=f, tt=tt, ps=ps: e.activation(
                            out=C[:, f * 1024 + tt * 512:f * 1024 + (tt + 1) * 512],
                            in_=ps[:, :], func=AF.Silu, bias=TMP[:, 100 + f:101 + f]),
                        waits=[evm, ev_bbT[f]])
            ev_wfree[buf] = last
        p.barrier()


        if PHASES <= 10:
            with nc.Block() as block:
                p.finalize(block)
            return nc
        UTS = [E[:, 0:1024], E[:, 1024:2048]]
        OUS = [E[:, 2048:2560], E[:, 2560:3072]]
        ev_wt = [None, None]
        ev_wfree = [None, None]

        def load_wo(dchunk, buf):
            evs = []
            for k in range(32):
                evs.append(p.dma("gpsimd", lambda e, k=k, dchunk=dchunk, buf=buf: e.dma_start(
                    out=WT[buf][:, k * 128:(k + 1) * 128],
                    in_=woT[k * 128:(k + 1) * 128, dchunk * 128:(dchunk + 1) * 128]),
                    waits=[ev_wfree[buf]]))
            return evs[-1]

        ev_wt[0] = load_wo(0, 0)
        ev_ut = [None, None]
        ev_ut_rd = [None, None]
        ev_ut[0] = p.dma("sync", lambda e: e.dma_start(out=UTS[0][:], in_=uT[0:128, :]))
        ev_ep = [None, None, None]
        ev_ost = [None, None]
        for dc in range(16):
            buf = dc % 2
            if dc < 15:
                ev_wt[(dc + 1) % 2] = load_wo(dc + 1, (dc + 1) % 2)
                ev_ut[(dc + 1) % 2] = p.dma("sync", lambda e, dc=dc: e.dma_start(
                    out=UTS[(dc + 1) % 2][:], in_=uT[(dc + 1) * 128:(dc + 2) * 128, :]),
                    waits=[ev_ut_rd[(dc + 1) % 2]])
            for tt in range(2):
                ps = PS[(dc * 2 + tt) % 3]
                builds = [lambda e, k=k, tt=tt, ps=ps, buf=buf: e.matmul(
                    ps[:, :], WT[buf][:, k * 128:(k + 1) * 128],
                    B[:, k * 1024 + tt * 512:k * 1024 + (tt + 1) * 512],
                    start=(k == 0), stop=(k == 31)) for k in range(32)]
                evm = p.mmg(builds, waits=[ev_wt[buf], ev_ep[(dc * 2 + tt) % 3]])
                ob = OUS[tt]
                e1 = p.op("vector", lambda e, dc=dc, tt=tt, ps=ps, ob=ob: e.tensor_add(
                    ob, ps[:, :], A[:, dc * 1024 + tt * 512:dc * 1024 + (tt + 1) * 512]),
                    waits=[evm, ev_ost[tt]])
                e2 = p.op("vector", lambda e, dc=dc, tt=tt, ob=ob: e.tensor_mul(
                    ob, ob, C[:, dc * 1024 + tt * 512:dc * 1024 + (tt + 1) * 512]),
                    waits=[e1])
                e3 = p.op("vector", lambda e, dc=dc, tt=tt, ob=ob, buf=buf: e.tensor_add(
                    ob, ob, UTS[buf][:, tt * 512:(tt + 1) * 512]),
                    waits=[e2, ev_ut[buf]])
                ev_ut_rd[buf] = e3
                ev_ep[(dc * 2 + tt) % 3] = e3
                ev_ost[tt] = p.dma("sync", lambda e, dc=dc, tt=tt, ob=ob: e.dma_start(
                    out=outT[dc * 128:(dc + 1) * 128, tt * 512:(tt + 1) * 512], in_=ob),
                    waits=[e3])
        p.barrier()

        with nc.Block() as block:
            p.finalize(block)

    return nc


def _pack_rows(I):
    rows = np.zeros((1, 8192), np.float32)

    def put(off, v):
        v = np.asarray(v, np.float32).reshape(-1)
        rows[0, off:off + v.size] = v

    put(0, I["L0_attn_ln_w"]); put(2048, I["L0_attn_ln_b"]); put(4096, I["L0_ffn_b_base"])
    put(6144, I["L0_attn_bnorm_w"]); put(6272, I["L0_attn_cnorm_w"])
    put(6400, I["L0_attn_D"]); put(6464, I["L0_attn_dt_bias"]); put(6528, I["L0_attn_A_log"])
    put(6592, I["L0_ffn_b_router"])
    tie = (np.arange(6)[None, :] < np.arange(6)[:, None]).astype(np.float32)
    put(6598, tie.reshape(-1))
    return rows


def _consts():
    ones = np.ones((1, 128), np.float32)
    idb = np.eye(128, dtype=np.float32)
    jj, ii = np.meshgrid(np.arange(128), np.arange(128), indexing="ij")
    ut2 = ((jj <= ii) & ((jj // 64) == (ii // 64))).astype(np.float32)
    ut1 = (jj <= ii).astype(np.float32)
    pick = np.zeros((128, 128), np.float32)
    pick[63, 0:64] = 1.0
    pick[127, 64:128] = 1.0
    picke = np.zeros((128, 128), np.float32)
    picke[63, :] = 1.0
    picko = np.zeros((128, 128), np.float32)
    picko[127, :] = 1.0
    m = (np.arange(64)[None, :] >= np.arange(64)[:, None]).astype(np.float32)
    tril = np.concatenate([m, m], axis=0)
    exp6 = np.zeros((6, 96), np.float32)
    for e6 in range(6):
        exp6[e6, e6 * 16:(e6 + 1) * 16] = 1.0
    return ones, idb, ut2, ut1, pick, picke, picko, tril, exp6


def kernel(**inputs):
    I = inputs
    u = np.ascontiguousarray(np.asarray(I["L0_attn_u"], np.float32)[0])
    wiT = np.ascontiguousarray(np.asarray(I["L0_attn_in_proj_w"], np.float32).T)
    woT = np.ascontiguousarray(np.asarray(I["L0_attn_out_proj_w"], np.float32).T)
    wb = np.ascontiguousarray(np.asarray(I["L0_ffn_W_base"], np.float32))
    as_ = np.ascontiguousarray(np.concatenate([
        np.asarray(I["L0_ffn_A_s"], np.float32),
        np.zeros((D, 16), np.float32)], axis=1))
    bs = np.ascontiguousarray(np.asarray(I["L0_ffn_B_s"], np.float32))
    arT = np.ascontiguousarray(
        np.asarray(I["L0_ffn_A_r"], np.float32).transpose(1, 0, 2).reshape(D, 96))
    br6 = np.ascontiguousarray(np.asarray(I["L0_ffn_B_r"], np.float32).reshape(96, D))
    wr = np.ascontiguousarray(np.asarray(I["L0_ffn_W_router"], np.float32))
    rows = _pack_rows(I)
    ones, idb, ut2, ut1, pick, picke, picko, tril, exp6 = _consts()

    nc = build_nc()
    in_maps = []
    for c in range(NCORE):
        sl = slice(c * T, (c + 1) * T)
        u2 = np.zeros((2 * T, D), np.float32)
        if c > 0:
            u2[0:T] = u[(c - 1) * T:c * T]
        u2[T:2 * T] = u[sl]
        r = rows.copy()
        r[0, 6634] = 0.0 if c == 0 else 1.0
        in_maps.append(dict(
            u2=u2, uT=np.ascontiguousarray(u[sl].T), wiT=wiT, woT=woT, wb=wb,
            as_=as_, bs=bs, arT=arT, br6=br6, wr=wr, vrows=r, c_ones=ones,
            c_idb=idb, c_ut2=ut2, c_ut1=ut1, c_pick=pick, c_picke=picke,
            c_picko=picko, c_tril=tril, c_exp6=exp6,
        ))
    res = run_bass_kernel_spmd(nc, in_maps, core_ids=list(range(NCORE)))
    global _LAST_RES
    _LAST_RES = res
    out = np.empty((1, 8192, D), np.float32)
    for c in range(NCORE):
        out[0, c * T:(c + 1) * T, :] = np.asarray(res.results[c]["outT"]).T
    return out


# version 07
# speedup vs baseline: 1.0354x; 1.0354x over previous
import numpy as np
import ml_dtypes
from contextlib import ExitStack

import concourse.bass as bass
import concourse.mybir as mybir
from concourse.bass_utils import run_bass_kernel_spmd

F32 = mybir.dt.float32
BF16 = mybir.dt.bfloat16
AF = mybir.ActivationFunctionType
ALU = mybir.AluOpType
AX = mybir.AxisListType
BF = ml_dtypes.bfloat16

NCORE = 8
T = 1024
NTB = 8
D = 2048
DI = 4096
NH, HD, DSTATE, CH = 64, 64, 128, 64
DPROJ = 8512
NCH = T // CH
WCOL = 256
EPS = 1e-5


class Prog:
    COMPUTE = {"tensor": "pe", "vector": "dve", "scalar": "act", "gpsimd": "pool"}
    DMAQ = {"gpsimd": "qg", "sync": "qs", "scalar": "qa"}
    ENGINES = ("tensor", "vector", "scalar", "gpsimd", "sync")

    def __init__(self, nc):
        self.nc = nc
        self.q = {e: [] for e in self.ENGINES}
        self.semnames = ["pe", "dve", "act", "pool", "qg", "qs", "qa"]
        self.sem = {}
        self.cnt = {s: 0 for s in self.semnames}
        self.waited = {e: {s: 0 for s in self.semnames} for e in self.ENGINES}

    @staticmethod
    def _flat(waits):
        out = []
        for ev in waits:
            if ev is None:
                continue
            if isinstance(ev, list):
                out.extend(Prog._flat(ev))
            else:
                out.append(ev)
        return out

    def _push(self, eng, waits, build, incsem, incn):
        w = []
        for (s, c) in self._flat(waits):
            if c > self.waited[eng][s]:
                w.append((s, c))
                self.waited[eng][s] = c
        target = None
        if incsem is not None:
            self.cnt[incsem] += incn
            target = self.cnt[incsem]
        sem = self.sem

        def run(e, w=tuple(w), build=build, incsem=incsem, target=target, incn=incn):
            for (s, c) in w:
                e.wait_ge(sem[s], c)
            inst = build(e)
            if incsem is not None and inst is not None:
                inst.then_inc(sem[incsem], incn)

        self.q[eng].append(run)
        return (incsem, target) if incsem else None

    def op(self, eng, build, waits=()):
        return self._push(eng, waits, build, self.COMPUTE[eng], 1)

    def mmg(self, builds, waits=()):
        for b in builds[:-1]:
            self._push("tensor", waits, b, None, 0)
            waits = ()
        return self._push("tensor", waits, builds[-1], "pe", 1)

    def dma(self, eng, build, waits=()):
        return self._push(eng, waits, build, self.DMAQ[eng], 16)

    def ev(self, semname):
        return (semname, self.cnt[semname])

    def barrier(self):
        evs = [(s, self.cnt[s]) for s in self.semnames if self.cnt[s] > 0]
        for e in self.ENGINES:
            self._push(e, evs, lambda _e: None, None, 0)

    def finalize(self, block):
        for ename in self.ENGINES:
            q = self.q[ename]
            deco = getattr(block, ename)

            def body(e, q=q):
                for f in q:
                    f(e)

            deco(body)


PHASES = 99
HALO_F = True
G_PARTS = 9
SERIAL_A = False
DEBUG = False
_LAST_RES = None


def build_nc():
    nc = bass.Bass("TRN2", target_bir_lowering=False)
    dt = nc.dram_tensor

    u2 = dt("u2", [2 * T, D], F32, kind="ExternalInput").ap()
    uT = dt("uT", [D, T], F32, kind="ExternalInput").ap()
    wiT = dt("wiT", [D, DPROJ], F32, kind="ExternalInput").ap()
    woT = dt("woT", [DI, D], F32, kind="ExternalInput").ap()
    wb = dt("wb", [D, D], F32, kind="ExternalInput").ap()
    as_ = dt("as_", [D, 32], F32, kind="ExternalInput").ap()
    bs = dt("bs", [16, D], F32, kind="ExternalInput").ap()
    arT = dt("arT", [D, 96], F32, kind="ExternalInput").ap()
    br6 = dt("br6", [96, D], F32, kind="ExternalInput").ap()
    wr = dt("wr", [D, 6], F32, kind="ExternalInput").ap()
    vrows = dt("vrows", [1, 8192], F32, kind="ExternalInput").ap()
    c_ones = dt("c_ones", [1, 128], F32, kind="ExternalInput").ap()
    c_idb = dt("c_idb", [128, 128], F32, kind="ExternalInput").ap()
    c_ut2 = dt("c_ut2", [128, 128], F32, kind="ExternalInput").ap()
    c_ut1 = dt("c_ut1", [128, 128], F32, kind="ExternalInput").ap()
    c_pick = dt("c_pick", [128, 128], F32, kind="ExternalInput").ap()
    c_picke = dt("c_picke", [128, 128], F32, kind="ExternalInput").ap()
    c_picko = dt("c_picko", [128, 128], F32, kind="ExternalInput").ap()
    c_tril = dt("c_tril", [128, 64], F32, kind="ExternalInput").ap()
    c_exp6 = dt("c_exp6", [6, 96], F32, kind="ExternalInput").ap()

    outT = dt("outT", [D, T], F32, kind="ExternalOutput").ap()

    KI = "ExternalOutput" if DEBUG else "Internal"
    d_un = dt("d_un", [T, D], BF16, kind=KI).ap()
    d_unh = dt("d_unh", [T, D], BF16, kind=KI).ap()
    d_xn = dt("d_xn", [T, D], BF16, kind=KI).ap()
    d_bc = dt("d_bc", [T, 2 * DSTATE], BF16, kind=KI).ap()
    d_svz = dt("d_svz", [T, DI], BF16, kind=KI).ap()
    d_yg = dt("d_yg", [T, DI], BF16, kind=KI).ap()
    d_w6 = dt("d_w6", [T, 128], BF16, kind=KI).ap()
    d_nss = dt("d_nss", [128, 4096], F32, kind=KI).ap() if DEBUG else None
    d_sc = dt("d_sc", [128, 4096], F32, kind=KI).ap() if DEBUG else None

    RO = {"lnw": 0, "lnb": 2048, "bb": 4096, "bnw": 6144, "cnw": 6272,
          "d": 6400, "dtb": 6464, "alog": 6528, "brt": 6592, "tie": 6598,
          "flag": 6634}

    with ExitStack() as ctx:
        E_ = ctx.enter_context
        A = E_(nc.sbuf_tensor("A", [128, 16384], BF16))
        B = E_(nc.sbuf_tensor("B", [128, 32768], BF16))
        C = E_(nc.sbuf_tensor("C", [128, 16384], BF16))
        NSS = E_(nc.sbuf_tensor("NSS", [128, 4096], F32))
        E = E_(nc.sbuf_tensor("E", [128, 4096], F32))
        W = E_(nc.sbuf_tensor("W", [128, 8192], BF16))
        Bn = E_(nc.sbuf_tensor("Bn", [128, 1024], BF16))
        Cn = E_(nc.sbuf_tensor("Cn", [128, 1024], BF16))
        Bh = E_(nc.sbuf_tensor("Bh", [128, 1024], BF16))
        DSC = E_(nc.sbuf_tensor("DSC", [128, 512], BF16))
        PST = E_(nc.sbuf_tensor("PST", [128, 512], BF16))
        GDS = E_(nc.sbuf_tensor("GDS", [128, 512], BF16))
        TREP = E_(nc.sbuf_tensor("TREP", [128, 1024], BF16))
        BFM = E_(nc.sbuf_tensor("BFM", [128, 1024], BF16))
        CFM = E_(nc.sbuf_tensor("CFM", [128, 1024], BF16))
        CBT = E_(nc.sbuf_tensor("CBT", [128, 512], BF16))
        IDB = E_(nc.sbuf_tensor("IDB", [128, 128], BF16))
        UT2 = E_(nc.sbuf_tensor("UT2", [128, 128], F32))
        UT1 = E_(nc.sbuf_tensor("UT1", [128, 128], F32))
        PICK = E_(nc.sbuf_tensor("PICK", [128, 128], F32))
        PICKE = E_(nc.sbuf_tensor("PICKE", [128, 128], F32))
        PICKO = E_(nc.sbuf_tensor("PICKO", [128, 128], F32))
        TRILM = E_(nc.sbuf_tensor("TRILM", [128, 64], F32))
        ONES = E_(nc.sbuf_tensor("ONES", [1, 128], F32))
        ROWB = E_(nc.sbuf_tensor("ROWB", [1, 512], F32))
        EXP6 = E_(nc.sbuf_tensor("EXP6", [6, 96], BF16))
        DD = E_(nc.sbuf_tensor("DD", [128, 64], BF16))
        DTB = E_(nc.sbuf_tensor("DTB", [128, 64], F32))
        ANEG = E_(nc.sbuf_tensor("ANEG", [128, 64], F32))
        BNWB = E_(nc.sbuf_tensor("BNWB", [128, 128], BF16))
        CNWB = E_(nc.sbuf_tensor("CNWB", [128, 128], BF16))
        STT = E_(nc.sbuf_tensor("STT", [128, 40], F32))
        TMP = E_(nc.sbuf_tensor("TMP", [128, 256], F32))
        dAm = E[:, 0:512]
        dAh = E[:, 512:1024]
        GAS = E[:, 1024:1536]
        PS = [E_(nc.psum_tensor(f"PS{i}", [128, 512], F32)) for i in range(4)]
        PSB = E_(nc.psum_tensor("PSB", [128, 2048], F32))

        p = Prog(nc)
        for s in p.semnames:
            p.sem[s] = E_(nc.semaphore(s))

        rowst = {"busy": None}

        def ldrow(name, w, off2=0):
            o = RO[name] + off2
            ev = p.dma("sync", lambda e, o=o, w=w: e.dma_start(
                out=ROWB[0:1, 0:w], in_=vrows[0:1, o:o + w]), waits=[rowst["busy"]])
            return ev

        p.dma("sync", lambda e: e.dma_start(out=ONES[:], in_=c_ones))
        p.dma("sync", lambda e: e.dma_start(out=UT2[:], in_=c_ut2))
        p.dma("sync", lambda e: e.dma_start(out=UT1[:], in_=c_ut1))
        p.dma("sync", lambda e: e.dma_start(out=PICK[:], in_=c_pick))
        p.dma("sync", lambda e: e.dma_start(out=PICKE[:], in_=c_picke))
        p.dma("sync", lambda e: e.dma_start(out=PICKO[:], in_=c_picko))
        p.dma("sync", lambda e: e.dma_start(out=TRILM[:], in_=c_tril))
        p.dma("gpsimd", lambda e: e.dma_start(out=IDB[:], in_=c_idb))
        p.dma("gpsimd", lambda e: e.dma_start(out=EXP6[:], in_=c_exp6))
        ev_qs0 = p.ev("qs")

        bc_prev = [None]

        def bcast_name(name, width, dst, off2=0, prep=None):
            evl = ldrow(name, width, off2)
            evp_ = evl
            if prep is not None:
                evp_ = prep(evl)
            evm = p.mmg([lambda e, width=width: e.matmul(
                PS[3][:, 0:width], ONES[0:1, :], ROWB[0:1, 0:width],
                start=True, stop=True)], waits=[ev_qs0, evp_, bc_prev[0]])
            rowst["busy"] = evm
            evc = p.op("vector", lambda e, dst=dst, width=width: e.tensor_copy(
                dst, PS[3][:, 0:width]), waits=[evm])
            bc_prev[0] = evc
            return evc

        def _alog_prep(evl):
            ea = p.op("scalar", lambda e: e.activation(
                out=ROWB[0:1, 0:NH], in_=ROWB[0:1, 0:NH], func=AF.Exp), waits=[evl])
            return p.op("vector", lambda e: e.tensor_scalar_mul(
                ROWB[0:1, 0:NH], ROWB[0:1, 0:NH], -1.0), waits=[ea])

        ev_dd = bcast_name("d", NH, DD[:, :])
        ev_dtb = bcast_name("dtb", NH, DTB[:, :])
        ev_aneg = bcast_name("alog", NH, ANEG[:, :], prep=_alog_prep)
        ev_bnw = bcast_name("bnw", DSTATE, BNWB[:, :])
        ev_cnw = bcast_name("cnw", DSTATE, CNWB[:, :])
        LNW = B[:, 0:2048]
        LNB = B[:, 2048:4096]
        for j in range(4):
            bcast_name("lnw", 512, LNW[:, j * 512:(j + 1) * 512], off2=j * 512)
        for j in range(4):
            ev_lnwb = bcast_name("lnb", 512, LNB[:, j * 512:(j + 1) * 512], off2=j * 512)


        if PHASES <= 1:
            with nc.Block() as block:
                p.finalize(block)
            return nc
        p.barrier()

        Ust = [E[:, 0:2048], E[:, 2048:4096]]
        XNS = [W[:, 0:2048], W[:, 2048:4096]]
        UNS = [W[:, 4096:6144], W[:, 6144:8192]]
        SQS = B[:, 4096:8192]
        ev_ub_free = [None, None]
        for i in range(16):
            halo = i < 8
            tb = i % 8
            par = i % 2
            ub, xb, nb = Ust[par], XNS[par], UNS[par]
            sq = SQS[:, par * 2048:(par + 1) * 2048]
            off = par * 16
            cc = {k: STT[:, off + j:off + j + 1] for j, k in enumerate(
                ["sq", "sum", "m", "msq", "v", "a", "r1", "w", "b", "r12", "negm"])}
            ld = p.dma("sync", lambda e, i=i, ub=ub: e.dma_start(
                out=ub, in_=u2[i * 128:(i + 1) * 128, :]),
                waits=[ev_ub_free[par]] + ([p.ev("dve"), p.ev("act"), p.ev("pool"),
                                            p.ev("qs")] if SERIAL_A else []))
            e1 = p.op("scalar", lambda e, ub=ub, sq=sq, c=cc["sq"]: e.activation(
                out=sq, in_=ub, func=AF.Square, accum_out=c), waits=[ld, ev_ub_free[par]])
            e2 = p.op("vector", lambda e, ub=ub, c=cc["sum"]: e.reduce_sum(
                c, ub, axis=AX.X), waits=[ld])
            e3 = p.op("vector", lambda e, cc=cc: e.tensor_scalar_mul(cc["m"], cc["sum"], 1.0 / D),
                      waits=[e2])
            e4 = p.op("vector", lambda e, cc=cc: e.tensor_scalar_mul(cc["msq"], cc["sq"], 1.0 / D),
                      waits=[e1])
            e5 = p.op("vector", lambda e, cc=cc: e.tensor_scalar(
                out=cc["v"], in0=cc["m"], scalar1=cc["m"], scalar2=None, op0=ALU.mult),
                waits=[e3])
            e6 = p.op("vector", lambda e, cc=cc: e.tensor_sub(cc["v"], cc["msq"], cc["v"]),
                      waits=[e4, e5])
            e7 = p.op("vector", lambda e, cc=cc: e.tensor_scalar_add(cc["a"], cc["v"], EPS),
                      waits=[e6])
            e8 = p.op("scalar", lambda e, cc=cc: e.activation(
                out=cc["r1"], in_=cc["a"], func=AF.Ln), waits=[e7])
            e8b = p.op("vector", lambda e, cc=cc: e.tensor_scalar_mul(
                cc["r1"], cc["r1"], -0.5), waits=[e8])
            e9 = p.op("scalar", lambda e, cc=cc: e.activation(
                out=cc["r1"], in_=cc["r1"], func=AF.Exp), waits=[e8b])
            e10 = p.op("vector", lambda e, cc=cc: e.reciprocal(cc["w"], cc["a"]), waits=[e7])
            e11 = p.op("vector", lambda e, cc=cc: e.tensor_scalar(
                out=cc["w"], in0=cc["w"], scalar1=cc["v"], scalar2=EPS, op0=ALU.mult,
                op1=ALU.add), waits=[e10])
            e12 = p.op("scalar", lambda e, cc=cc: e.activation(
                out=cc["b"], in_=cc["w"], func=AF.Ln), waits=[e11])
            e12b = p.op("vector", lambda e, cc=cc: e.tensor_scalar_mul(
                cc["b"], cc["b"], -0.5), waits=[e12])
            e13 = p.op("scalar", lambda e, cc=cc: e.activation(
                out=cc["b"], in_=cc["b"], func=AF.Exp), waits=[e12b])
            e15 = p.op("vector", lambda e, cc=cc: e.tensor_scalar_mul(cc["negm"], cc["m"], -1.0),
                      waits=[e3])
            exn = p.op("vector", lambda e, ub=ub, xb=xb, cc=cc: e.tensor_scalar(
                out=xb, in0=ub, scalar1=cc["negm"], scalar2=cc["r1"], op0=ALU.add,
                op1=ALU.mult), waits=[e15, e9])
            eun1 = p.op("vector", lambda e, xb=xb, nb=nb, cc=cc: e.scalar_tensor_tensor(
                out=nb, in0=xb, scalar=cc["b"], in1=LNW, op0=ALU.mult, op1=ALU.mult),
                waits=[exn, e13, ev_lnwb])
            eun2 = p.op("gpsimd", lambda e, nb=nb: e.tensor_add(nb, nb, LNB), waits=[eun1])
            frees = [e1, e2, exn, eun2, e13, e9]
            if DEBUG == 2 and i == 15:
                p.dma("sync", lambda e, ub=ub: e.dma_start(
                    out=d_sc[:, 0:2048], in_=ub), waits=[exn])
                p.dma("sync", lambda e: e.dma_start(
                    out=d_sc[:, 2048:2088], in_=STT[:, :]), waits=[exn, eun2])
            if not halo:
                exs = p.dma("sync", lambda e, tb=tb, xb=xb: e.dma_start(
                    out=d_xn[tb * 128:(tb + 1) * 128, :], in_=xb), waits=[exn])
                frees.append(exs)
            dun = d_unh if halo else d_un
            eus = p.dma("sync", lambda e, tb=tb, nb=nb, dun=dun: e.dma_start(
                out=dun[tb * 128:(tb + 1) * 128, :], in_=nb), waits=[eun2])
            frees.append(eus)
            ev_ub_free[par] = frees


        if PHASES <= 2:
            with nc.Block() as block:
                p.finalize(block)
            return nc
        p.barrier()
        for k in range(16):
            p.dma("scalar", lambda e, k=k: e.dma_start(
                out=A[:, k * 1024:(k + 1) * 1024], in_=d_un[:, k * 128:(k + 1) * 128],
                transpose=True))
            p.dma("scalar", lambda e, k=k: e.dma_start(
                out=C[:, k * 1024:(k + 1) * 1024], in_=d_unh[:, k * 128:(k + 1) * 128],
                transpose=True))
        p.barrier()

        def lhs_un(i, k):
            reg = C if i < 8 else A
            tb = i % 8
            return reg[:, k * 1024 + tb * 128: k * 1024 + (tb + 1) * 128]


        if PHASES <= 3:
            with nc.Block() as block:
                p.finalize(block)
            return nc
        NB = 320
        ev_wb = None
        for k in range(16):
            ev_wb = p.dma("gpsimd", lambda e, k=k: e.dma_start(
                out=W[:, k * NB:(k + 1) * NB],
                in_=wiT[k * 128:(k + 1) * 128, 8192:8512]))
        ev_evac3 = [None, None, None]
        for i in range(16):
            halo = i < 8
            tb = i % 8
            ps = PS[i % 3]
            builds = [lambda e, i=i, k=k, ps=ps: e.matmul(
                ps[:, 0:NB], lhs_un(i, k), W[:, k * NB:(k + 1) * NB],
                start=(k == 0), stop=(k == 15)) for k in range(16)]
            evm = p.mmg(builds, waits=[ev_wb, ev_evac3[i % 3]])
            readers = []
            bn_dst = Bh if halo else Bn
            c_bs, c_br = STT[:, 32:33], STT[:, 33:34]
            eb1 = p.op("scalar", lambda e, ps=ps, c=c_bs: e.activation(
                out=TMP[:, 0:128], in_=ps[:, 0:128], func=AF.Square, accum_out=c),
                waits=[evm, ev_evac3[i % 3]])
            eb1b = p.op("vector", lambda e: e.tensor_scalar(
                out=c_br, in0=c_bs, scalar1=1.0 / DSTATE, scalar2=EPS, op0=ALU.mult,
                op1=ALU.add), waits=[eb1])
            eb2 = p.op("scalar", lambda e: e.activation(
                out=c_br, in_=c_br, func=AF.Ln), waits=[eb1b])
            eb2b = p.op("vector", lambda e: e.tensor_scalar_mul(c_br, c_br, -0.5),
                        waits=[eb2])
            eb3 = p.op("scalar", lambda e: e.activation(
                out=c_br, in_=c_br, func=AF.Exp), waits=[eb2b])
            eb4 = p.op("vector", lambda e, ps=ps, bn_dst=bn_dst, tb=tb: e.scalar_tensor_tensor(
                out=bn_dst[:, tb * 128:(tb + 1) * 128], in0=ps[:, 0:128], scalar=c_br,
                in1=BNWB[:, :], op0=ALU.mult, op1=ALU.mult), waits=[eb3, ev_bnw])
            readers += [eb1, eb4]
            if not halo:
                c_cs, c_cr = STT[:, 34:35], STT[:, 35:36]
                ec1 = p.op("scalar", lambda e, ps=ps, c=c_cs: e.activation(
                    out=TMP[:, 0:128], in_=ps[:, 128:256], func=AF.Square, accum_out=c),
                    waits=[eb1])
                ec1b = p.op("vector", lambda e: e.tensor_scalar(
                    out=c_cr, in0=c_cs, scalar1=1.0 / DSTATE, scalar2=EPS, op0=ALU.mult,
                    op1=ALU.add), waits=[ec1])
                ec2 = p.op("scalar", lambda e: e.activation(
                    out=c_cr, in_=c_cr, func=AF.Ln), waits=[ec1b])
                ec2b = p.op("vector", lambda e: e.tensor_scalar_mul(c_cr, c_cr, -0.5),
                            waits=[ec2])
                ec3 = p.op("scalar", lambda e: e.activation(
                    out=c_cr, in_=c_cr, func=AF.Exp), waits=[ec2b])
                ec4 = p.op("vector", lambda e, ps=ps, tb=tb: e.scalar_tensor_tensor(
                    out=Cn[:, tb * 128:(tb + 1) * 128], in0=ps[:, 128:256], scalar=c_cr,
                    in1=CNWB[:, :], op0=ALU.mult, op1=ALU.mult), waits=[ec3, ev_cnw])
                readers += [ec1, ec4]
            da_dst = dAh if halo else dAm
            ed1 = p.op("vector", lambda e, ps=ps: e.tensor_add(
                TMP[:, 128:192], ps[:, 256:320], DTB[:, :]), waits=[evm, ev_dtb, eb4])
            ed2 = p.op("scalar", lambda e: e.activation(
                out=TMP[:, 192:256], in_=TMP[:, 128:192], func=AF.Abs), waits=[ed1])
            ed3 = p.op("scalar", lambda e: e.activation(
                out=TMP[:, 192:256], in_=TMP[:, 192:256], func=AF.Exp, scale=-1.0),
                waits=[ed2])
            ed4 = p.op("scalar", lambda e: e.activation(
                out=TMP[:, 192:256], in_=TMP[:, 192:256], func=AF.Ln, bias=1.0), waits=[ed3])
            ed5 = p.op("vector", lambda e: e.tensor_scalar_max(
                TMP[:, 128:192], TMP[:, 128:192], 0.0), waits=[ed2])
            ed6 = p.op("vector", lambda e: e.tensor_add(
                TMP[:, 128:192], TMP[:, 128:192], TMP[:, 192:256]), waits=[ed4, ed5])
            ed7 = p.op("vector", lambda e, da_dst=da_dst, tb=tb: e.tensor_mul(
                da_dst[:, tb * 64:(tb + 1) * 64], TMP[:, 128:192], ANEG[:, :]),
                waits=[ed6, ev_aneg])
            readers.append(ed1)
            ev_evac3[i % 3] = readers + [ed7]

        p.barrier()


        if PHASES <= 4:
            with nc.Block() as block:
                p.finalize(block)
            return nc
        prev_ed = prev_ds = prev_te = prev_to = None
        for tb in range(NTB):
            evm = p.mmg([lambda e, tb=tb: e.matmul(
                PS[0][:, 0:64], UT2[:], dAm[:, tb * 64:(tb + 1) * 64],
                start=True, stop=True)], waits=[prev_ed])
            eac = p.op("vector", lambda e: e.tensor_copy(TMP[:, 0:64], PS[0][:, 0:64]),
                       waits=[evm])
            evp = p.mmg([lambda e: e.matmul(
                PS[1][:, 0:64], PICK[:, 0:128], TMP[:, 0:64], start=True, stop=True)],
                waits=[eac])
            ed = p.op("vector", lambda e: e.tensor_sub(
                TMP[:, 64:128], PS[1][:, 0:64], TMP[:, 0:64]), waits=[evp, prev_ds])
            ds1 = p.op("scalar", lambda e, tb=tb: e.activation(
                out=DSC[:, tb * 64:(tb + 1) * 64], in_=TMP[:, 64:128], func=AF.Exp),
                waits=[ed])
            ds2 = p.op("scalar", lambda e, tb=tb: e.activation(
                out=PST[:, tb * 64:(tb + 1) * 64], in_=TMP[:, 64:128], func=AF.Exp,
                scale=-1.0), waits=[ed])
            prev_ds = [ds1, ds2]
            prev_ed = ed
            ev_e = p.mmg([lambda e: e.matmul(
                PS[2][:, 0:64], PICKE[:], TMP[:, 0:64], start=True, stop=True)],
                waits=[eac, prev_te])
            prev_te = p.op("scalar", lambda e, tb=tb: e.activation(
                out=TREP[:, (2 * tb) * 64:(2 * tb + 1) * 64], in_=PS[2][:, 0:64],
                func=AF.Exp), waits=[ev_e])
            ev_o = p.mmg([lambda e: e.matmul(
                PS[2][:, 64:128], PICKO[:], TMP[:, 0:64], start=True, stop=True)],
                waits=[eac, prev_to])
            prev_to = p.op("scalar", lambda e, tb=tb: e.activation(
                out=TREP[:, (2 * tb + 1) * 64:(2 * tb + 2) * 64], in_=PS[2][:, 64:128],
                func=AF.Exp), waits=[ev_o])

        OFFR = TMP[0:1, 128:192]
        prev = p.op("vector", lambda e: e.memset(OFFR[:, 0:64], 0.0),
                    waits=[p.ev("act")])
        for tb in range(NTB):
            evm = p.mmg([lambda e, tb=tb: e.matmul(
                PS[0][:, 0:64], UT1[:], dAh[:, tb * 64:(tb + 1) * 64],
                start=True, stop=True)], waits=[p.ev("dve"), p.ev("act")])
            evb = p.mmg([lambda e: e.matmul(
                PS[1][:, 0:64], ONES[0:1, :], OFFR[:, 0:64], start=True, stop=True)],
                waits=[prev])
            ecb = p.op("vector", lambda e: e.tensor_copy(TMP[:, 64:128], PS[1][:, 0:64]),
                       waits=[evb])
            ega = p.op("vector", lambda e, tb=tb: e.tensor_add(
                GAS[:, tb * 64:(tb + 1) * 64], PS[0][:, 0:64], TMP[:, 64:128]),
                waits=[evm, ecb])
            evr = p.mmg([lambda e, tb=tb: e.matmul(
                PS[2][0:1, 0:64], PICK[:, 64:65], GAS[:, tb * 64:(tb + 1) * 64],
                start=True, stop=True)], waits=[ega])
            prev = p.op("vector", lambda e: e.tensor_copy(OFFR[:, 0:64], PS[2][0:1, 0:64]),
                        waits=[evr])
        evb = p.mmg([lambda e: e.matmul(
            PS[1][:, 0:64], ONES[0:1, :], OFFR[:, 0:64], start=True, stop=True)],
            waits=[prev])
        egb = p.op("vector", lambda e: e.tensor_copy(TMP[:, 0:64], PS[1][:, 0:64]),
                   waits=[evb])
        ev_gds = []
        for tb in range(NTB):
            eg1 = p.op("vector", lambda e, tb=tb: e.tensor_sub(
                GAS[:, tb * 64:(tb + 1) * 64], TMP[:, 0:64], GAS[:, tb * 64:(tb + 1) * 64]),
                waits=[egb])
            ev_gds.append(p.op("scalar", lambda e, tb=tb: e.activation(
                out=GDS[:, tb * 64:(tb + 1) * 64], in_=GAS[:, tb * 64:(tb + 1) * 64],
                func=AF.Exp), waits=[eg1]))
        ev_fl = ldrow("flag", 1)
        evf = p.mmg([lambda e: e.matmul(
            PS[1][:, 0:1], ONES[0:1, :], ROWB[0:1, 0:1], start=True, stop=True)],
            waits=[egb, ev_fl])
        rowst["busy"] = evf
        evf = p.op("vector", lambda e: e.tensor_copy(STT[:, 36:37], PS[1][:, 0:1]),
                   waits=[evf])
        p.op("vector", lambda e: e.tensor_scalar(
            out=GDS[:], in0=GDS[:], scalar1=STT[:, 36:37], scalar2=None, op0=ALU.mult),
            waits=[evf] + ev_gds)

        p.barrier()


        if PHASES <= 5:
            with nc.Block() as block:
                p.finalize(block)
            return nc
        WT = [W[:, 0:4096], W[:, 4096:8192]]
        HST = [E[:, i * 128:(i + 1) * 128].bitcast(BF16) for i in range(8)]
        ev_wt = [None, None]
        ev_wfree = [None, None]

        def load_tile(col0, buf):
            evs = []
            for k in range(16):
                evs.append(p.dma("gpsimd", lambda e, k=k, col0=col0, buf=buf: e.dma_start(
                    out=WT[buf][:, k * WCOL:(k + 1) * WCOL],
                    in_=wiT[k * 128:(k + 1) * 128, col0:col0 + WCOL]),
                    waits=[ev_wfree[buf]]))
            return evs[-1]

        ev_wt[0] = load_tile(4096, 0)
        ev_evac3 = [None, None, None]
        ev_hst_free = [None, None]
        ev_fchain = None
        for j in range(16):
            buf = j % 2
            if j < 15:
                ev_wt[(j + 1) % 2] = load_tile(4096 + (j + 1) * WCOL, (j + 1) % 2)
            last_mm = None
            for i in range(16):
                halo = i < 8
                tb = i % 8
                ps = PS[i % 3]
                builds = [lambda e, i=i, k=k, ps=ps, buf=buf: e.matmul(
                    ps[:, 0:WCOL], lhs_un(i, k), WT[buf][:, k * WCOL:(k + 1) * WCOL],
                    start=(k == 0), stop=(k == 15)) for k in range(16)]
                evm = p.mmg(builds, waits=[ev_wt[buf], ev_evac3[i % 3]])
                last_mm = evm
                if halo and not HALO_F:
                    ev_evac3[i % 3] = evm
                elif halo:
                    hb = HST[tb]
                    exx = p.op("vector", lambda e, ps=ps, hb=hb, tb=tb, j=j: e.tensor_tensor(
                        out=hb[:].rearrange("p (h q) -> p h q", h=4),
                        in0=ps[:, 0:WCOL].rearrange("p (h q) -> p h q", h=4),
                        in1=GDS[:, tb * 64 + j * 4: tb * 64 + (j + 1) * 4, None]
                            .broadcast_to([128, 4, 64]),
                        op=ALU.mult), waits=[evm, ev_hst_free[0]])
                    ev_exx = exx
                    ev_evac3[i % 3] = exx
                    if tb == 7 and HALO_F != "noF":
                        fb = []
                        for tb2 in range(8):
                            fb.append(lambda e, tb2=tb2: e.matmul(
                                PSB[:, 0:256],
                                Bh[:, tb2 * 128:(tb2 + 1) * 128],
                                HST[tb2][:, :],
                                start=(tb2 == 0), stop=(tb2 == 7)))
                        ev_hst_free[0] = p.mmg(fb, waits=[p.ev("dve"), ev_fchain])
                else:
                    ecp = p.op("vector", lambda e, ps=ps, tb=tb, j=j: e.tensor_copy(
                        B[:, tb * 4096 + j * WCOL: tb * 4096 + (j + 1) * WCOL],
                        ps[:, 0:WCOL]), waits=[evm])
                    ev_evac3[i % 3] = ecp
            ev_wfree[buf] = last_mm
            if HALO_F is True:
                ev_fchain = p.op("vector", lambda e, j=j: e.tensor_copy(
                    NSS[:, j * 256:(j + 1) * 256], PSB[:, 0:256]),
                    waits=[ev_hst_free[0]])

        p.barrier()
        if DEBUG == 1:
            p.dma("sync", lambda e: e.dma_start(out=d_nss, in_=NSS[:]))
            p.op("vector", lambda e: e.tensor_copy(E[:, 0:256], DSC[:].bitcast(F32)),
                 waits=[p.ev("qs")])
            p.op("vector", lambda e: e.tensor_copy(E[:, 256:512], PST[:].bitcast(F32)))
            p.op("vector", lambda e: e.tensor_copy(E[:, 512:1024], TREP[:].bitcast(F32)))
            p.op("vector", lambda e: e.tensor_copy(E[:, 1024:1280], GDS[:].bitcast(F32)))
            p.dma("sync", lambda e: e.dma_start(out=d_sc, in_=E[:]), waits=[p.ev("dve")])
            p.barrier()

        if PHASES <= 6:
            with nc.Block() as block:
                p.finalize(block)
            return nc
        SVS = [E[:, 0:128].bitcast(BF16), E[:, 128:256].bitcast(BF16)]
        ev_wt = [None, None]
        ev_wfree = [None, None]
        ev_wt[0] = load_tile(0, 0)
        ev_sv = [None, None]
        ev_evac3 = [None, None, None]
        for j in range(16):
            buf = j % 2
            if j < 15:
                ev_wt[(j + 1) % 2] = load_tile((j + 1) * WCOL, (j + 1) % 2)
            last_mm = None
            for tb in range(8):
                i = 8 + tb
                ps = PS[tb % 3]
                sv = SVS[tb % 2]
                builds = [lambda e, i=i, k=k, ps=ps, buf=buf: e.matmul(
                    ps[:, 0:WCOL], lhs_un(i, k), WT[buf][:, k * WCOL:(k + 1) * WCOL],
                    start=(k == 0), stop=(k == 15)) for k in range(16)]
                evm = p.mmg(builds, waits=[ev_wt[buf], ev_evac3[tb % 3]])
                last_mm = evm
                esi = p.op("scalar", lambda e, ps=ps, sv=sv: e.activation(
                    out=sv, in_=ps[:, 0:WCOL], func=AF.Silu), waits=[evm, ev_sv[tb % 2]])
                ev_evac3[tb % 3] = esi
                ev_sv[tb % 2] = p.dma("sync", lambda e, tb=tb, j=j, sv=sv: e.dma_start(
                    out=d_svz[tb * 128:(tb + 1) * 128, j * WCOL:(j + 1) * WCOL], in_=sv),
                    waits=[esi])
            ev_wfree[buf] = last_mm

        for tb in range(NTB):
            p.dma("sync", lambda e, tb=tb: e.dma_start(
                out=d_bc[tb * 128:(tb + 1) * 128, 0:128], in_=Bn[:, tb * 128:(tb + 1) * 128]))
            p.dma("sync", lambda e, tb=tb: e.dma_start(
                out=d_bc[tb * 128:(tb + 1) * 128, 128:256], in_=Cn[:, tb * 128:(tb + 1) * 128]))
        ev_bcw = p.ev("qs")
        p.dma("scalar", lambda e: e.dma_start(out=BFM[:], in_=d_bc[:, 0:128], transpose=True),
              waits=[ev_bcw])
        p.dma("scalar", lambda e: e.dma_start(out=CFM[:], in_=d_bc[:, 128:256], transpose=True),
              waits=[ev_bcw])
        ev_bcr = p.ev("qa")

        prev_cbt = None
        for z in range(NCH):
            par = z % 2
            evm = p.mmg([lambda e, z=z, par=par: e.matmul(
                PS[0][par * 64:(par + 1) * 64, 0:64], BFM[:, z * 64:(z + 1) * 64],
                CFM[:, z * 64:(z + 1) * 64], start=True, stop=True,
                tile_position=(0, par * 64))], waits=[ev_bcr, prev_cbt, p.ev("act")])
            prev_cbt = p.op("vector", lambda e, z=z, par=par: e.tensor_mul(
                CBT[par * 64:(par + 1) * 64, (z // 2) * 64:(z // 2 + 1) * 64],
                PS[0][par * 64:(par + 1) * 64, 0:64],
                TRILM[par * 64:(par + 1) * 64, :]), waits=[evm])
        for k in range(16):
            p.dma("scalar", lambda e, k=k: e.dma_start(
                out=A[:, k * 1024:(k + 1) * 1024], in_=d_xn[:, k * 128:(k + 1) * 128],
                transpose=True), waits=[p.ev("pe")])
        p.barrier()


        if PHASES <= 7:
            with nc.Block() as block:
                p.finalize(block)
            return nc
        XP = C[:, 0:4096]
        XPD = C[:, 4096:8192]
        SPZ = C[:, 8192:12288]
        NSB = C[:, 12288:16384]
        YG = W[:, 0:4096]
        SVZ = [E[:, 0:2048].bitcast(BF16), E[:, 2048:4096].bitcast(BF16)]
        ev_svld = [None, None]
        ev_svld[0] = p.dma("sync", lambda e: e.dma_start(out=SVZ[0][:], in_=d_svz[0:128, :]))
        ev_pair = [None, None, None]
        ev_sadd = None
        ev_smm = None
        ev_ymm = None
        ev_yev = [None, None]
        ev_ygdma = [None, None]
        for z in range(NCH):
            tb = z // 2
            par = z % 2
            if par == 0:
                if tb < 7:
                    ev_svld[(tb + 1) % 2] = p.dma("sync", lambda e, tb=tb: e.dma_start(
                        out=SVZ[(tb + 1) % 2][:], in_=d_svz[(tb + 1) * 128:(tb + 2) * 128, :]),
                        waits=[ev_pair[2]])
                exp_ = p.op("gpsimd", lambda e, tb=tb: e.tensor_tensor(
                    out=XP[:].rearrange("p (h q) -> p h q", h=NH),
                    in0=B[:, tb * 4096:(tb + 1) * 4096].rearrange("p (h q) -> p h q", h=NH),
                    in1=DSC[:, tb * 64:(tb + 1) * 64, None].broadcast_to([128, NH, HD]),
                    op=ALU.mult), waits=[ev_ymm, ev_smm])
                exd = p.op("gpsimd", lambda e: e.tensor_tensor(
                    out=XPD[:].rearrange("p (h q) -> p h q", h=NH),
                    in0=XP[:].rearrange("p (h q) -> p h q", h=NH),
                    in1=DD[:, :, None].broadcast_to([128, NH, HD]),
                    op=ALU.mult), waits=[exp_, ev_dd, ev_ymm])
                esp = p.op("gpsimd", lambda e, tb=tb: e.tensor_tensor(
                    out=SPZ[:].rearrange("p (h q) -> p h q", h=NH),
                    in0=SVZ[tb % 2][:].rearrange("p (h q) -> p h q", h=NH),
                    in1=PST[:, tb * 64:(tb + 1) * 64, None].broadcast_to([128, NH, HD]),
                    op=ALU.mult), waits=[ev_svld[tb % 2], ev_yev[0], ev_yev[1]])
                ev_pair = [exp_, exd, esp]
            emul = p.op("gpsimd", lambda e, z=z: e.tensor_tensor(
                out=NSS[:].rearrange("p (h q) -> p h q", h=NH),
                in0=NSS[:].rearrange("p (h q) -> p h q", h=NH),
                in1=TREP[:, z * 64:(z + 1) * 64, None].broadcast_to([128, NH, HD]),
                op=ALU.mult), waits=[ev_sadd, ev_ymm])
            ev_nsb = p.op("vector", lambda e: e.tensor_copy(NSB[:], NSS[:]), waits=[emul])
            for hh in range(2):
                b1 = []
                for q in range(4):
                    cs = hh * 2048 + q * 512
                    b1.append(lambda e, par=par, z=z, cs=cs, q=q: e.matmul(
                        PSB[par * 64:(par + 1) * 64, q * 512:(q + 1) * 512],
                        CBT[par * 64:(par + 1) * 64, (z // 2) * 64:(z // 2 + 1) * 64],
                        XP[par * 64:(par + 1) * 64, cs:cs + 512],
                        start=True, stop=False, tile_position=(par * 64, par * 64)))
                for q in range(4):
                    cs = hh * 2048 + q * 512
                    b1.append(lambda e, par=par, cs=cs, q=q: e.matmul(
                        PSB[par * 64:(par + 1) * 64, q * 512:(q + 1) * 512],
                        IDB[par * 64:(par + 1) * 64, par * 64:(par + 1) * 64],
                        XPD[par * 64:(par + 1) * 64, cs:cs + 512],
                        start=False, stop=False, tile_position=(par * 64, par * 64)))
                p.mmg(b1, waits=[ev_pair[0], ev_pair[1], ev_yev[0], ev_yev[1]])
                b2 = []
                for q in range(4):
                    cs = hh * 2048 + q * 512
                    b2.append(lambda e, par=par, z=z, cs=cs, q=q: e.matmul(
                        PSB[par * 64:(par + 1) * 64, q * 512:(q + 1) * 512],
                        CFM[:, z * 64:(z + 1) * 64], NSB[:, cs:cs + 512],
                        start=False, stop=True, tile_position=(0, par * 64)))
                ev_ymm = p.mmg(b2, waits=[ev_nsb])
                ev_yev[hh] = p.op("vector", lambda e, par=par, hh=hh: e.tensor_mul(
                    YG[par * 64:(par + 1) * 64, hh * 2048:(hh + 1) * 2048],
                    PSB[par * 64:(par + 1) * 64, :],
                    SPZ[par * 64:(par + 1) * 64, hh * 2048:(hh + 1) * 2048]),
                    waits=[ev_ymm, ev_pair[2], ev_ygdma[par]])
            ev_ygdma[par] = p.dma("sync", lambda e, z=z, par=par: e.dma_start(
                out=d_yg[z * 64:(z + 1) * 64, :], in_=YG[par * 64:(par + 1) * 64, :]),
                waits=[ev_yev[0], ev_yev[1]])
            for wave in range(2):
                smms = []
                for q in range(4):
                    cq = wave * 2048 + q * 512
                    smms.append(lambda e, par=par, tb=tb, cq=cq, q=q: e.matmul(
                        PS[q][:, :], Bn[par * 64:(par + 1) * 64, tb * 128:(tb + 1) * 128],
                        XP[par * 64:(par + 1) * 64, cq:cq + 512],
                        start=True, stop=True, tile_position=(par * 64, 0)))
                ev_smm = p.mmg(smms, waits=[ev_sadd, ev_pair[0]])
                for q in range(4):
                    cq = wave * 2048 + q * 512
                    ev_sadd = p.op("vector", lambda e, cq=cq, q=q: e.scalar_tensor_tensor(
                        out=NSS[:, cq:cq + 512], in0=PS[q][:, :], scalar=1.0,
                        in1=NSS[:, cq:cq + 512], op0=ALU.mult, op1=ALU.add),
                        waits=[ev_smm])

        p.barrier()


        if PHASES <= 8:
            with nc.Block() as block:
                p.finalize(block)
            return nc
        W6 = E[:, 0:512].bitcast(BF16)
        HE = E[:, 512:1024].bitcast(BF16)
        HW = E[:, 1024:1536].bitcast(BF16)
        H1 = E[:, 1536:2048].bitcast(BF16)
        BS6 = E[:, 2048:3072].bitcast(BF16)
        BR6s = E[:, 3072:4096].bitcast(BF16)
        G6 = TMP[:, 0:48]
        GX = W[:, 4096:4608].bitcast(F32)
        WRT = W[:, 4608:4704]
        AST = W[:, 4704:5216]
        ART = W[:, 5216:6752]
        ev_wr = p.dma("gpsimd", lambda e: e.dma_start(
            out=WRT[:].rearrange("p (k n) -> p k n", k=16),
            in_=wr.rearrange("(k p) n -> p k n", k=16)))
        ev_as = p.dma("gpsimd", lambda e: e.dma_start(
            out=AST[:].rearrange("p (k n) -> p k n", k=16),
            in_=as_.rearrange("(k p) n -> p k n", k=16)))
        ev_ar = p.dma("gpsimd", lambda e: e.dma_start(
            out=ART[:].rearrange("p (k n) -> p k n", k=16),
            in_=arT.rearrange("(k p) n -> p k n", k=16)))
        ev_bs6 = p.dma("gpsimd", lambda e: e.dma_start(out=BS6[0:16, :], in_=bs))
        ev_br6 = p.dma("gpsimd", lambda e: e.dma_start(out=BR6s[0:96, :], in_=br6))
        ev_l1 = ldrow("brt", 6)
        evb1 = p.mmg([lambda e: e.matmul(PS[3][:, 0:6], ONES[0:1, :], ROWB[0:1, 0:6],
                                         start=True, stop=True)], waits=[ev_l1])
        rowst["busy"] = evb1
        ev_brtb = p.op("vector", lambda e: e.tensor_copy(TMP[:, 48:54], PS[3][:, 0:6]),
                       waits=[evb1])
        ev_l2 = ldrow("tie", 36)
        evb2 = p.mmg([lambda e: e.matmul(PS[3][:, 64:100], ONES[0:1, :], ROWB[0:1, 0:36],
                                         start=True, stop=True)], waits=[ev_brtb, ev_l2])
        rowst["busy"] = evb2
        ev_tieb = p.op("vector", lambda e: e.tensor_copy(TMP[:, 64:100], PS[3][:, 64:100]),
                       waits=[evb2])
        if G_PARTS <= 1:
            with nc.Block() as block:
                p.finalize(block)
            return nc
        ev_w6w = None
        for tb in range(NTB):
            builds = [lambda e, tb=tb, k=k: e.matmul(
                PS[0][:, 0:6], A[:, k * 1024 + tb * 128:k * 1024 + (tb + 1) * 128],
                WRT[:, k * 6:(k + 1) * 6], start=(k == 0), stop=(k == 15))
                for k in range(16)]
            evm = p.mmg(builds, waits=[ev_wr, ev_w6w])
            el = p.op("vector", lambda e: e.tensor_add(G6[:, 0:6], PS[0][:, 0:6],
                                                       TMP[:, 48:54]), waits=[evm, ev_brtb])
            em = p.op("vector", lambda e: e.reduce_max(STT[:, 0:1], G6[:, 0:6], axis=AX.X),
                      waits=[el])
            em2 = p.op("vector", lambda e: e.tensor_scalar_mul(STT[:, 1:2], STT[:, 0:1], -1.0),
                       waits=[em])
            ee = p.op("scalar", lambda e: e.activation(
                out=G6[:, 8:14], in_=G6[:, 0:6], func=AF.Exp, bias=STT[:, 1:2]),
                waits=[em2])
            es = p.op("vector", lambda e: e.reduce_sum(STT[:, 2:3], G6[:, 8:14], axis=AX.X),
                      waits=[ee])
            er = p.op("vector", lambda e: e.reciprocal(STT[:, 3:4], STT[:, 2:3]), waits=[es])
            eg = p.op("vector", lambda e: e.tensor_scalar(
                out=G6[:, 16:22], in0=G6[:, 8:14], scalar1=STT[:, 3:4], scalar2=None,
                op0=ALU.mult), waits=[er])
            e1 = p.op("vector", lambda e: e.tensor_tensor(
                out=GX[:, 0:36].rearrange("p (a b) -> p a b", a=6),
                in0=G6[:, None, 16:22].broadcast_to([128, 6, 6]),
                in1=G6[:, 16:22, None].broadcast_to([128, 6, 6]),
                op=ALU.is_gt), waits=[eg])
            e2 = p.op("vector", lambda e: e.tensor_tensor(
                out=GX[:, 36:72].rearrange("p (a b) -> p a b", a=6),
                in0=G6[:, None, 16:22].broadcast_to([128, 6, 6]),
                in1=G6[:, 16:22, None].broadcast_to([128, 6, 6]),
                op=ALU.is_equal), waits=[e1])
            e3 = p.op("vector", lambda e: e.tensor_mul(
                GX[:, 36:72], GX[:, 36:72], TMP[:, 64:100]), waits=[e2, ev_tieb])
            e4 = p.op("vector", lambda e: e.tensor_add(
                GX[:, 0:36], GX[:, 0:36], GX[:, 36:72]), waits=[e3])
            e5 = p.op("vector", lambda e: e.tensor_reduce(
                out=GX[:, 72:78, None], in_=GX[:, 0:36].rearrange("p (a b) -> p a b", a=6),
                axis=AX.X, op=ALU.add), waits=[e4])
            e6 = p.op("vector", lambda e: e.tensor_single_scalar(
                GX[:, 78:84], GX[:, 72:78], 2.0, ALU.is_lt), waits=[e5])
            e7 = p.op("vector", lambda e: e.tensor_mul(
                GX[:, 84:90], G6[:, 16:22], GX[:, 78:84]), waits=[e6])
            e8 = p.op("vector", lambda e: e.tensor_copy(W6[0:128, 0:6], GX[:, 84:90]),
                      waits=[e7, ev_w6w])
            ev_w6w = p.dma("sync", lambda e, tb=tb: e.dma_start(
                out=d_w6[tb * 128:(tb + 1) * 128, 0:6], in_=W6[0:128, 0:6]), waits=[e8])
        p.barrier()
        if G_PARTS <= 2:
            with nc.Block() as block:
                p.finalize(block)
            return nc
        ev_w6t = p.dma("scalar", lambda e: e.dma_start(out=W6[:], in_=d_w6, transpose=True))
        if G_PARTS <= 3:
            with nc.Block() as block:
                p.finalize(block)
            return nc
        for h2 in range(2):
            builds = [lambda e, k=k, h2=h2: e.matmul(
                PSB[0:96, h2 * 512:(h2 + 1) * 512], ART[:, k * 96:(k + 1) * 96],
                A[:, k * 1024 + h2 * 512:k * 1024 + (h2 + 1) * 512],
                start=(k == 0), stop=(k == 15)) for k in range(16)]
            evh = p.mmg(builds, waits=[ev_ar])
            p.op("vector", lambda e, h2=h2: e.tensor_copy(
                HE[0:96, h2 * 512:(h2 + 1) * 512], PSB[0:96, h2 * 512:(h2 + 1) * 512]),
                waits=[evh])
        if G_PARTS <= 4:
            with nc.Block() as block:
                p.finalize(block)
            return nc
        prev_h1 = None
        for h2 in range(2):
            builds = [lambda e, k=k, h2=h2: e.matmul(
                PS[1][0:32, 0:512], AST[:, k * 32:(k + 1) * 32],
                A[:, k * 1024 + h2 * 512:k * 1024 + (h2 + 1) * 512],
                start=(k == 0), stop=(k == 15)) for k in range(16)]
            evh = p.mmg(builds, waits=[ev_as, prev_h1])
            prev_h1 = p.op("vector", lambda e, h2=h2: e.tensor_copy(
                H1[0:16, h2 * 512:(h2 + 1) * 512], PS[1][0:16, 0:512]), waits=[evh])
        if G_PARTS <= 5:
            with nc.Block() as block:
                p.finalize(block)
            return nc
        prev_hw = None
        for h2 in range(2):
            evw = p.mmg([lambda e, h2=h2: e.matmul(
                PS[2][0:96, 0:512], EXP6[0:6, :], W6[0:6, h2 * 512:(h2 + 1) * 512],
                start=True, stop=True)], waits=[ev_w6t, prev_hw])
            prev_hw = p.op("vector", lambda e, h2=h2: e.tensor_mul(
                HW[0:96, h2 * 512:(h2 + 1) * 512], HE[0:96, h2 * 512:(h2 + 1) * 512],
                PS[2][0:96, 0:512]), waits=[evw, p.ev("dve")])
        p.barrier()


        if PHASES <= 9:
            with nc.Block() as block:
                p.finalize(block)
            return nc
        for ec in range(32):
            p.dma("scalar", lambda e, ec=ec: e.dma_start(
                out=B[:, ec * 1024:(ec + 1) * 1024], in_=d_yg[:, ec * 128:(ec + 1) * 128],
                transpose=True))
        ev_bbT = []
        prevb = None
        for f in range(16):
            evl = ldrow("bb", 128, off2=f * 128)
            evm = p.mmg([lambda e, f=f: e.matmul(
                PS[3][:, 160 + f:161 + f], ROWB[0:1, 0:128],
                ONES[0:1, 0:1], start=True, stop=True)], waits=[prevb, evl])
            rowst["busy"] = evm
            prevb = p.op("vector", lambda e, f=f: e.tensor_copy(
                TMP[:, 100 + f:101 + f], PS[3][:, 160 + f:161 + f]), waits=[evm])
            ev_bbT.append(prevb)
        ev_wt = [None, None]
        ev_wfree = [None, None]

        def load_wb(j, buf):
            evs = []
            for k in range(16):
                evs.append(p.dma("gpsimd", lambda e, k=k, j=j, buf=buf: e.dma_start(
                    out=WT[buf][:, k * WCOL:(k + 1) * WCOL],
                    in_=wb[k * 128:(k + 1) * 128, j * WCOL:(j + 1) * WCOL]),
                    waits=[ev_wfree[buf]]))
            return evs[-1]

        ev_wt[0] = load_wb(0, 0)
        ev_gev = [None, None, None]
        for j in range(8):
            buf = j % 2
            if j < 7:
                ev_wt[(j + 1) % 2] = load_wb(j + 1, (j + 1) % 2)
            last = None
            for fc2 in range(2):
                f = j * 2 + fc2
                for tt in range(2):
                    ps = PS[(f * 2 + tt) % 3]
                    builds = [lambda e, k=k, fc2=fc2, tt=tt, ps=ps, buf=buf: e.matmul(
                        ps[:, :], WT[buf][:, k * WCOL + fc2 * 128:k * WCOL + (fc2 + 1) * 128],
                        A[:, k * 1024 + tt * 512:k * 1024 + (tt + 1) * 512],
                        start=(k == 0), stop=False) for k in range(16)]
                    builds.append(lambda e, f=f, tt=tt, ps=ps: e.matmul(
                        ps[:, :], BS6[0:16, f * 128:(f + 1) * 128],
                        H1[0:16, tt * 512:(tt + 1) * 512], start=False, stop=False))
                    builds.append(lambda e, f=f, tt=tt, ps=ps: e.matmul(
                        ps[:, :], BR6s[0:96, f * 128:(f + 1) * 128],
                        HW[0:96, tt * 512:(tt + 1) * 512], start=False, stop=True))
                    evm = p.mmg(builds, waits=[ev_wt[buf], ev_bs6, ev_br6,
                                               ev_gev[(f * 2 + tt) % 3]])
                    last = evm
                    ev_gev[(f * 2 + tt) % 3] = p.op(
                        "scalar", lambda e, f=f, tt=tt, ps=ps: e.activation(
                            out=C[:, f * 1024 + tt * 512:f * 1024 + (tt + 1) * 512],
                            in_=ps[:, :], func=AF.Silu, bias=TMP[:, 100 + f:101 + f]),
                        waits=[evm, ev_bbT[f]])
            ev_wfree[buf] = last
        p.barrier()


        if PHASES <= 10:
            with nc.Block() as block:
                p.finalize(block)
            return nc
        UTS = [E[:, 0:1024], E[:, 1024:2048]]
        OUS = [E[:, 2048:2560], E[:, 2560:3072]]
        ev_wt = [None, None]
        ev_wfree = [None, None]

        def load_wo(dchunk, buf):
            evs = []
            for k in range(32):
                evs.append(p.dma("gpsimd", lambda e, k=k, dchunk=dchunk, buf=buf: e.dma_start(
                    out=WT[buf][:, k * 128:(k + 1) * 128],
                    in_=woT[k * 128:(k + 1) * 128, dchunk * 128:(dchunk + 1) * 128]),
                    waits=[ev_wfree[buf]]))
            return evs[-1]

        ev_wt[0] = load_wo(0, 0)
        ev_ut = [None, None]
        ev_ut_rd = [None, None]
        ev_ut[0] = p.dma("sync", lambda e: e.dma_start(out=UTS[0][:], in_=uT[0:128, :]))
        ev_ep = [None, None, None]
        ev_ost = [None, None]
        for dc in range(16):
            buf = dc % 2
            if dc < 15:
                ev_wt[(dc + 1) % 2] = load_wo(dc + 1, (dc + 1) % 2)
                ev_ut[(dc + 1) % 2] = p.dma("sync", lambda e, dc=dc: e.dma_start(
                    out=UTS[(dc + 1) % 2][:], in_=uT[(dc + 1) * 128:(dc + 2) * 128, :]),
                    waits=[ev_ut_rd[(dc + 1) % 2]])
            for tt in range(2):
                ps = PS[(dc * 2 + tt) % 3]
                builds = [lambda e, k=k, tt=tt, ps=ps, buf=buf: e.matmul(
                    ps[:, :], WT[buf][:, k * 128:(k + 1) * 128],
                    B[:, k * 1024 + tt * 512:k * 1024 + (tt + 1) * 512],
                    start=(k == 0), stop=(k == 31)) for k in range(32)]
                evm = p.mmg(builds, waits=[ev_wt[buf], ev_ep[(dc * 2 + tt) % 3]])
                ob = OUS[tt]
                e1 = p.op("vector", lambda e, dc=dc, tt=tt, ps=ps, ob=ob: e.tensor_add(
                    ob, ps[:, :], A[:, dc * 1024 + tt * 512:dc * 1024 + (tt + 1) * 512]),
                    waits=[evm, ev_ost[tt]])
                e2 = p.op("vector", lambda e, dc=dc, tt=tt, ob=ob: e.tensor_mul(
                    ob, ob, C[:, dc * 1024 + tt * 512:dc * 1024 + (tt + 1) * 512]),
                    waits=[e1])
                e3 = p.op("vector", lambda e, dc=dc, tt=tt, ob=ob, buf=buf: e.tensor_add(
                    ob, ob, UTS[buf][:, tt * 512:(tt + 1) * 512]),
                    waits=[e2, ev_ut[buf]])
                ev_ut_rd[buf] = e3
                ev_ep[(dc * 2 + tt) % 3] = e3
                ev_ost[tt] = p.dma("sync", lambda e, dc=dc, tt=tt, ob=ob: e.dma_start(
                    out=outT[dc * 128:(dc + 1) * 128, tt * 512:(tt + 1) * 512], in_=ob),
                    waits=[e3])
        p.barrier()

        with nc.Block() as block:
            p.finalize(block)

    return nc


def _pack_rows(I):
    rows = np.zeros((1, 8192), np.float32)

    def put(off, v):
        v = np.asarray(v, np.float32).reshape(-1)
        rows[0, off:off + v.size] = v

    put(0, I["L0_attn_ln_w"]); put(2048, I["L0_attn_ln_b"]); put(4096, I["L0_ffn_b_base"])
    put(6144, I["L0_attn_bnorm_w"]); put(6272, I["L0_attn_cnorm_w"])
    put(6400, I["L0_attn_D"]); put(6464, I["L0_attn_dt_bias"]); put(6528, I["L0_attn_A_log"])
    put(6592, I["L0_ffn_b_router"])
    tie = (np.arange(6)[None, :] < np.arange(6)[:, None]).astype(np.float32)
    put(6598, tie.reshape(-1))
    return rows


def _consts():
    ones = np.ones((1, 128), np.float32)
    idb = np.eye(128, dtype=np.float32)
    jj, ii = np.meshgrid(np.arange(128), np.arange(128), indexing="ij")
    ut2 = ((jj <= ii) & ((jj // 64) == (ii // 64))).astype(np.float32)
    ut1 = (jj <= ii).astype(np.float32)
    pick = np.zeros((128, 128), np.float32)
    pick[63, 0:64] = 1.0
    pick[127, 64:128] = 1.0
    picke = np.zeros((128, 128), np.float32)
    picke[63, :] = 1.0
    picko = np.zeros((128, 128), np.float32)
    picko[127, :] = 1.0
    m = (np.arange(64)[None, :] >= np.arange(64)[:, None]).astype(np.float32)
    tril = np.concatenate([m, m], axis=0)
    exp6 = np.zeros((6, 96), np.float32)
    for e6 in range(6):
        exp6[e6, e6 * 16:(e6 + 1) * 16] = 1.0
    return ones, idb, ut2, ut1, pick, picke, picko, tril, exp6


def kernel(**inputs):
    I = inputs
    u = np.ascontiguousarray(np.asarray(I["L0_attn_u"], np.float32)[0])
    wiT = np.ascontiguousarray(np.asarray(I["L0_attn_in_proj_w"], np.float32).T)
    woT = np.ascontiguousarray(np.asarray(I["L0_attn_out_proj_w"], np.float32).T)
    wb = np.ascontiguousarray(np.asarray(I["L0_ffn_W_base"], np.float32))
    as_ = np.ascontiguousarray(np.concatenate([
        np.asarray(I["L0_ffn_A_s"], np.float32),
        np.zeros((D, 16), np.float32)], axis=1))
    bs = np.ascontiguousarray(np.asarray(I["L0_ffn_B_s"], np.float32))
    arT = np.ascontiguousarray(
        np.asarray(I["L0_ffn_A_r"], np.float32).transpose(1, 0, 2).reshape(D, 96))
    br6 = np.ascontiguousarray(np.asarray(I["L0_ffn_B_r"], np.float32).reshape(96, D))
    wr = np.ascontiguousarray(np.asarray(I["L0_ffn_W_router"], np.float32))
    rows = _pack_rows(I)
    ones, idb, ut2, ut1, pick, picke, picko, tril, exp6 = _consts()

    nc = build_nc()
    in_maps = []
    for c in range(NCORE):
        sl = slice(c * T, (c + 1) * T)
        u2 = np.zeros((2 * T, D), np.float32)
        if c > 0:
            u2[0:T] = u[(c - 1) * T:c * T]
        u2[T:2 * T] = u[sl]
        r = rows.copy()
        r[0, 6634] = 0.0 if c == 0 else 1.0
        in_maps.append(dict(
            u2=u2, uT=np.ascontiguousarray(u[sl].T), wiT=wiT, woT=woT, wb=wb,
            as_=as_, bs=bs, arT=arT, br6=br6, wr=wr, vrows=r, c_ones=ones,
            c_idb=idb, c_ut2=ut2, c_ut1=ut1, c_pick=pick, c_picke=picke,
            c_picko=picko, c_tril=tril, c_exp6=exp6,
        ))
    res = run_bass_kernel_spmd(nc, in_maps, core_ids=list(range(NCORE)))
    global _LAST_RES
    _LAST_RES = res
    out = np.empty((1, 8192, D), np.float32)
    for c in range(NCORE):
        out[0, c * T:(c + 1) * T, :] = np.asarray(res.results[c]["outT"]).T
    return out


# version 10
# speedup vs baseline: 1.0436x; 1.0079x over previous
import numpy as np
import ml_dtypes
from contextlib import ExitStack

import concourse.bass as bass
import concourse.mybir as mybir
from concourse.bass_utils import run_bass_kernel_spmd

F32 = mybir.dt.float32
BF16 = mybir.dt.bfloat16
AF = mybir.ActivationFunctionType
ALU = mybir.AluOpType
AX = mybir.AxisListType
BF = ml_dtypes.bfloat16

NCORE = 8
T = 1024
NTB = 8
D = 2048
DI = 4096
NH, HD, DSTATE, CH = 64, 64, 128, 64
DPROJ = 8512
NCH = T // CH
WCOL = 256
EPS = 1e-5


class Prog:
    COMPUTE = {"tensor": "pe", "vector": "dve", "scalar": "act", "gpsimd": "pool"}
    DMAQ = {"gpsimd": "qg", "sync": "qs", "scalar": "qa"}
    ENGINES = ("tensor", "vector", "scalar", "gpsimd", "sync")

    def __init__(self, nc):
        self.nc = nc
        self.q = {e: [] for e in self.ENGINES}
        self.semnames = ["pe", "dve", "act", "pool", "qg", "qs", "qa"]
        self.sem = {}
        self.cnt = {s: 0 for s in self.semnames}
        self.waited = {e: {s: 0 for s in self.semnames} for e in self.ENGINES}

    @staticmethod
    def _flat(waits):
        out = []
        for ev in waits:
            if ev is None:
                continue
            if isinstance(ev, list):
                out.extend(Prog._flat(ev))
            else:
                out.append(ev)
        return out

    def _push(self, eng, waits, build, incsem, incn):
        w = []
        for (s, c) in self._flat(waits):
            if c > self.waited[eng][s]:
                w.append((s, c))
                self.waited[eng][s] = c
        target = None
        if incsem is not None:
            self.cnt[incsem] += incn
            target = self.cnt[incsem]
        sem = self.sem

        def run(e, w=tuple(w), build=build, incsem=incsem, target=target, incn=incn):
            for (s, c) in w:
                e.wait_ge(sem[s], c)
            inst = build(e)
            if incsem is not None and inst is not None:
                inst.then_inc(sem[incsem], incn)

        self.q[eng].append(run)
        return (incsem, target) if incsem else None

    def op(self, eng, build, waits=()):
        return self._push(eng, waits, build, self.COMPUTE[eng], 1)

    def mmg(self, builds, waits=()):
        for b in builds[:-1]:
            self._push("tensor", waits, b, None, 0)
            waits = ()
        return self._push("tensor", waits, builds[-1], "pe", 1)

    def dma(self, eng, build, waits=()):
        return self._push(eng, waits, build, self.DMAQ[eng], 16)

    def ev(self, semname):
        return (semname, self.cnt[semname])

    def barrier(self):
        evs = [(s, self.cnt[s]) for s in self.semnames if self.cnt[s] > 0]
        for e in self.ENGINES:
            self._push(e, evs, lambda _e: None, None, 0)

    def finalize(self, block):
        for ename in self.ENGINES:
            q = self.q[ename]
            deco = getattr(block, ename)

            def body(e, q=q):
                for f in q:
                    f(e)

            deco(body)


PHASES = 99
HALO_F = True
G_PARTS = 9
SERIAL_A = False
DEBUG = False
_LAST_RES = None


def build_nc():
    nc = bass.Bass("TRN2", target_bir_lowering=False)
    dt = nc.dram_tensor

    u2 = dt("u2", [2 * T, D], F32, kind="ExternalInput").ap()
    uT = dt("uT", [D, T], F32, kind="ExternalInput").ap()
    wiT = dt("wiT", [D, DPROJ], F32, kind="ExternalInput").ap()
    woT = dt("woT", [DI, D], F32, kind="ExternalInput").ap()
    wb = dt("wb", [D, D], F32, kind="ExternalInput").ap()
    as_ = dt("as_", [D, 32], F32, kind="ExternalInput").ap()
    bs = dt("bs", [16, D], F32, kind="ExternalInput").ap()
    arT = dt("arT", [D, 96], F32, kind="ExternalInput").ap()
    br6 = dt("br6", [96, D], F32, kind="ExternalInput").ap()
    wr = dt("wr", [D, 6], F32, kind="ExternalInput").ap()
    vrows = dt("vrows", [1, 8192], F32, kind="ExternalInput").ap()
    c_ones = dt("c_ones", [1, 128], F32, kind="ExternalInput").ap()
    c_idb = dt("c_idb", [128, 128], F32, kind="ExternalInput").ap()
    c_ut2 = dt("c_ut2", [128, 128], F32, kind="ExternalInput").ap()
    c_ut1 = dt("c_ut1", [128, 128], F32, kind="ExternalInput").ap()
    c_pick = dt("c_pick", [128, 128], F32, kind="ExternalInput").ap()
    c_picke = dt("c_picke", [128, 128], F32, kind="ExternalInput").ap()
    c_picko = dt("c_picko", [128, 128], F32, kind="ExternalInput").ap()
    c_tril = dt("c_tril", [128, 64], F32, kind="ExternalInput").ap()
    c_exp6 = dt("c_exp6", [6, 96], F32, kind="ExternalInput").ap()

    outT = dt("outT", [D, T], F32, kind="ExternalOutput").ap()

    KI = "ExternalOutput" if DEBUG else "Internal"
    d_un = dt("d_un", [T, D], BF16, kind=KI).ap()
    d_unh = dt("d_unh", [T, D], BF16, kind=KI).ap()
    d_xn = dt("d_xn", [T, D], BF16, kind=KI).ap()
    d_bc = dt("d_bc", [T, 2 * DSTATE], BF16, kind=KI).ap()
    d_svz = dt("d_svz", [T, DI], BF16, kind=KI).ap()
    d_yg = dt("d_yg", [T, DI], BF16, kind=KI).ap()
    d_w6 = dt("d_w6", [T, 128], BF16, kind=KI).ap()
    d_nss = dt("d_nss", [128, 4096], F32, kind=KI).ap() if DEBUG else None
    d_sc = dt("d_sc", [128, 4096], F32, kind=KI).ap() if DEBUG else None

    RO = {"lnw": 0, "lnb": 2048, "bb": 4096, "bnw": 6144, "cnw": 6272,
          "d": 6400, "dtb": 6464, "alog": 6528, "brt": 6592, "tie": 6598,
          "flag": 6634}

    with ExitStack() as ctx:
        E_ = ctx.enter_context
        A = E_(nc.sbuf_tensor("A", [128, 16384], BF16))
        B = E_(nc.sbuf_tensor("B", [128, 32768], BF16))
        C = E_(nc.sbuf_tensor("C", [128, 16384], BF16))
        NSS = E_(nc.sbuf_tensor("NSS", [128, 4096], F32))
        E = E_(nc.sbuf_tensor("E", [128, 4096], F32))
        W = E_(nc.sbuf_tensor("W", [128, 8192], BF16))
        Bn = E_(nc.sbuf_tensor("Bn", [128, 1024], BF16))
        Cn = E_(nc.sbuf_tensor("Cn", [128, 1024], BF16))
        Bh = E_(nc.sbuf_tensor("Bh", [128, 1024], BF16))
        DSC = E_(nc.sbuf_tensor("DSC", [128, 512], BF16))
        PST = E_(nc.sbuf_tensor("PST", [128, 512], BF16))
        GDS = E_(nc.sbuf_tensor("GDS", [128, 512], BF16))
        TREP = E_(nc.sbuf_tensor("TREP", [128, 1024], BF16))
        BFM = E_(nc.sbuf_tensor("BFM", [128, 1024], BF16))
        CFM = E_(nc.sbuf_tensor("CFM", [128, 1024], BF16))
        CBT = E_(nc.sbuf_tensor("CBT", [128, 512], BF16))
        IDB = E_(nc.sbuf_tensor("IDB", [128, 128], BF16))
        UT2 = E_(nc.sbuf_tensor("UT2", [128, 128], F32))
        UT1 = E_(nc.sbuf_tensor("UT1", [128, 128], F32))
        PICK = E_(nc.sbuf_tensor("PICK", [128, 128], F32))
        PICKE = E_(nc.sbuf_tensor("PICKE", [128, 128], F32))
        PICKO = E_(nc.sbuf_tensor("PICKO", [128, 128], F32))
        TRILM = E_(nc.sbuf_tensor("TRILM", [128, 64], F32))
        ONES = E_(nc.sbuf_tensor("ONES", [1, 128], F32))
        ROWB = E_(nc.sbuf_tensor("ROWB", [1, 512], F32))
        EXP6 = E_(nc.sbuf_tensor("EXP6", [6, 96], BF16))
        DD = E_(nc.sbuf_tensor("DD", [128, 64], BF16))
        DTB = E_(nc.sbuf_tensor("DTB", [128, 64], F32))
        ANEG = E_(nc.sbuf_tensor("ANEG", [128, 64], F32))
        BNWB = E_(nc.sbuf_tensor("BNWB", [128, 128], BF16))
        CNWB = E_(nc.sbuf_tensor("CNWB", [128, 128], BF16))
        STT = E_(nc.sbuf_tensor("STT", [128, 64], F32))
        TMP = E_(nc.sbuf_tensor("TMP", [128, 256], F32))
        dAm = E[:, 0:512]
        dAh = E[:, 512:1024]
        GAS = E[:, 1024:1536]
        PS = [E_(nc.psum_tensor(f"PS{i}", [128, 512], F32)) for i in range(4)]
        PSB = E_(nc.psum_tensor("PSB", [128, 2048], F32))

        p = Prog(nc)
        for s in p.semnames:
            p.sem[s] = E_(nc.semaphore(s))

        rowst = {"busy": None}

        def ldrow(name, w, off2=0):
            o = RO[name] + off2
            ev = p.dma("sync", lambda e, o=o, w=w: e.dma_start(
                out=ROWB[0:1, 0:w], in_=vrows[0:1, o:o + w]), waits=[rowst["busy"]])
            return ev

        p.dma("sync", lambda e: e.dma_start(out=ONES[:], in_=c_ones))
        p.dma("sync", lambda e: e.dma_start(out=UT2[:], in_=c_ut2))
        p.dma("sync", lambda e: e.dma_start(out=UT1[:], in_=c_ut1))
        p.dma("sync", lambda e: e.dma_start(out=PICK[:], in_=c_pick))
        p.dma("sync", lambda e: e.dma_start(out=PICKE[:], in_=c_picke))
        p.dma("sync", lambda e: e.dma_start(out=PICKO[:], in_=c_picko))
        p.dma("sync", lambda e: e.dma_start(out=TRILM[:], in_=c_tril))
        p.dma("gpsimd", lambda e: e.dma_start(out=IDB[:], in_=c_idb))
        p.dma("gpsimd", lambda e: e.dma_start(out=EXP6[:], in_=c_exp6))
        ev_qs0 = p.ev("qs")

        bc_prev = [None]

        def bcast_name(name, width, dst, off2=0, prep=None):
            evl = ldrow(name, width, off2)
            evp_ = evl
            if prep is not None:
                evp_ = prep(evl)
            evm = p.mmg([lambda e, width=width: e.matmul(
                PS[3][:, 0:width], ONES[0:1, :], ROWB[0:1, 0:width],
                start=True, stop=True)], waits=[ev_qs0, evp_, bc_prev[0]])
            rowst["busy"] = evm
            evc = p.op("vector", lambda e, dst=dst, width=width: e.tensor_copy(
                dst, PS[3][:, 0:width]), waits=[evm])
            bc_prev[0] = evc
            return evc

        def _alog_prep(evl):
            ea = p.op("scalar", lambda e: e.activation(
                out=ROWB[0:1, 0:NH], in_=ROWB[0:1, 0:NH], func=AF.Exp), waits=[evl])
            return p.op("vector", lambda e: e.tensor_scalar_mul(
                ROWB[0:1, 0:NH], ROWB[0:1, 0:NH], -1.0), waits=[ea])

        ev_dd = bcast_name("d", NH, DD[:, :])
        ev_dtb = bcast_name("dtb", NH, DTB[:, :])
        ev_aneg = bcast_name("alog", NH, ANEG[:, :], prep=_alog_prep)
        ev_bnw = bcast_name("bnw", DSTATE, BNWB[:, :])
        ev_cnw = bcast_name("cnw", DSTATE, CNWB[:, :])
        LNW = B[:, 0:2048]
        LNB = B[:, 2048:4096]
        for j in range(4):
            bcast_name("lnw", 512, LNW[:, j * 512:(j + 1) * 512], off2=j * 512)
        for j in range(4):
            ev_lnwb = bcast_name("lnb", 512, LNB[:, j * 512:(j + 1) * 512], off2=j * 512)


        if PHASES <= 1:
            with nc.Block() as block:
                p.finalize(block)
            return nc
        p.barrier()

        UBv = B[:, 8192:24576].bitcast(F32)
        Ust = [UBv[:, j * 2048:(j + 1) * 2048] for j in range(4)]
        XNS = [W[:, 0:2048], W[:, 2048:4096]]
        UNS = [W[:, 4096:6144], W[:, 6144:8192]]
        SQS = B[:, 4096:8192]
        ev_ub_free = [None] * 4
        ev_cc_free = [None] * 4
        ev_xb_free = [None, None]
        SB = {}

        def stage1(i):
            p4 = i % 4
            ub = Ust[p4]
            sq = SQS[:, (i % 2) * 2048:(i % 2 + 1) * 2048]
            off = p4 * 12
            cc = {k: STT[:, off + j:off + j + 1] for j, k in enumerate(
                ["sq", "sum", "m", "msq", "v", "a", "r1", "w", "b", "negm"])}
            ld = p.dma("sync", lambda e, i=i, ub=ub: e.dma_start(
                out=ub, in_=u2[i * 128:(i + 1) * 128, :]), waits=[ev_ub_free[p4]])
            e1 = p.op("scalar", lambda e, ub=ub, sq=sq, c=cc["sq"]: e.activation(
                out=sq, in_=ub, func=AF.Square, accum_out=c),
                waits=[ld, ev_cc_free[p4]])
            e2 = p.op("vector", lambda e, ub=ub, c=cc["sum"]: e.reduce_sum(
                c, ub, axis=AX.X), waits=[ld, ev_cc_free[p4]])
            e3 = p.op("vector", lambda e, cc=cc: e.tensor_scalar_mul(
                cc["m"], cc["sum"], 1.0 / D), waits=[e2])
            e4 = p.op("vector", lambda e, cc=cc: e.tensor_scalar_mul(
                cc["msq"], cc["sq"], 1.0 / D), waits=[e1])
            e5 = p.op("vector", lambda e, cc=cc: e.tensor_scalar(
                out=cc["v"], in0=cc["m"], scalar1=cc["m"], scalar2=None, op0=ALU.mult),
                waits=[e3])
            e6 = p.op("vector", lambda e, cc=cc: e.tensor_sub(cc["v"], cc["msq"], cc["v"]),
                      waits=[e4, e5])
            e7 = p.op("vector", lambda e, cc=cc: e.tensor_scalar_add(cc["a"], cc["v"], EPS),
                      waits=[e6])
            e8 = p.op("scalar", lambda e, cc=cc: e.activation(
                out=cc["r1"], in_=cc["a"], func=AF.Ln), waits=[e7])
            e8b = p.op("vector", lambda e, cc=cc: e.tensor_scalar_mul(
                cc["r1"], cc["r1"], -0.5), waits=[e8])
            e9 = p.op("scalar", lambda e, cc=cc: e.activation(
                out=cc["r1"], in_=cc["r1"], func=AF.Exp), waits=[e8b])
            e10 = p.op("vector", lambda e, cc=cc: e.reciprocal(cc["w"], cc["a"]), waits=[e7])
            e11 = p.op("vector", lambda e, cc=cc: e.tensor_scalar(
                out=cc["w"], in0=cc["w"], scalar1=cc["v"], scalar2=EPS, op0=ALU.mult,
                op1=ALU.add), waits=[e10])
            e12 = p.op("scalar", lambda e, cc=cc: e.activation(
                out=cc["b"], in_=cc["w"], func=AF.Ln), waits=[e11])
            e12b = p.op("vector", lambda e, cc=cc: e.tensor_scalar_mul(
                cc["b"], cc["b"], -0.5), waits=[e12])
            e13 = p.op("scalar", lambda e, cc=cc: e.activation(
                out=cc["b"], in_=cc["b"], func=AF.Exp), waits=[e12b])
            e15 = p.op("vector", lambda e, cc=cc: e.tensor_scalar_mul(
                cc["negm"], cc["m"], -1.0), waits=[e3])
            SB[i] = dict(cc=cc, ub=ub, e1=e1, e2=e2, e9=e9, e13=e13, e15=e15)

        def stage2(i):
            halo = i < 8
            tb = i % 8
            p4 = i % 4
            st = SB.pop(i)
            cc, ub = st["cc"], st["ub"]
            xb, nb = XNS[i % 2], UNS[i % 2]
            exn = p.op("vector", lambda e, ub=ub, xb=xb, cc=cc: e.tensor_scalar(
                out=xb, in0=ub, scalar1=cc["negm"], scalar2=cc["r1"], op0=ALU.add,
                op1=ALU.mult), waits=[st["e15"], st["e9"], ev_xb_free[i % 2]])
            eun1 = p.op("vector", lambda e, xb=xb, nb=nb, cc=cc: e.scalar_tensor_tensor(
                out=nb, in0=xb, scalar=cc["b"], in1=LNW, op0=ALU.mult, op1=ALU.mult),
                waits=[exn, st["e13"], ev_lnwb])
            eun2 = p.op("gpsimd", lambda e, nb=nb: e.tensor_add(nb, nb, LNB), waits=[eun1])
            frees = [exn, eun1, eun2]
            if not halo:
                exs = p.dma("sync", lambda e, tb=tb, xb=xb: e.dma_start(
                    out=d_xn[tb * 128:(tb + 1) * 128, :], in_=xb), waits=[exn])
                frees.append(exs)
            dun = d_unh if halo else d_un
            eus = p.dma("sync", lambda e, tb=tb, nb=nb, dun=dun: e.dma_start(
                out=dun[tb * 128:(tb + 1) * 128, :], in_=nb), waits=[eun2])
            frees.append(eus)
            ev_xb_free[i % 2] = frees
            ev_ub_free[p4] = [st["e1"], st["e2"], exn]
            ev_cc_free[p4] = [exn, eun1]

        for i in range(16):
            stage1(i)
            if i >= 1:
                stage2(i - 1)
        stage2(15)

        if PHASES <= 2:
            with nc.Block() as block:
                p.finalize(block)
            return nc
        p.barrier()
        for k in range(16):
            p.dma("scalar", lambda e, k=k: e.dma_start(
                out=A[:, k * 1024:(k + 1) * 1024], in_=d_un[:, k * 128:(k + 1) * 128],
                transpose=True))
            p.dma("scalar", lambda e, k=k: e.dma_start(
                out=C[:, k * 1024:(k + 1) * 1024], in_=d_unh[:, k * 128:(k + 1) * 128],
                transpose=True))
        p.barrier()

        def lhs_un(i, k):
            reg = C if i < 8 else A
            tb = i % 8
            return reg[:, k * 1024 + tb * 128: k * 1024 + (tb + 1) * 128]


        if PHASES <= 3:
            with nc.Block() as block:
                p.finalize(block)
            return nc
        NB = 320
        ev_wb = None
        for k in range(16):
            ev_wb = p.dma("gpsimd", lambda e, k=k: e.dma_start(
                out=W[:, k * NB:(k + 1) * NB],
                in_=wiT[k * 128:(k + 1) * 128, 8192:8512]))
        ev_evac3 = [None, None, None]
        for i in range(16):
            halo = i < 8
            tb = i % 8
            ps = PS[i % 3]
            builds = [lambda e, i=i, k=k, ps=ps: e.matmul(
                ps[:, 0:NB], lhs_un(i, k), W[:, k * NB:(k + 1) * NB],
                start=(k == 0), stop=(k == 15)) for k in range(16)]
            evm = p.mmg(builds, waits=[ev_wb, ev_evac3[i % 3]])
            readers = []
            bn_dst = Bh if halo else Bn
            c_bs, c_br = STT[:, 53:54], STT[:, 54:55]
            eb1 = p.op("scalar", lambda e, ps=ps, c=c_bs: e.activation(
                out=TMP[:, 0:128], in_=ps[:, 0:128], func=AF.Square, accum_out=c),
                waits=[evm, ev_evac3[i % 3]])
            eb1b = p.op("vector", lambda e: e.tensor_scalar(
                out=c_br, in0=c_bs, scalar1=1.0 / DSTATE, scalar2=EPS, op0=ALU.mult,
                op1=ALU.add), waits=[eb1])
            eb2 = p.op("scalar", lambda e: e.activation(
                out=c_br, in_=c_br, func=AF.Ln), waits=[eb1b])
            eb2b = p.op("vector", lambda e: e.tensor_scalar_mul(c_br, c_br, -0.5),
                        waits=[eb2])
            eb3 = p.op("scalar", lambda e: e.activation(
                out=c_br, in_=c_br, func=AF.Exp), waits=[eb2b])
            eb4 = p.op("vector", lambda e, ps=ps, bn_dst=bn_dst, tb=tb: e.scalar_tensor_tensor(
                out=bn_dst[:, tb * 128:(tb + 1) * 128], in0=ps[:, 0:128], scalar=c_br,
                in1=BNWB[:, :], op0=ALU.mult, op1=ALU.mult), waits=[eb3, ev_bnw])
            readers += [eb1, eb4]
            if not halo:
                c_cs, c_cr = STT[:, 55:56], STT[:, 56:57]
                ec1 = p.op("scalar", lambda e, ps=ps, c=c_cs: e.activation(
                    out=TMP[:, 0:128], in_=ps[:, 128:256], func=AF.Square, accum_out=c),
                    waits=[eb1])
                ec1b = p.op("vector", lambda e: e.tensor_scalar(
                    out=c_cr, in0=c_cs, scalar1=1.0 / DSTATE, scalar2=EPS, op0=ALU.mult,
                    op1=ALU.add), waits=[ec1])
                ec2 = p.op("scalar", lambda e: e.activation(
                    out=c_cr, in_=c_cr, func=AF.Ln), waits=[ec1b])
                ec2b = p.op("vector", lambda e: e.tensor_scalar_mul(c_cr, c_cr, -0.5),
                            waits=[ec2])
                ec3 = p.op("scalar", lambda e: e.activation(
                    out=c_cr, in_=c_cr, func=AF.Exp), waits=[ec2b])
                ec4 = p.op("vector", lambda e, ps=ps, tb=tb: e.scalar_tensor_tensor(
                    out=Cn[:, tb * 128:(tb + 1) * 128], in0=ps[:, 128:256], scalar=c_cr,
                    in1=CNWB[:, :], op0=ALU.mult, op1=ALU.mult), waits=[ec3, ev_cnw])
                readers += [ec1, ec4]
            da_dst = dAh if halo else dAm
            ed1 = p.op("vector", lambda e, ps=ps: e.tensor_add(
                TMP[:, 128:192], ps[:, 256:320], DTB[:, :]), waits=[evm, ev_dtb, eb4])
            ed2 = p.op("scalar", lambda e: e.activation(
                out=TMP[:, 192:256], in_=TMP[:, 128:192], func=AF.Abs), waits=[ed1])
            ed3 = p.op("scalar", lambda e: e.activation(
                out=TMP[:, 192:256], in_=TMP[:, 192:256], func=AF.Exp, scale=-1.0),
                waits=[ed2])
            ed4 = p.op("scalar", lambda e: e.activation(
                out=TMP[:, 192:256], in_=TMP[:, 192:256], func=AF.Ln, bias=1.0), waits=[ed3])
            ed5 = p.op("vector", lambda e: e.tensor_scalar_max(
                TMP[:, 128:192], TMP[:, 128:192], 0.0), waits=[ed2])
            ed6 = p.op("vector", lambda e: e.tensor_add(
                TMP[:, 128:192], TMP[:, 128:192], TMP[:, 192:256]), waits=[ed4, ed5])
            ed7 = p.op("vector", lambda e, da_dst=da_dst, tb=tb: e.tensor_mul(
                da_dst[:, tb * 64:(tb + 1) * 64], TMP[:, 128:192], ANEG[:, :]),
                waits=[ed6, ev_aneg])
            readers.append(ed1)
            ev_evac3[i % 3] = readers + [ed7]

        p.barrier()


        if PHASES <= 4:
            with nc.Block() as block:
                p.finalize(block)
            return nc
        prev_ed = prev_ds = prev_te = prev_to = None
        for tb in range(NTB):
            evm = p.mmg([lambda e, tb=tb: e.matmul(
                PS[0][:, 0:64], UT2[:], dAm[:, tb * 64:(tb + 1) * 64],
                start=True, stop=True)], waits=[prev_ed])
            eac = p.op("vector", lambda e: e.tensor_copy(TMP[:, 0:64], PS[0][:, 0:64]),
                       waits=[evm])
            evp = p.mmg([lambda e: e.matmul(
                PS[1][:, 0:64], PICK[:, 0:128], TMP[:, 0:64], start=True, stop=True)],
                waits=[eac])
            ed = p.op("vector", lambda e: e.tensor_sub(
                TMP[:, 64:128], PS[1][:, 0:64], TMP[:, 0:64]), waits=[evp, prev_ds])
            ds1 = p.op("scalar", lambda e, tb=tb: e.activation(
                out=DSC[:, tb * 64:(tb + 1) * 64], in_=TMP[:, 64:128], func=AF.Exp),
                waits=[ed])
            ds2 = p.op("scalar", lambda e, tb=tb: e.activation(
                out=PST[:, tb * 64:(tb + 1) * 64], in_=TMP[:, 64:128], func=AF.Exp,
                scale=-1.0), waits=[ed])
            prev_ds = [ds1, ds2]
            prev_ed = ed
            ev_e = p.mmg([lambda e: e.matmul(
                PS[2][:, 0:64], PICKE[:], TMP[:, 0:64], start=True, stop=True)],
                waits=[eac, prev_te])
            prev_te = p.op("scalar", lambda e, tb=tb: e.activation(
                out=TREP[:, (2 * tb) * 64:(2 * tb + 1) * 64], in_=PS[2][:, 0:64],
                func=AF.Exp), waits=[ev_e])
            ev_o = p.mmg([lambda e: e.matmul(
                PS[2][:, 64:128], PICKO[:], TMP[:, 0:64], start=True, stop=True)],
                waits=[eac, prev_to])
            prev_to = p.op("scalar", lambda e, tb=tb: e.activation(
                out=TREP[:, (2 * tb + 1) * 64:(2 * tb + 2) * 64], in_=PS[2][:, 64:128],
                func=AF.Exp), waits=[ev_o])

        OFFR = TMP[0:1, 128:192]
        prev = p.op("vector", lambda e: e.memset(OFFR[:, 0:64], 0.0),
                    waits=[p.ev("act")])
        for tb in range(NTB):
            evm = p.mmg([lambda e, tb=tb: e.matmul(
                PS[0][:, 0:64], UT1[:], dAh[:, tb * 64:(tb + 1) * 64],
                start=True, stop=True)], waits=[p.ev("dve"), p.ev("act")])
            evb = p.mmg([lambda e: e.matmul(
                PS[1][:, 0:64], ONES[0:1, :], OFFR[:, 0:64], start=True, stop=True)],
                waits=[prev])
            ecb = p.op("vector", lambda e: e.tensor_copy(TMP[:, 64:128], PS[1][:, 0:64]),
                       waits=[evb])
            ega = p.op("vector", lambda e, tb=tb: e.tensor_add(
                GAS[:, tb * 64:(tb + 1) * 64], PS[0][:, 0:64], TMP[:, 64:128]),
                waits=[evm, ecb])
            evr = p.mmg([lambda e, tb=tb: e.matmul(
                PS[2][0:1, 0:64], PICK[:, 64:65], GAS[:, tb * 64:(tb + 1) * 64],
                start=True, stop=True)], waits=[ega])
            prev = p.op("vector", lambda e: e.tensor_copy(OFFR[:, 0:64], PS[2][0:1, 0:64]),
                        waits=[evr])
        evb = p.mmg([lambda e: e.matmul(
            PS[1][:, 0:64], ONES[0:1, :], OFFR[:, 0:64], start=True, stop=True)],
            waits=[prev])
        egb = p.op("vector", lambda e: e.tensor_copy(TMP[:, 0:64], PS[1][:, 0:64]),
                   waits=[evb])
        ev_gds = []
        for tb in range(NTB):
            eg1 = p.op("vector", lambda e, tb=tb: e.tensor_sub(
                GAS[:, tb * 64:(tb + 1) * 64], TMP[:, 0:64], GAS[:, tb * 64:(tb + 1) * 64]),
                waits=[egb])
            ev_gds.append(p.op("scalar", lambda e, tb=tb: e.activation(
                out=GDS[:, tb * 64:(tb + 1) * 64], in_=GAS[:, tb * 64:(tb + 1) * 64],
                func=AF.Exp), waits=[eg1]))
        ev_fl = ldrow("flag", 1)
        evf = p.mmg([lambda e: e.matmul(
            PS[1][:, 0:1], ONES[0:1, :], ROWB[0:1, 0:1], start=True, stop=True)],
            waits=[egb, ev_fl])
        rowst["busy"] = evf
        evf = p.op("vector", lambda e: e.tensor_copy(STT[:, 52:53], PS[1][:, 0:1]),
                   waits=[evf])
        p.op("vector", lambda e: e.tensor_scalar(
            out=GDS[:], in0=GDS[:], scalar1=STT[:, 52:53], scalar2=None, op0=ALU.mult),
            waits=[evf] + ev_gds)

        p.barrier()


        if PHASES <= 5:
            with nc.Block() as block:
                p.finalize(block)
            return nc
        WT = [W[:, 0:4096], W[:, 4096:8192]]
        HST = [E[:, i * 128:(i + 1) * 128].bitcast(BF16) for i in range(8)]
        ev_wt = [None, None]
        ev_wfree = [None, None]

        def load_tile(col0, buf):
            evs = []
            for k in range(16):
                evs.append(p.dma("gpsimd", lambda e, k=k, col0=col0, buf=buf: e.dma_start(
                    out=WT[buf][:, k * WCOL:(k + 1) * WCOL],
                    in_=wiT[k * 128:(k + 1) * 128, col0:col0 + WCOL]),
                    waits=[ev_wfree[buf]]))
            return evs[-1]

        ev_wt[0] = load_tile(4096, 0)
        ev_evac3 = [None, None, None]
        ev_hst_free = [None, None]
        ev_fchain = None
        for j in range(16):
            buf = j % 2
            if j < 15:
                ev_wt[(j + 1) % 2] = load_tile(4096 + (j + 1) * WCOL, (j + 1) % 2)
            last_mm = None
            for i in range(16):
                halo = i < 8
                tb = i % 8
                ps = PS[i % 3]
                builds = [lambda e, i=i, k=k, ps=ps, buf=buf: e.matmul(
                    ps[:, 0:WCOL], lhs_un(i, k), WT[buf][:, k * WCOL:(k + 1) * WCOL],
                    start=(k == 0), stop=(k == 15)) for k in range(16)]
                evm = p.mmg(builds, waits=[ev_wt[buf], ev_evac3[i % 3]])
                last_mm = evm
                if halo and not HALO_F:
                    ev_evac3[i % 3] = evm
                elif halo:
                    hb = HST[tb]
                    exx = p.op("vector", lambda e, ps=ps, hb=hb, tb=tb, j=j: e.tensor_tensor(
                        out=hb[:].rearrange("p (h q) -> p h q", h=4),
                        in0=ps[:, 0:WCOL].rearrange("p (h q) -> p h q", h=4),
                        in1=GDS[:, tb * 64 + j * 4: tb * 64 + (j + 1) * 4, None]
                            .broadcast_to([128, 4, 64]),
                        op=ALU.mult), waits=[evm, ev_hst_free[0]])
                    ev_exx = exx
                    ev_evac3[i % 3] = exx
                    if tb == 7 and HALO_F != "noF":
                        fb = []
                        for tb2 in range(8):
                            fb.append(lambda e, tb2=tb2: e.matmul(
                                PSB[:, 0:256],
                                Bh[:, tb2 * 128:(tb2 + 1) * 128],
                                HST[tb2][:, :],
                                start=(tb2 == 0), stop=(tb2 == 7)))
                        ev_hst_free[0] = p.mmg(fb, waits=[p.ev("dve"), ev_fchain])
                else:
                    ecp = p.op("vector", lambda e, ps=ps, tb=tb, j=j: e.tensor_copy(
                        B[:, tb * 4096 + j * WCOL: tb * 4096 + (j + 1) * WCOL],
                        ps[:, 0:WCOL]), waits=[evm])
                    ev_evac3[i % 3] = ecp
            ev_wfree[buf] = last_mm
            if HALO_F is True:
                ev_fchain = p.op("vector", lambda e, j=j: e.tensor_copy(
                    NSS[:, j * 256:(j + 1) * 256], PSB[:, 0:256]),
                    waits=[ev_hst_free[0]])

        p.barrier()
        if DEBUG == 1:
            p.dma("sync", lambda e: e.dma_start(out=d_nss, in_=NSS[:]))
            p.op("vector", lambda e: e.tensor_copy(E[:, 0:256], DSC[:].bitcast(F32)),
                 waits=[p.ev("qs")])
            p.op("vector", lambda e: e.tensor_copy(E[:, 256:512], PST[:].bitcast(F32)))
            p.op("vector", lambda e: e.tensor_copy(E[:, 512:1024], TREP[:].bitcast(F32)))
            p.op("vector", lambda e: e.tensor_copy(E[:, 1024:1280], GDS[:].bitcast(F32)))
            p.dma("sync", lambda e: e.dma_start(out=d_sc, in_=E[:]), waits=[p.ev("dve")])
            p.barrier()

        if PHASES <= 6:
            with nc.Block() as block:
                p.finalize(block)
            return nc
        SVS = [E[:, 0:128].bitcast(BF16), E[:, 128:256].bitcast(BF16)]
        ev_wt = [None, None]
        ev_wfree = [None, None]
        ev_wt[0] = load_tile(0, 0)
        ev_sv = [None, None]
        ev_evac3 = [None, None, None]
        for j in range(16):
            buf = j % 2
            if j < 15:
                ev_wt[(j + 1) % 2] = load_tile((j + 1) * WCOL, (j + 1) % 2)
            last_mm = None
            for tb in range(8):
                i = 8 + tb
                ps = PS[tb % 3]
                sv = SVS[tb % 2]
                builds = [lambda e, i=i, k=k, ps=ps, buf=buf: e.matmul(
                    ps[:, 0:WCOL], lhs_un(i, k), WT[buf][:, k * WCOL:(k + 1) * WCOL],
                    start=(k == 0), stop=(k == 15)) for k in range(16)]
                evm = p.mmg(builds, waits=[ev_wt[buf], ev_evac3[tb % 3]])
                last_mm = evm
                esi = p.op("scalar", lambda e, ps=ps, sv=sv: e.activation(
                    out=sv, in_=ps[:, 0:WCOL], func=AF.Silu), waits=[evm, ev_sv[tb % 2]])
                ev_evac3[tb % 3] = esi
                ev_sv[tb % 2] = p.dma("sync", lambda e, tb=tb, j=j, sv=sv: e.dma_start(
                    out=d_svz[tb * 128:(tb + 1) * 128, j * WCOL:(j + 1) * WCOL], in_=sv),
                    waits=[esi])
            ev_wfree[buf] = last_mm

        for tb in range(NTB):
            p.dma("sync", lambda e, tb=tb: e.dma_start(
                out=d_bc[tb * 128:(tb + 1) * 128, 0:128], in_=Bn[:, tb * 128:(tb + 1) * 128]))
            p.dma("sync", lambda e, tb=tb: e.dma_start(
                out=d_bc[tb * 128:(tb + 1) * 128, 128:256], in_=Cn[:, tb * 128:(tb + 1) * 128]))
        ev_bcw = p.ev("qs")
        p.dma("scalar", lambda e: e.dma_start(out=BFM[:], in_=d_bc[:, 0:128], transpose=True),
              waits=[ev_bcw])
        p.dma("scalar", lambda e: e.dma_start(out=CFM[:], in_=d_bc[:, 128:256], transpose=True),
              waits=[ev_bcw])
        ev_bcr = p.ev("qa")

        prev_cbt = None
        for z in range(NCH):
            par = z % 2
            evm = p.mmg([lambda e, z=z, par=par: e.matmul(
                PS[0][par * 64:(par + 1) * 64, 0:64], BFM[:, z * 64:(z + 1) * 64],
                CFM[:, z * 64:(z + 1) * 64], start=True, stop=True,
                tile_position=(0, par * 64))], waits=[ev_bcr, prev_cbt, p.ev("act")])
            prev_cbt = p.op("vector", lambda e, z=z, par=par: e.tensor_mul(
                CBT[par * 64:(par + 1) * 64, (z // 2) * 64:(z // 2 + 1) * 64],
                PS[0][par * 64:(par + 1) * 64, 0:64],
                TRILM[par * 64:(par + 1) * 64, :]), waits=[evm])
        for k in range(16):
            p.dma("scalar", lambda e, k=k: e.dma_start(
                out=A[:, k * 1024:(k + 1) * 1024], in_=d_xn[:, k * 128:(k + 1) * 128],
                transpose=True), waits=[p.ev("pe")])
        p.barrier()


        if PHASES <= 7:
            with nc.Block() as block:
                p.finalize(block)
            return nc
        XP = C[:, 0:4096]
        XPD = C[:, 4096:8192]
        SPZ = C[:, 8192:12288]
        NSB = C[:, 12288:16384]
        YG = W[:, 0:4096]
        SVZ = [E[:, 0:2048].bitcast(BF16), E[:, 2048:4096].bitcast(BF16)]
        ev_svld = [None, None]
        ev_svld[0] = p.dma("sync", lambda e: e.dma_start(out=SVZ[0][:], in_=d_svz[0:128, :]))
        ev_pair = [None, None, None]
        ev_sadd = None
        ev_smm = None
        ev_ymm = None
        ev_yev = [None, None]
        ev_ygdma = [None, None]
        for z in range(NCH):
            tb = z // 2
            par = z % 2
            if par == 0:
                if tb < 7:
                    ev_svld[(tb + 1) % 2] = p.dma("sync", lambda e, tb=tb: e.dma_start(
                        out=SVZ[(tb + 1) % 2][:], in_=d_svz[(tb + 1) * 128:(tb + 2) * 128, :]),
                        waits=[ev_pair[2]])
                exp_ = p.op("gpsimd", lambda e, tb=tb: e.tensor_tensor(
                    out=XP[:].rearrange("p (h q) -> p h q", h=NH),
                    in0=B[:, tb * 4096:(tb + 1) * 4096].rearrange("p (h q) -> p h q", h=NH),
                    in1=DSC[:, tb * 64:(tb + 1) * 64, None].broadcast_to([128, NH, HD]),
                    op=ALU.mult), waits=[ev_ymm, ev_smm])
                exd = p.op("gpsimd", lambda e: e.tensor_tensor(
                    out=XPD[:].rearrange("p (h q) -> p h q", h=NH),
                    in0=XP[:].rearrange("p (h q) -> p h q", h=NH),
                    in1=DD[:, :, None].broadcast_to([128, NH, HD]),
                    op=ALU.mult), waits=[exp_, ev_dd, ev_ymm])
                esp = p.op("gpsimd", lambda e, tb=tb: e.tensor_tensor(
                    out=SPZ[:].rearrange("p (h q) -> p h q", h=NH),
                    in0=SVZ[tb % 2][:].rearrange("p (h q) -> p h q", h=NH),
                    in1=PST[:, tb * 64:(tb + 1) * 64, None].broadcast_to([128, NH, HD]),
                    op=ALU.mult), waits=[ev_svld[tb % 2], ev_yev[0], ev_yev[1]])
                ev_pair = [exp_, exd, esp]
            emul = p.op("gpsimd", lambda e, z=z: e.tensor_tensor(
                out=NSS[:].rearrange("p (h q) -> p h q", h=NH),
                in0=NSS[:].rearrange("p (h q) -> p h q", h=NH),
                in1=TREP[:, z * 64:(z + 1) * 64, None].broadcast_to([128, NH, HD]),
                op=ALU.mult), waits=[ev_sadd, ev_ymm])
            ev_nsb = p.op("vector", lambda e: e.tensor_copy(NSB[:], NSS[:]), waits=[emul])
            for hh in range(2):
                b1 = []
                for q in range(4):
                    cs = hh * 2048 + q * 512
                    b1.append(lambda e, par=par, z=z, cs=cs, q=q: e.matmul(
                        PSB[par * 64:(par + 1) * 64, q * 512:(q + 1) * 512],
                        CBT[par * 64:(par + 1) * 64, (z // 2) * 64:(z // 2 + 1) * 64],
                        XP[par * 64:(par + 1) * 64, cs:cs + 512],
                        start=True, stop=False, tile_position=(par * 64, par * 64)))
                for q in range(4):
                    cs = hh * 2048 + q * 512
                    b1.append(lambda e, par=par, cs=cs, q=q: e.matmul(
                        PSB[par * 64:(par + 1) * 64, q * 512:(q + 1) * 512],
                        IDB[par * 64:(par + 1) * 64, par * 64:(par + 1) * 64],
                        XPD[par * 64:(par + 1) * 64, cs:cs + 512],
                        start=False, stop=False, tile_position=(par * 64, par * 64)))
                p.mmg(b1, waits=[ev_pair[0], ev_pair[1], ev_yev[0], ev_yev[1]])
                b2 = []
                for q in range(4):
                    cs = hh * 2048 + q * 512
                    b2.append(lambda e, par=par, z=z, cs=cs, q=q: e.matmul(
                        PSB[par * 64:(par + 1) * 64, q * 512:(q + 1) * 512],
                        CFM[:, z * 64:(z + 1) * 64], NSB[:, cs:cs + 512],
                        start=False, stop=True, tile_position=(0, par * 64)))
                ev_ymm = p.mmg(b2, waits=[ev_nsb])
                ev_yev[hh] = p.op("vector", lambda e, par=par, hh=hh: e.tensor_mul(
                    YG[par * 64:(par + 1) * 64, hh * 2048:(hh + 1) * 2048],
                    PSB[par * 64:(par + 1) * 64, :],
                    SPZ[par * 64:(par + 1) * 64, hh * 2048:(hh + 1) * 2048]),
                    waits=[ev_ymm, ev_pair[2], ev_ygdma[par]])
            ev_ygdma[par] = p.dma("sync", lambda e, z=z, par=par: e.dma_start(
                out=d_yg[z * 64:(z + 1) * 64, :], in_=YG[par * 64:(par + 1) * 64, :]),
                waits=[ev_yev[0], ev_yev[1]])
            for wave in range(2):
                smms = []
                for q in range(4):
                    cq = wave * 2048 + q * 512
                    smms.append(lambda e, par=par, tb=tb, cq=cq, q=q: e.matmul(
                        PS[q][:, :], Bn[par * 64:(par + 1) * 64, tb * 128:(tb + 1) * 128],
                        XP[par * 64:(par + 1) * 64, cq:cq + 512],
                        start=True, stop=True, tile_position=(par * 64, 0)))
                ev_smm = p.mmg(smms, waits=[ev_sadd, ev_pair[0]])
                for q in range(4):
                    cq = wave * 2048 + q * 512
                    ev_sadd = p.op("vector", lambda e, cq=cq, q=q: e.scalar_tensor_tensor(
                        out=NSS[:, cq:cq + 512], in0=PS[q][:, :], scalar=1.0,
                        in1=NSS[:, cq:cq + 512], op0=ALU.mult, op1=ALU.add),
                        waits=[ev_smm])

        p.barrier()


        if PHASES <= 8:
            with nc.Block() as block:
                p.finalize(block)
            return nc
        W6 = E[:, 0:512].bitcast(BF16)
        HE = E[:, 512:1024].bitcast(BF16)
        HW = E[:, 1024:1536].bitcast(BF16)
        H1 = E[:, 1536:2048].bitcast(BF16)
        BS6 = E[:, 2048:3072].bitcast(BF16)
        BR6s = E[:, 3072:4096].bitcast(BF16)
        G6 = TMP[:, 0:48]
        GX = W[:, 4096:4608].bitcast(F32)
        WRT = W[:, 4608:4704]
        AST = W[:, 4704:5216]
        ART = W[:, 5216:6752]
        ev_wr = p.dma("gpsimd", lambda e: e.dma_start(
            out=WRT[:].rearrange("p (k n) -> p k n", k=16),
            in_=wr.rearrange("(k p) n -> p k n", k=16)))
        ev_as = p.dma("gpsimd", lambda e: e.dma_start(
            out=AST[:].rearrange("p (k n) -> p k n", k=16),
            in_=as_.rearrange("(k p) n -> p k n", k=16)))
        ev_ar = p.dma("gpsimd", lambda e: e.dma_start(
            out=ART[:].rearrange("p (k n) -> p k n", k=16),
            in_=arT.rearrange("(k p) n -> p k n", k=16)))
        ev_bs6 = p.dma("gpsimd", lambda e: e.dma_start(out=BS6[0:16, :], in_=bs))
        ev_br6 = p.dma("gpsimd", lambda e: e.dma_start(out=BR6s[0:96, :], in_=br6))
        ev_l1 = ldrow("brt", 6)
        evb1 = p.mmg([lambda e: e.matmul(PS[3][:, 0:6], ONES[0:1, :], ROWB[0:1, 0:6],
                                         start=True, stop=True)], waits=[ev_l1])
        rowst["busy"] = evb1
        ev_brtb = p.op("vector", lambda e: e.tensor_copy(TMP[:, 48:54], PS[3][:, 0:6]),
                       waits=[evb1])
        ev_l2 = ldrow("tie", 36)
        evb2 = p.mmg([lambda e: e.matmul(PS[3][:, 64:100], ONES[0:1, :], ROWB[0:1, 0:36],
                                         start=True, stop=True)], waits=[ev_brtb, ev_l2])
        rowst["busy"] = evb2
        ev_tieb = p.op("vector", lambda e: e.tensor_copy(TMP[:, 64:100], PS[3][:, 64:100]),
                       waits=[evb2])
        if G_PARTS <= 1:
            with nc.Block() as block:
                p.finalize(block)
            return nc
        ev_w6w = None
        for tb in range(NTB):
            builds = [lambda e, tb=tb, k=k: e.matmul(
                PS[0][:, 0:6], A[:, k * 1024 + tb * 128:k * 1024 + (tb + 1) * 128],
                WRT[:, k * 6:(k + 1) * 6], start=(k == 0), stop=(k == 15))
                for k in range(16)]
            evm = p.mmg(builds, waits=[ev_wr, ev_w6w])
            el = p.op("vector", lambda e: e.tensor_add(G6[:, 0:6], PS[0][:, 0:6],
                                                       TMP[:, 48:54]), waits=[evm, ev_brtb])
            em = p.op("vector", lambda e: e.reduce_max(STT[:, 0:1], G6[:, 0:6], axis=AX.X),
                      waits=[el])
            em2 = p.op("vector", lambda e: e.tensor_scalar_mul(STT[:, 1:2], STT[:, 0:1], -1.0),
                       waits=[em])
            ee = p.op("scalar", lambda e: e.activation(
                out=G6[:, 8:14], in_=G6[:, 0:6], func=AF.Exp, bias=STT[:, 1:2]),
                waits=[em2])
            es = p.op("vector", lambda e: e.reduce_sum(STT[:, 2:3], G6[:, 8:14], axis=AX.X),
                      waits=[ee])
            er = p.op("vector", lambda e: e.reciprocal(STT[:, 3:4], STT[:, 2:3]), waits=[es])
            eg = p.op("vector", lambda e: e.tensor_scalar(
                out=G6[:, 16:22], in0=G6[:, 8:14], scalar1=STT[:, 3:4], scalar2=None,
                op0=ALU.mult), waits=[er])
            e1 = p.op("vector", lambda e: e.tensor_tensor(
                out=GX[:, 0:36].rearrange("p (a b) -> p a b", a=6),
                in0=G6[:, None, 16:22].broadcast_to([128, 6, 6]),
                in1=G6[:, 16:22, None].broadcast_to([128, 6, 6]),
                op=ALU.is_gt), waits=[eg])
            e2 = p.op("vector", lambda e: e.tensor_tensor(
                out=GX[:, 36:72].rearrange("p (a b) -> p a b", a=6),
                in0=G6[:, None, 16:22].broadcast_to([128, 6, 6]),
                in1=G6[:, 16:22, None].broadcast_to([128, 6, 6]),
                op=ALU.is_equal), waits=[e1])
            e3 = p.op("vector", lambda e: e.tensor_mul(
                GX[:, 36:72], GX[:, 36:72], TMP[:, 64:100]), waits=[e2, ev_tieb])
            e4 = p.op("vector", lambda e: e.tensor_add(
                GX[:, 0:36], GX[:, 0:36], GX[:, 36:72]), waits=[e3])
            e5 = p.op("vector", lambda e: e.tensor_reduce(
                out=GX[:, 72:78, None], in_=GX[:, 0:36].rearrange("p (a b) -> p a b", a=6),
                axis=AX.X, op=ALU.add), waits=[e4])
            e6 = p.op("vector", lambda e: e.tensor_single_scalar(
                GX[:, 78:84], GX[:, 72:78], 2.0, ALU.is_lt), waits=[e5])
            e7 = p.op("vector", lambda e: e.tensor_mul(
                GX[:, 84:90], G6[:, 16:22], GX[:, 78:84]), waits=[e6])
            e8 = p.op("vector", lambda e: e.tensor_copy(W6[0:128, 0:6], GX[:, 84:90]),
                      waits=[e7, ev_w6w])
            ev_w6w = p.dma("sync", lambda e, tb=tb: e.dma_start(
                out=d_w6[tb * 128:(tb + 1) * 128, 0:6], in_=W6[0:128, 0:6]), waits=[e8])
        p.barrier()
        if G_PARTS <= 2:
            with nc.Block() as block:
                p.finalize(block)
            return nc
        ev_w6t = p.dma("scalar", lambda e: e.dma_start(out=W6[:], in_=d_w6, transpose=True))
        if G_PARTS <= 3:
            with nc.Block() as block:
                p.finalize(block)
            return nc
        for h2 in range(2):
            builds = [lambda e, k=k, h2=h2: e.matmul(
                PSB[0:96, h2 * 512:(h2 + 1) * 512], ART[:, k * 96:(k + 1) * 96],
                A[:, k * 1024 + h2 * 512:k * 1024 + (h2 + 1) * 512],
                start=(k == 0), stop=(k == 15)) for k in range(16)]
            evh = p.mmg(builds, waits=[ev_ar])
            p.op("vector", lambda e, h2=h2: e.tensor_copy(
                HE[0:96, h2 * 512:(h2 + 1) * 512], PSB[0:96, h2 * 512:(h2 + 1) * 512]),
                waits=[evh])
        if G_PARTS <= 4:
            with nc.Block() as block:
                p.finalize(block)
            return nc
        prev_h1 = None
        for h2 in range(2):
            builds = [lambda e, k=k, h2=h2: e.matmul(
                PS[1][0:32, 0:512], AST[:, k * 32:(k + 1) * 32],
                A[:, k * 1024 + h2 * 512:k * 1024 + (h2 + 1) * 512],
                start=(k == 0), stop=(k == 15)) for k in range(16)]
            evh = p.mmg(builds, waits=[ev_as, prev_h1])
            prev_h1 = p.op("vector", lambda e, h2=h2: e.tensor_copy(
                H1[0:16, h2 * 512:(h2 + 1) * 512], PS[1][0:16, 0:512]), waits=[evh])
        if G_PARTS <= 5:
            with nc.Block() as block:
                p.finalize(block)
            return nc
        prev_hw = None
        for h2 in range(2):
            evw = p.mmg([lambda e, h2=h2: e.matmul(
                PS[2][0:96, 0:512], EXP6[0:6, :], W6[0:6, h2 * 512:(h2 + 1) * 512],
                start=True, stop=True)], waits=[ev_w6t, prev_hw])
            prev_hw = p.op("vector", lambda e, h2=h2: e.tensor_mul(
                HW[0:96, h2 * 512:(h2 + 1) * 512], HE[0:96, h2 * 512:(h2 + 1) * 512],
                PS[2][0:96, 0:512]), waits=[evw, p.ev("dve")])
        p.barrier()


        if PHASES <= 9:
            with nc.Block() as block:
                p.finalize(block)
            return nc
        for ec in range(32):
            p.dma("scalar", lambda e, ec=ec: e.dma_start(
                out=B[:, ec * 1024:(ec + 1) * 1024], in_=d_yg[:, ec * 128:(ec + 1) * 128],
                transpose=True))
        ev_bbT = []
        prevb = None
        for f in range(16):
            evl = ldrow("bb", 128, off2=f * 128)
            evm = p.mmg([lambda e, f=f: e.matmul(
                PS[3][:, 160 + f:161 + f], ROWB[0:1, 0:128],
                ONES[0:1, 0:1], start=True, stop=True)], waits=[prevb, evl])
            rowst["busy"] = evm
            prevb = p.op("vector", lambda e, f=f: e.tensor_copy(
                TMP[:, 100 + f:101 + f], PS[3][:, 160 + f:161 + f]), waits=[evm])
            ev_bbT.append(prevb)
        ev_wt = [None, None]
        ev_wfree = [None, None]

        def load_wb(j, buf):
            evs = []
            for k in range(16):
                evs.append(p.dma("gpsimd", lambda e, k=k, j=j, buf=buf: e.dma_start(
                    out=WT[buf][:, k * WCOL:(k + 1) * WCOL],
                    in_=wb[k * 128:(k + 1) * 128, j * WCOL:(j + 1) * WCOL]),
                    waits=[ev_wfree[buf]]))
            return evs[-1]

        ev_wt[0] = load_wb(0, 0)
        ev_gev = [None, None, None]
        for j in range(8):
            buf = j % 2
            if j < 7:
                ev_wt[(j + 1) % 2] = load_wb(j + 1, (j + 1) % 2)
            last = None
            for fc2 in range(2):
                f = j * 2 + fc2
                for tt in range(2):
                    ps = PS[(f * 2 + tt) % 3]
                    builds = [lambda e, k=k, fc2=fc2, tt=tt, ps=ps, buf=buf: e.matmul(
                        ps[:, :], WT[buf][:, k * WCOL + fc2 * 128:k * WCOL + (fc2 + 1) * 128],
                        A[:, k * 1024 + tt * 512:k * 1024 + (tt + 1) * 512],
                        start=(k == 0), stop=False) for k in range(16)]
                    builds.append(lambda e, f=f, tt=tt, ps=ps: e.matmul(
                        ps[:, :], BS6[0:16, f * 128:(f + 1) * 128],
                        H1[0:16, tt * 512:(tt + 1) * 512], start=False, stop=False))
                    builds.append(lambda e, f=f, tt=tt, ps=ps: e.matmul(
                        ps[:, :], BR6s[0:96, f * 128:(f + 1) * 128],
                        HW[0:96, tt * 512:(tt + 1) * 512], start=False, stop=True))
                    evm = p.mmg(builds, waits=[ev_wt[buf], ev_bs6, ev_br6,
                                               ev_gev[(f * 2 + tt) % 3]])
                    last = evm
                    ev_gev[(f * 2 + tt) % 3] = p.op(
                        "scalar", lambda e, f=f, tt=tt, ps=ps: e.activation(
                            out=C[:, f * 1024 + tt * 512:f * 1024 + (tt + 1) * 512],
                            in_=ps[:, :], func=AF.Silu, bias=TMP[:, 100 + f:101 + f]),
                        waits=[evm, ev_bbT[f]])
            ev_wfree[buf] = last
        p.barrier()


        if PHASES <= 10:
            with nc.Block() as block:
                p.finalize(block)
            return nc
        UTS = [E[:, 0:1024], E[:, 1024:2048]]
        OUS = [E[:, 2048:2560], E[:, 2560:3072]]
        ev_wt = [None, None]
        ev_wfree = [None, None]

        def load_wo(dchunk, buf):
            evs = []
            for k in range(32):
                evs.append(p.dma("gpsimd", lambda e, k=k, dchunk=dchunk, buf=buf: e.dma_start(
                    out=WT[buf][:, k * 128:(k + 1) * 128],
                    in_=woT[k * 128:(k + 1) * 128, dchunk * 128:(dchunk + 1) * 128]),
                    waits=[ev_wfree[buf]]))
            return evs[-1]

        ev_wt[0] = load_wo(0, 0)
        ev_ut = [None, None]
        ev_ut_rd = [None, None]
        ev_ut[0] = p.dma("sync", lambda e: e.dma_start(out=UTS[0][:], in_=uT[0:128, :]))
        ev_ep = [None, None, None]
        ev_ost = [None, None]
        for dc in range(16):
            buf = dc % 2
            if dc < 15:
                ev_wt[(dc + 1) % 2] = load_wo(dc + 1, (dc + 1) % 2)
                ev_ut[(dc + 1) % 2] = p.dma("sync", lambda e, dc=dc: e.dma_start(
                    out=UTS[(dc + 1) % 2][:], in_=uT[(dc + 1) * 128:(dc + 2) * 128, :]),
                    waits=[ev_ut_rd[(dc + 1) % 2]])
            for tt in range(2):
                ps = PS[(dc * 2 + tt) % 3]
                builds = [lambda e, k=k, tt=tt, ps=ps, buf=buf: e.matmul(
                    ps[:, :], WT[buf][:, k * 128:(k + 1) * 128],
                    B[:, k * 1024 + tt * 512:k * 1024 + (tt + 1) * 512],
                    start=(k == 0), stop=(k == 31)) for k in range(32)]
                evm = p.mmg(builds, waits=[ev_wt[buf], ev_ep[(dc * 2 + tt) % 3]])
                ob = OUS[tt]
                e1 = p.op("vector", lambda e, dc=dc, tt=tt, ps=ps, ob=ob: e.tensor_add(
                    ob, ps[:, :], A[:, dc * 1024 + tt * 512:dc * 1024 + (tt + 1) * 512]),
                    waits=[evm, ev_ost[tt]])
                e2 = p.op("vector", lambda e, dc=dc, tt=tt, ob=ob: e.tensor_mul(
                    ob, ob, C[:, dc * 1024 + tt * 512:dc * 1024 + (tt + 1) * 512]),
                    waits=[e1])
                e3 = p.op("vector", lambda e, dc=dc, tt=tt, ob=ob, buf=buf: e.tensor_add(
                    ob, ob, UTS[buf][:, tt * 512:(tt + 1) * 512]),
                    waits=[e2, ev_ut[buf]])
                ev_ut_rd[buf] = e3
                ev_ep[(dc * 2 + tt) % 3] = e3
                ev_ost[tt] = p.dma("sync", lambda e, dc=dc, tt=tt, ob=ob: e.dma_start(
                    out=outT[dc * 128:(dc + 1) * 128, tt * 512:(tt + 1) * 512], in_=ob),
                    waits=[e3])
        p.barrier()

        with nc.Block() as block:
            p.finalize(block)

    return nc


def _pack_rows(I):
    rows = np.zeros((1, 8192), np.float32)

    def put(off, v):
        v = np.asarray(v, np.float32).reshape(-1)
        rows[0, off:off + v.size] = v

    put(0, I["L0_attn_ln_w"]); put(2048, I["L0_attn_ln_b"]); put(4096, I["L0_ffn_b_base"])
    put(6144, I["L0_attn_bnorm_w"]); put(6272, I["L0_attn_cnorm_w"])
    put(6400, I["L0_attn_D"]); put(6464, I["L0_attn_dt_bias"]); put(6528, I["L0_attn_A_log"])
    put(6592, I["L0_ffn_b_router"])
    tie = (np.arange(6)[None, :] < np.arange(6)[:, None]).astype(np.float32)
    put(6598, tie.reshape(-1))
    return rows


def _consts():
    ones = np.ones((1, 128), np.float32)
    idb = np.eye(128, dtype=np.float32)
    jj, ii = np.meshgrid(np.arange(128), np.arange(128), indexing="ij")
    ut2 = ((jj <= ii) & ((jj // 64) == (ii // 64))).astype(np.float32)
    ut1 = (jj <= ii).astype(np.float32)
    pick = np.zeros((128, 128), np.float32)
    pick[63, 0:64] = 1.0
    pick[127, 64:128] = 1.0
    picke = np.zeros((128, 128), np.float32)
    picke[63, :] = 1.0
    picko = np.zeros((128, 128), np.float32)
    picko[127, :] = 1.0
    m = (np.arange(64)[None, :] >= np.arange(64)[:, None]).astype(np.float32)
    tril = np.concatenate([m, m], axis=0)
    exp6 = np.zeros((6, 96), np.float32)
    for e6 in range(6):
        exp6[e6, e6 * 16:(e6 + 1) * 16] = 1.0
    return ones, idb, ut2, ut1, pick, picke, picko, tril, exp6


def kernel(**inputs):
    I = inputs
    u = np.ascontiguousarray(np.asarray(I["L0_attn_u"], np.float32)[0])
    wiT = np.ascontiguousarray(np.asarray(I["L0_attn_in_proj_w"], np.float32).T)
    woT = np.ascontiguousarray(np.asarray(I["L0_attn_out_proj_w"], np.float32).T)
    wb = np.ascontiguousarray(np.asarray(I["L0_ffn_W_base"], np.float32))
    as_ = np.ascontiguousarray(np.concatenate([
        np.asarray(I["L0_ffn_A_s"], np.float32),
        np.zeros((D, 16), np.float32)], axis=1))
    bs = np.ascontiguousarray(np.asarray(I["L0_ffn_B_s"], np.float32))
    arT = np.ascontiguousarray(
        np.asarray(I["L0_ffn_A_r"], np.float32).transpose(1, 0, 2).reshape(D, 96))
    br6 = np.ascontiguousarray(np.asarray(I["L0_ffn_B_r"], np.float32).reshape(96, D))
    wr = np.ascontiguousarray(np.asarray(I["L0_ffn_W_router"], np.float32))
    rows = _pack_rows(I)
    ones, idb, ut2, ut1, pick, picke, picko, tril, exp6 = _consts()

    nc = build_nc()
    in_maps = []
    for c in range(NCORE):
        sl = slice(c * T, (c + 1) * T)
        u2 = np.zeros((2 * T, D), np.float32)
        if c > 0:
            u2[0:T] = u[(c - 1) * T:c * T]
        u2[T:2 * T] = u[sl]
        r = rows.copy()
        r[0, 6634] = 0.0 if c == 0 else 1.0
        in_maps.append(dict(
            u2=u2, uT=np.ascontiguousarray(u[sl].T), wiT=wiT, woT=woT, wb=wb,
            as_=as_, bs=bs, arT=arT, br6=br6, wr=wr, vrows=r, c_ones=ones,
            c_idb=idb, c_ut2=ut2, c_ut1=ut1, c_pick=pick, c_picke=picke,
            c_picko=picko, c_tril=tril, c_exp6=exp6,
        ))
    res = run_bass_kernel_spmd(nc, in_maps, core_ids=list(range(NCORE)))
    global _LAST_RES
    _LAST_RES = res
    out = np.empty((1, 8192, D), np.float32)
    for c in range(NCORE):
        out[0, c * T:(c + 1) * T, :] = np.asarray(res.results[c]["outT"]).T
    return out


# version 11 (final)
# speedup vs baseline: 1.0561x; 1.0119x over previous
import numpy as np
import ml_dtypes
from contextlib import ExitStack

import concourse.bass as bass
import concourse.mybir as mybir
from concourse.bass_utils import run_bass_kernel_spmd

F32 = mybir.dt.float32
BF16 = mybir.dt.bfloat16
AF = mybir.ActivationFunctionType
ALU = mybir.AluOpType
AX = mybir.AxisListType
BF = ml_dtypes.bfloat16

NCORE = 8
T = 1024
NTB = 8
D = 2048
DI = 4096
NH, HD, DSTATE, CH = 64, 64, 128, 64
DPROJ = 8512
NCH = T // CH
WCOL = 256
EPS = 1e-5


class Prog:
    COMPUTE = {"tensor": "pe", "vector": "dve", "scalar": "act", "gpsimd": "pool"}
    DMAQ = {"gpsimd": "qg", "sync": "qs", "scalar": "qa"}
    ENGINES = ("tensor", "vector", "scalar", "gpsimd", "sync")

    def __init__(self, nc):
        self.nc = nc
        self.q = {e: [] for e in self.ENGINES}
        self.semnames = ["pe", "dve", "act", "pool", "qg", "qs", "qa"]
        self.sem = {}
        self.cnt = {s: 0 for s in self.semnames}
        self.waited = {e: {s: 0 for s in self.semnames} for e in self.ENGINES}

    @staticmethod
    def _flat(waits):
        out = []
        for ev in waits:
            if ev is None:
                continue
            if isinstance(ev, list):
                out.extend(Prog._flat(ev))
            else:
                out.append(ev)
        return out

    def _push(self, eng, waits, build, incsem, incn):
        w = []
        for (s, c) in self._flat(waits):
            if c > self.waited[eng][s]:
                w.append((s, c))
                self.waited[eng][s] = c
        target = None
        if incsem is not None:
            self.cnt[incsem] += incn
            target = self.cnt[incsem]
        sem = self.sem

        def run(e, w=tuple(w), build=build, incsem=incsem, target=target, incn=incn):
            for (s, c) in w:
                e.wait_ge(sem[s], c)
            inst = build(e)
            if incsem is not None and inst is not None:
                inst.then_inc(sem[incsem], incn)

        self.q[eng].append(run)
        return (incsem, target) if incsem else None

    def op(self, eng, build, waits=()):
        return self._push(eng, waits, build, self.COMPUTE[eng], 1)

    def mmg(self, builds, waits=()):
        for b in builds[:-1]:
            self._push("tensor", waits, b, None, 0)
            waits = ()
        return self._push("tensor", waits, builds[-1], "pe", 1)

    def dma(self, eng, build, waits=()):
        return self._push(eng, waits, build, self.DMAQ[eng], 16)

    def ev(self, semname):
        return (semname, self.cnt[semname])

    def barrier(self):
        evs = [(s, self.cnt[s]) for s in self.semnames if self.cnt[s] > 0]
        for e in self.ENGINES:
            self._push(e, evs, lambda _e: None, None, 0)

    def finalize(self, block):
        for ename in self.ENGINES:
            q = self.q[ename]
            deco = getattr(block, ename)

            def body(e, q=q):
                for f in q:
                    f(e)

            deco(body)


PHASES = 99
HALO_F = True
G_PARTS = 9
SERIAL_A = False
DEBUG = False
_LAST_RES = None


def build_nc():
    nc = bass.Bass("TRN2", target_bir_lowering=False)
    dt = nc.dram_tensor

    u2 = dt("u2", [2 * T, D], F32, kind="ExternalInput").ap()
    uT = dt("uT", [D, T], F32, kind="ExternalInput").ap()
    wiT = dt("wiT", [D, DPROJ], F32, kind="ExternalInput").ap()
    woT = dt("woT", [DI, D], F32, kind="ExternalInput").ap()
    wb = dt("wb", [D, D], F32, kind="ExternalInput").ap()
    as_ = dt("as_", [D, 32], F32, kind="ExternalInput").ap()
    bs = dt("bs", [16, D], F32, kind="ExternalInput").ap()
    arT = dt("arT", [D, 96], F32, kind="ExternalInput").ap()
    br6 = dt("br6", [96, D], F32, kind="ExternalInput").ap()
    wr = dt("wr", [D, 6], F32, kind="ExternalInput").ap()
    vrows = dt("vrows", [1, 8192], F32, kind="ExternalInput").ap()
    c_ones = dt("c_ones", [1, 128], F32, kind="ExternalInput").ap()
    c_idb = dt("c_idb", [128, 128], F32, kind="ExternalInput").ap()
    c_ut2 = dt("c_ut2", [128, 128], F32, kind="ExternalInput").ap()
    c_ut1 = dt("c_ut1", [128, 128], F32, kind="ExternalInput").ap()
    c_pick = dt("c_pick", [128, 128], F32, kind="ExternalInput").ap()
    c_picke = dt("c_picke", [128, 128], F32, kind="ExternalInput").ap()
    c_picko = dt("c_picko", [128, 128], F32, kind="ExternalInput").ap()
    c_tril = dt("c_tril", [128, 64], F32, kind="ExternalInput").ap()
    c_exp6 = dt("c_exp6", [6, 96], F32, kind="ExternalInput").ap()

    outT = dt("outT", [D, T], F32, kind="ExternalOutput").ap()

    KI = "ExternalOutput" if DEBUG else "Internal"
    d_un = dt("d_un", [T, D], BF16, kind=KI).ap()
    d_unh = dt("d_unh", [T, D], BF16, kind=KI).ap()
    d_xn = dt("d_xn", [T, D], BF16, kind=KI).ap()
    d_bc = dt("d_bc", [T, 2 * DSTATE], BF16, kind=KI).ap()
    d_svz = dt("d_svz", [T, DI], BF16, kind=KI).ap()
    d_yg = dt("d_yg", [T, DI], BF16, kind=KI).ap()
    d_w6 = dt("d_w6", [T, 128], BF16, kind=KI).ap()
    d_nss = dt("d_nss", [128, 4096], F32, kind=KI).ap() if DEBUG else None
    d_sc = dt("d_sc", [128, 4096], F32, kind=KI).ap() if DEBUG else None

    RO = {"lnw": 0, "lnb": 2048, "bb": 4096, "bnw": 6144, "cnw": 6272,
          "d": 6400, "dtb": 6464, "alog": 6528, "brt": 6592, "tie": 6598,
          "flag": 6634}

    with ExitStack() as ctx:
        E_ = ctx.enter_context
        A = E_(nc.sbuf_tensor("A", [128, 16384], BF16))
        B = E_(nc.sbuf_tensor("B", [128, 32768], BF16))
        C = E_(nc.sbuf_tensor("C", [128, 16384], BF16))
        NSS = E_(nc.sbuf_tensor("NSS", [128, 4096], F32))
        E = E_(nc.sbuf_tensor("E", [128, 4096], F32))
        W = E_(nc.sbuf_tensor("W", [128, 8192], BF16))
        Bn = E_(nc.sbuf_tensor("Bn", [128, 1024], BF16))
        Cn = E_(nc.sbuf_tensor("Cn", [128, 1024], BF16))
        Bh = E_(nc.sbuf_tensor("Bh", [128, 1024], BF16))
        DSC = E_(nc.sbuf_tensor("DSC", [128, 512], BF16))
        PST = E_(nc.sbuf_tensor("PST", [128, 512], BF16))
        GDS = E_(nc.sbuf_tensor("GDS", [128, 512], BF16))
        TREP = E_(nc.sbuf_tensor("TREP", [128, 1024], BF16))
        BFM = E_(nc.sbuf_tensor("BFM", [128, 1024], BF16))
        CFM = E_(nc.sbuf_tensor("CFM", [128, 1024], BF16))
        CBT = E_(nc.sbuf_tensor("CBT", [128, 512], BF16))
        IDB = E_(nc.sbuf_tensor("IDB", [128, 128], BF16))
        UT2 = E_(nc.sbuf_tensor("UT2", [128, 128], F32))
        UT1 = E_(nc.sbuf_tensor("UT1", [128, 128], F32))
        PICK = E_(nc.sbuf_tensor("PICK", [128, 128], F32))
        PICKE = E_(nc.sbuf_tensor("PICKE", [128, 128], F32))
        PICKO = E_(nc.sbuf_tensor("PICKO", [128, 128], F32))
        TRILM = E_(nc.sbuf_tensor("TRILM", [128, 64], F32))
        ONES = E_(nc.sbuf_tensor("ONES", [1, 128], F32))
        ROWB = E_(nc.sbuf_tensor("ROWB", [1, 512], F32))
        EXP6 = E_(nc.sbuf_tensor("EXP6", [6, 96], BF16))
        DD = E_(nc.sbuf_tensor("DD", [128, 64], BF16))
        DTB = E_(nc.sbuf_tensor("DTB", [128, 64], F32))
        ANEG = E_(nc.sbuf_tensor("ANEG", [128, 64], F32))
        BNWB = E_(nc.sbuf_tensor("BNWB", [128, 128], BF16))
        CNWB = E_(nc.sbuf_tensor("CNWB", [128, 128], BF16))
        STT = E_(nc.sbuf_tensor("STT", [128, 64], F32))
        TMP = E_(nc.sbuf_tensor("TMP", [128, 256], F32))
        dAm = E[:, 0:512]
        dAh = E[:, 512:1024]
        GAS = E[:, 1024:1536]
        PS = [E_(nc.psum_tensor(f"PS{i}", [128, 512], F32)) for i in range(4)]
        PSB = E_(nc.psum_tensor("PSB", [128, 2048], F32))

        p = Prog(nc)
        for s in p.semnames:
            p.sem[s] = E_(nc.semaphore(s))

        rowst = {"busy": None}

        def ldrow(name, w, off2=0):
            o = RO[name] + off2
            ev = p.dma("sync", lambda e, o=o, w=w: e.dma_start(
                out=ROWB[0:1, 0:w], in_=vrows[0:1, o:o + w]), waits=[rowst["busy"]])
            return ev

        p.dma("sync", lambda e: e.dma_start(out=ONES[:], in_=c_ones))
        p.dma("sync", lambda e: e.dma_start(out=UT2[:], in_=c_ut2))
        p.dma("sync", lambda e: e.dma_start(out=UT1[:], in_=c_ut1))
        p.dma("sync", lambda e: e.dma_start(out=PICK[:], in_=c_pick))
        p.dma("sync", lambda e: e.dma_start(out=PICKE[:], in_=c_picke))
        p.dma("sync", lambda e: e.dma_start(out=PICKO[:], in_=c_picko))
        p.dma("sync", lambda e: e.dma_start(out=TRILM[:], in_=c_tril))
        p.dma("gpsimd", lambda e: e.dma_start(out=IDB[:], in_=c_idb))
        p.dma("gpsimd", lambda e: e.dma_start(out=EXP6[:], in_=c_exp6))
        ev_qs0 = p.ev("qs")

        bc_prev = [None]

        def bcast_name(name, width, dst, off2=0, prep=None):
            evl = ldrow(name, width, off2)
            evp_ = evl
            if prep is not None:
                evp_ = prep(evl)
            evm = p.mmg([lambda e, width=width: e.matmul(
                PS[3][:, 0:width], ONES[0:1, :], ROWB[0:1, 0:width],
                start=True, stop=True)], waits=[ev_qs0, evp_, bc_prev[0]])
            rowst["busy"] = evm
            evc = p.op("vector", lambda e, dst=dst, width=width: e.tensor_copy(
                dst, PS[3][:, 0:width]), waits=[evm])
            bc_prev[0] = evc
            return evc

        def _alog_prep(evl):
            ea = p.op("scalar", lambda e: e.activation(
                out=ROWB[0:1, 0:NH], in_=ROWB[0:1, 0:NH], func=AF.Exp), waits=[evl])
            return p.op("vector", lambda e: e.tensor_scalar_mul(
                ROWB[0:1, 0:NH], ROWB[0:1, 0:NH], -1.0), waits=[ea])

        ev_dd = bcast_name("d", NH, DD[:, :])
        ev_dtb = bcast_name("dtb", NH, DTB[:, :])
        ev_aneg = bcast_name("alog", NH, ANEG[:, :], prep=_alog_prep)
        ev_bnw = bcast_name("bnw", DSTATE, BNWB[:, :])
        ev_cnw = bcast_name("cnw", DSTATE, CNWB[:, :])
        LNW = B[:, 0:2048]
        LNB = B[:, 2048:4096]
        for j in range(4):
            bcast_name("lnw", 512, LNW[:, j * 512:(j + 1) * 512], off2=j * 512)
        for j in range(4):
            ev_lnwb = bcast_name("lnb", 512, LNB[:, j * 512:(j + 1) * 512], off2=j * 512)


        if PHASES <= 1:
            with nc.Block() as block:
                p.finalize(block)
            return nc
        p.barrier()

        UBv = B[:, 8192:24576].bitcast(F32)
        Ust = [UBv[:, j * 2048:(j + 1) * 2048] for j in range(4)]
        XNS = [W[:, 0:2048], W[:, 2048:4096]]
        UNS = [W[:, 4096:6144], W[:, 6144:8192]]
        SQS = B[:, 4096:8192]
        ev_ub_free = [None] * 4
        ev_cc_free = [None] * 4
        ev_xb_free = [None, None]
        SB = {}

        def stage1(i):
            p4 = i % 4
            ub = Ust[p4]
            sq = SQS[:, (i % 2) * 2048:(i % 2 + 1) * 2048]
            off = p4 * 12
            cc = {k: STT[:, off + j:off + j + 1] for j, k in enumerate(
                ["sq", "sum", "m", "msq", "v", "a", "r1", "w", "b", "negm"])}
            ld = p.dma("sync", lambda e, i=i, ub=ub: e.dma_start(
                out=ub, in_=u2[i * 128:(i + 1) * 128, :]), waits=[ev_ub_free[p4]])
            e1 = p.op("scalar", lambda e, ub=ub, sq=sq, c=cc["sq"]: e.activation(
                out=sq, in_=ub, func=AF.Square, accum_out=c),
                waits=[ld, ev_cc_free[p4]])
            e2 = p.op("vector", lambda e, ub=ub, c=cc["sum"]: e.reduce_sum(
                c, ub, axis=AX.X), waits=[ld, ev_cc_free[p4]])
            e3 = p.op("vector", lambda e, cc=cc: e.tensor_scalar_mul(
                cc["m"], cc["sum"], 1.0 / D), waits=[e2])
            e4 = p.op("vector", lambda e, cc=cc: e.tensor_scalar_mul(
                cc["msq"], cc["sq"], 1.0 / D), waits=[e1])
            e5 = p.op("vector", lambda e, cc=cc: e.tensor_scalar(
                out=cc["v"], in0=cc["m"], scalar1=cc["m"], scalar2=None, op0=ALU.mult),
                waits=[e3])
            e6 = p.op("vector", lambda e, cc=cc: e.tensor_sub(cc["v"], cc["msq"], cc["v"]),
                      waits=[e4, e5])
            e7 = p.op("vector", lambda e, cc=cc: e.tensor_scalar_add(cc["a"], cc["v"], EPS),
                      waits=[e6])
            e8 = p.op("scalar", lambda e, cc=cc: e.activation(
                out=cc["r1"], in_=cc["a"], func=AF.Ln), waits=[e7])
            e8b = p.op("vector", lambda e, cc=cc: e.tensor_scalar_mul(
                cc["r1"], cc["r1"], -0.5), waits=[e8])
            e9 = p.op("scalar", lambda e, cc=cc: e.activation(
                out=cc["r1"], in_=cc["r1"], func=AF.Exp), waits=[e8b])
            e10 = p.op("vector", lambda e, cc=cc: e.reciprocal(cc["w"], cc["a"]), waits=[e7])
            e11 = p.op("vector", lambda e, cc=cc: e.tensor_scalar(
                out=cc["w"], in0=cc["w"], scalar1=cc["v"], scalar2=EPS, op0=ALU.mult,
                op1=ALU.add), waits=[e10])
            e12 = p.op("scalar", lambda e, cc=cc: e.activation(
                out=cc["b"], in_=cc["w"], func=AF.Ln), waits=[e11])
            e12b = p.op("vector", lambda e, cc=cc: e.tensor_scalar_mul(
                cc["b"], cc["b"], -0.5), waits=[e12])
            e13 = p.op("scalar", lambda e, cc=cc: e.activation(
                out=cc["b"], in_=cc["b"], func=AF.Exp), waits=[e12b])
            e15 = p.op("vector", lambda e, cc=cc: e.tensor_scalar_mul(
                cc["negm"], cc["m"], -1.0), waits=[e3])
            SB[i] = dict(cc=cc, ub=ub, e1=e1, e2=e2, e9=e9, e13=e13, e15=e15)

        def stage2(i):
            halo = i < 8
            tb = i % 8
            p4 = i % 4
            st = SB.pop(i)
            cc, ub = st["cc"], st["ub"]
            xb, nb = XNS[i % 2], UNS[i % 2]
            exn = p.op("vector", lambda e, ub=ub, xb=xb, cc=cc: e.tensor_scalar(
                out=xb, in0=ub, scalar1=cc["negm"], scalar2=cc["r1"], op0=ALU.add,
                op1=ALU.mult), waits=[st["e15"], st["e9"], ev_xb_free[i % 2]])
            eun1 = p.op("vector", lambda e, xb=xb, nb=nb, cc=cc: e.scalar_tensor_tensor(
                out=nb, in0=xb, scalar=cc["b"], in1=LNW, op0=ALU.mult, op1=ALU.mult),
                waits=[exn, st["e13"], ev_lnwb])
            eun2 = p.op("gpsimd", lambda e, nb=nb: e.tensor_add(nb, nb, LNB), waits=[eun1])
            frees = [exn, eun1, eun2]
            if not halo:
                exs = p.dma("sync", lambda e, tb=tb, xb=xb: e.dma_start(
                    out=d_xn[tb * 128:(tb + 1) * 128, :], in_=xb), waits=[exn])
                frees.append(exs)
            dun = d_unh if halo else d_un
            eus = p.dma("sync", lambda e, tb=tb, nb=nb, dun=dun: e.dma_start(
                out=dun[tb * 128:(tb + 1) * 128, :], in_=nb), waits=[eun2])
            frees.append(eus)
            ev_xb_free[i % 2] = frees
            ev_ub_free[p4] = [st["e1"], st["e2"], exn]
            ev_cc_free[p4] = [exn, eun1]

        for i in range(16):
            stage1(i)
            if i >= 1:
                stage2(i - 1)
        stage2(15)

        if PHASES <= 2:
            with nc.Block() as block:
                p.finalize(block)
            return nc
        p.barrier()
        for k in range(16):
            p.dma("scalar", lambda e, k=k: e.dma_start(
                out=A[:, k * 1024:(k + 1) * 1024], in_=d_un[:, k * 128:(k + 1) * 128],
                transpose=True))
            p.dma("scalar", lambda e, k=k: e.dma_start(
                out=C[:, k * 1024:(k + 1) * 1024], in_=d_unh[:, k * 128:(k + 1) * 128],
                transpose=True))
        p.barrier()

        def lhs_un(i, k):
            reg = C if i < 8 else A
            tb = i % 8
            return reg[:, k * 1024 + tb * 128: k * 1024 + (tb + 1) * 128]


        if PHASES <= 3:
            with nc.Block() as block:
                p.finalize(block)
            return nc
        NB = 320
        ev_wb = None
        for k in range(16):
            ev_wb = p.dma("gpsimd", lambda e, k=k: e.dma_start(
                out=W[:, k * NB:(k + 1) * NB],
                in_=wiT[k * 128:(k + 1) * 128, 8192:8512]))
        ev_evac3 = [None, None, None]
        for i in range(16):
            halo = i < 8
            tb = i % 8
            ps = PS[i % 3]
            builds = [lambda e, i=i, k=k, ps=ps: e.matmul(
                ps[:, 0:NB], lhs_un(i, k), W[:, k * NB:(k + 1) * NB],
                start=(k == 0), stop=(k == 15)) for k in range(16)]
            evm = p.mmg(builds, waits=[ev_wb, ev_evac3[i % 3]])
            readers = []
            bn_dst = Bh if halo else Bn
            c_bs, c_br = STT[:, 53:54], STT[:, 54:55]
            eb1 = p.op("scalar", lambda e, ps=ps, c=c_bs: e.activation(
                out=TMP[:, 0:128], in_=ps[:, 0:128], func=AF.Square, accum_out=c),
                waits=[evm, ev_evac3[i % 3]])
            eb1b = p.op("vector", lambda e: e.tensor_scalar(
                out=c_br, in0=c_bs, scalar1=1.0 / DSTATE, scalar2=EPS, op0=ALU.mult,
                op1=ALU.add), waits=[eb1])
            eb2 = p.op("scalar", lambda e: e.activation(
                out=c_br, in_=c_br, func=AF.Ln), waits=[eb1b])
            eb2b = p.op("vector", lambda e: e.tensor_scalar_mul(c_br, c_br, -0.5),
                        waits=[eb2])
            eb3 = p.op("scalar", lambda e: e.activation(
                out=c_br, in_=c_br, func=AF.Exp), waits=[eb2b])
            eb4 = p.op("vector", lambda e, ps=ps, bn_dst=bn_dst, tb=tb: e.scalar_tensor_tensor(
                out=bn_dst[:, tb * 128:(tb + 1) * 128], in0=ps[:, 0:128], scalar=c_br,
                in1=BNWB[:, :], op0=ALU.mult, op1=ALU.mult), waits=[eb3, ev_bnw])
            readers += [eb1, eb4]
            if not halo:
                c_cs, c_cr = STT[:, 55:56], STT[:, 56:57]
                ec1 = p.op("scalar", lambda e, ps=ps, c=c_cs: e.activation(
                    out=TMP[:, 0:128], in_=ps[:, 128:256], func=AF.Square, accum_out=c),
                    waits=[eb1])
                ec1b = p.op("vector", lambda e: e.tensor_scalar(
                    out=c_cr, in0=c_cs, scalar1=1.0 / DSTATE, scalar2=EPS, op0=ALU.mult,
                    op1=ALU.add), waits=[ec1])
                ec2 = p.op("scalar", lambda e: e.activation(
                    out=c_cr, in_=c_cr, func=AF.Ln), waits=[ec1b])
                ec2b = p.op("vector", lambda e: e.tensor_scalar_mul(c_cr, c_cr, -0.5),
                            waits=[ec2])
                ec3 = p.op("scalar", lambda e: e.activation(
                    out=c_cr, in_=c_cr, func=AF.Exp), waits=[ec2b])
                ec4 = p.op("vector", lambda e, ps=ps, tb=tb: e.scalar_tensor_tensor(
                    out=Cn[:, tb * 128:(tb + 1) * 128], in0=ps[:, 128:256], scalar=c_cr,
                    in1=CNWB[:, :], op0=ALU.mult, op1=ALU.mult), waits=[ec3, ev_cnw])
                readers += [ec1, ec4]
            da_dst = dAh if halo else dAm
            ed1 = p.op("vector", lambda e, ps=ps: e.tensor_add(
                TMP[:, 128:192], ps[:, 256:320], DTB[:, :]), waits=[evm, ev_dtb, eb4])
            ed2 = p.op("scalar", lambda e: e.activation(
                out=TMP[:, 192:256], in_=TMP[:, 128:192], func=AF.Abs), waits=[ed1])
            ed3 = p.op("scalar", lambda e: e.activation(
                out=TMP[:, 192:256], in_=TMP[:, 192:256], func=AF.Exp, scale=-1.0),
                waits=[ed2])
            ed4 = p.op("scalar", lambda e: e.activation(
                out=TMP[:, 192:256], in_=TMP[:, 192:256], func=AF.Ln, bias=1.0), waits=[ed3])
            ed5 = p.op("vector", lambda e: e.tensor_scalar_max(
                TMP[:, 128:192], TMP[:, 128:192], 0.0), waits=[ed2])
            ed6 = p.op("vector", lambda e: e.tensor_add(
                TMP[:, 128:192], TMP[:, 128:192], TMP[:, 192:256]), waits=[ed4, ed5])
            ed7 = p.op("vector", lambda e, da_dst=da_dst, tb=tb: e.tensor_mul(
                da_dst[:, tb * 64:(tb + 1) * 64], TMP[:, 128:192], ANEG[:, :]),
                waits=[ed6, ev_aneg])
            readers.append(ed1)
            ev_evac3[i % 3] = readers + [ed7]

        p.barrier()


        if PHASES <= 4:
            with nc.Block() as block:
                p.finalize(block)
            return nc
        prev_ed = prev_ds = prev_te = prev_to = None
        for tb in range(NTB):
            evm = p.mmg([lambda e, tb=tb: e.matmul(
                PS[0][:, 0:64], UT2[:], dAm[:, tb * 64:(tb + 1) * 64],
                start=True, stop=True)], waits=[prev_ed])
            eac = p.op("vector", lambda e: e.tensor_copy(TMP[:, 0:64], PS[0][:, 0:64]),
                       waits=[evm])
            evp = p.mmg([lambda e: e.matmul(
                PS[1][:, 0:64], PICK[:, 0:128], TMP[:, 0:64], start=True, stop=True)],
                waits=[eac])
            ed = p.op("vector", lambda e: e.tensor_sub(
                TMP[:, 64:128], PS[1][:, 0:64], TMP[:, 0:64]), waits=[evp, prev_ds])
            ds1 = p.op("scalar", lambda e, tb=tb: e.activation(
                out=DSC[:, tb * 64:(tb + 1) * 64], in_=TMP[:, 64:128], func=AF.Exp),
                waits=[ed])
            ds2 = p.op("scalar", lambda e, tb=tb: e.activation(
                out=PST[:, tb * 64:(tb + 1) * 64], in_=TMP[:, 64:128], func=AF.Exp,
                scale=-1.0), waits=[ed])
            prev_ds = [ds1, ds2]
            prev_ed = ed
            ev_e = p.mmg([lambda e: e.matmul(
                PS[2][:, 0:64], PICKE[:], TMP[:, 0:64], start=True, stop=True)],
                waits=[eac, prev_te])
            prev_te = p.op("scalar", lambda e, tb=tb: e.activation(
                out=TREP[:, (2 * tb) * 64:(2 * tb + 1) * 64], in_=PS[2][:, 0:64],
                func=AF.Exp), waits=[ev_e])
            ev_o = p.mmg([lambda e: e.matmul(
                PS[2][:, 64:128], PICKO[:], TMP[:, 0:64], start=True, stop=True)],
                waits=[eac, prev_to])
            prev_to = p.op("scalar", lambda e, tb=tb: e.activation(
                out=TREP[:, (2 * tb + 1) * 64:(2 * tb + 2) * 64], in_=PS[2][:, 64:128],
                func=AF.Exp), waits=[ev_o])

        OFFR = TMP[0:1, 128:192]
        prev = p.op("vector", lambda e: e.memset(OFFR[:, 0:64], 0.0),
                    waits=[p.ev("act")])
        for tb in range(NTB):
            evm = p.mmg([lambda e, tb=tb: e.matmul(
                PS[0][:, 0:64], UT1[:], dAh[:, tb * 64:(tb + 1) * 64],
                start=True, stop=True)], waits=[p.ev("dve"), p.ev("act")])
            evb = p.mmg([lambda e: e.matmul(
                PS[1][:, 0:64], ONES[0:1, :], OFFR[:, 0:64], start=True, stop=True)],
                waits=[prev])
            ecb = p.op("vector", lambda e: e.tensor_copy(TMP[:, 64:128], PS[1][:, 0:64]),
                       waits=[evb])
            ega = p.op("vector", lambda e, tb=tb: e.tensor_add(
                GAS[:, tb * 64:(tb + 1) * 64], PS[0][:, 0:64], TMP[:, 64:128]),
                waits=[evm, ecb])
            evr = p.mmg([lambda e, tb=tb: e.matmul(
                PS[2][0:1, 0:64], PICK[:, 64:65], GAS[:, tb * 64:(tb + 1) * 64],
                start=True, stop=True)], waits=[ega])
            prev = p.op("vector", lambda e: e.tensor_copy(OFFR[:, 0:64], PS[2][0:1, 0:64]),
                        waits=[evr])
        evb = p.mmg([lambda e: e.matmul(
            PS[1][:, 0:64], ONES[0:1, :], OFFR[:, 0:64], start=True, stop=True)],
            waits=[prev])
        egb = p.op("vector", lambda e: e.tensor_copy(TMP[:, 0:64], PS[1][:, 0:64]),
                   waits=[evb])
        ev_gds = []
        for tb in range(NTB):
            eg1 = p.op("vector", lambda e, tb=tb: e.tensor_sub(
                GAS[:, tb * 64:(tb + 1) * 64], TMP[:, 0:64], GAS[:, tb * 64:(tb + 1) * 64]),
                waits=[egb])
            ev_gds.append(p.op("scalar", lambda e, tb=tb: e.activation(
                out=GDS[:, tb * 64:(tb + 1) * 64], in_=GAS[:, tb * 64:(tb + 1) * 64],
                func=AF.Exp), waits=[eg1]))
        ev_fl = ldrow("flag", 1)
        evf = p.mmg([lambda e: e.matmul(
            PS[1][:, 0:1], ONES[0:1, :], ROWB[0:1, 0:1], start=True, stop=True)],
            waits=[egb, ev_fl])
        rowst["busy"] = evf
        evf = p.op("vector", lambda e: e.tensor_copy(STT[:, 52:53], PS[1][:, 0:1]),
                   waits=[evf])
        p.op("vector", lambda e: e.tensor_scalar(
            out=GDS[:], in0=GDS[:], scalar1=STT[:, 52:53], scalar2=None, op0=ALU.mult),
            waits=[evf] + ev_gds)

        p.barrier()


        if PHASES <= 5:
            with nc.Block() as block:
                p.finalize(block)
            return nc
        WT = [W[:, 0:4096], W[:, 4096:8192]]
        HST = [E[:, i * 128:(i + 1) * 128].bitcast(BF16) for i in range(8)]
        ev_wt = [None, None]
        ev_wfree = [None, None]

        def load_tile(col0, buf):
            evs = []
            for k in range(16):
                evs.append(p.dma("gpsimd", lambda e, k=k, col0=col0, buf=buf: e.dma_start(
                    out=WT[buf][:, k * WCOL:(k + 1) * WCOL],
                    in_=wiT[k * 128:(k + 1) * 128, col0:col0 + WCOL]),
                    waits=[ev_wfree[buf]]))
            return evs[-1]

        ev_wt[0] = load_tile(4096, 0)
        ev_evac3 = [None, None, None]
        ev_hst_free = [None, None]
        ev_fchain = None
        for j in range(16):
            buf = j % 2
            if j < 15:
                ev_wt[(j + 1) % 2] = load_tile(4096 + (j + 1) * WCOL, (j + 1) % 2)
            last_mm = None
            for i in range(16):
                halo = i < 8
                tb = i % 8
                ps = PS[i % 3]
                builds = [lambda e, i=i, k=k, ps=ps, buf=buf: e.matmul(
                    ps[:, 0:WCOL], lhs_un(i, k), WT[buf][:, k * WCOL:(k + 1) * WCOL],
                    start=(k == 0), stop=(k == 15)) for k in range(16)]
                evm = p.mmg(builds, waits=[ev_wt[buf], ev_evac3[i % 3]])
                last_mm = evm
                if halo and not HALO_F:
                    ev_evac3[i % 3] = evm
                elif halo:
                    hb = HST[tb]
                    exx = p.op("vector", lambda e, ps=ps, hb=hb, tb=tb, j=j: e.tensor_tensor(
                        out=hb[:].rearrange("p (h q) -> p h q", h=4),
                        in0=ps[:, 0:WCOL].rearrange("p (h q) -> p h q", h=4),
                        in1=GDS[:, tb * 64 + j * 4: tb * 64 + (j + 1) * 4, None]
                            .broadcast_to([128, 4, 64]),
                        op=ALU.mult), waits=[evm, ev_hst_free[0]])
                    ev_exx = exx
                    ev_evac3[i % 3] = exx
                    if tb == 7 and HALO_F != "noF":
                        fb = []
                        for tb2 in range(8):
                            fb.append(lambda e, tb2=tb2: e.matmul(
                                PSB[:, 0:256],
                                Bh[:, tb2 * 128:(tb2 + 1) * 128],
                                HST[tb2][:, :],
                                start=(tb2 == 0), stop=(tb2 == 7)))
                        ev_hst_free[0] = p.mmg(fb, waits=[p.ev("dve"), ev_fchain])
                else:
                    ecp = p.op("vector", lambda e, ps=ps, tb=tb, j=j: e.tensor_copy(
                        B[:, tb * 4096 + j * WCOL: tb * 4096 + (j + 1) * WCOL],
                        ps[:, 0:WCOL]), waits=[evm])
                    ev_evac3[i % 3] = ecp
            ev_wfree[buf] = last_mm
            if HALO_F is True:
                ev_fchain = p.op("vector", lambda e, j=j: e.tensor_copy(
                    NSS[:, j * 256:(j + 1) * 256], PSB[:, 0:256]),
                    waits=[ev_hst_free[0]])

        p.barrier()
        if DEBUG == 1:
            p.dma("sync", lambda e: e.dma_start(out=d_nss, in_=NSS[:]))
            p.op("vector", lambda e: e.tensor_copy(E[:, 0:256], DSC[:].bitcast(F32)),
                 waits=[p.ev("qs")])
            p.op("vector", lambda e: e.tensor_copy(E[:, 256:512], PST[:].bitcast(F32)))
            p.op("vector", lambda e: e.tensor_copy(E[:, 512:1024], TREP[:].bitcast(F32)))
            p.op("vector", lambda e: e.tensor_copy(E[:, 1024:1280], GDS[:].bitcast(F32)))
            p.dma("sync", lambda e: e.dma_start(out=d_sc, in_=E[:]), waits=[p.ev("dve")])
            p.barrier()

        if PHASES <= 6:
            with nc.Block() as block:
                p.finalize(block)
            return nc
        SVS = [E[:, 0:128].bitcast(BF16), E[:, 128:256].bitcast(BF16)]
        ev_wt = [None, None]
        ev_wfree = [None, None]
        ev_wt[0] = load_tile(0, 0)
        ev_sv = [None, None]
        ev_evac3 = [None, None, None]
        for j in range(16):
            buf = j % 2
            if j < 15:
                ev_wt[(j + 1) % 2] = load_tile((j + 1) * WCOL, (j + 1) % 2)
            last_mm = None
            for tb in range(8):
                i = 8 + tb
                ps = PS[tb % 3]
                sv = SVS[tb % 2]
                builds = [lambda e, i=i, k=k, ps=ps, buf=buf: e.matmul(
                    ps[:, 0:WCOL], lhs_un(i, k), WT[buf][:, k * WCOL:(k + 1) * WCOL],
                    start=(k == 0), stop=(k == 15)) for k in range(16)]
                evm = p.mmg(builds, waits=[ev_wt[buf], ev_evac3[tb % 3]])
                last_mm = evm
                esi = p.op("scalar", lambda e, ps=ps, sv=sv: e.activation(
                    out=sv, in_=ps[:, 0:WCOL], func=AF.Silu), waits=[evm, ev_sv[tb % 2]])
                ev_evac3[tb % 3] = esi
                ev_sv[tb % 2] = p.dma("sync", lambda e, tb=tb, j=j, sv=sv: e.dma_start(
                    out=d_svz[tb * 128:(tb + 1) * 128, j * WCOL:(j + 1) * WCOL], in_=sv),
                    waits=[esi])
            ev_wfree[buf] = last_mm

        for tb in range(NTB):
            p.dma("sync", lambda e, tb=tb: e.dma_start(
                out=d_bc[tb * 128:(tb + 1) * 128, 0:128], in_=Bn[:, tb * 128:(tb + 1) * 128]))
            p.dma("sync", lambda e, tb=tb: e.dma_start(
                out=d_bc[tb * 128:(tb + 1) * 128, 128:256], in_=Cn[:, tb * 128:(tb + 1) * 128]))
        ev_bcw = p.ev("qs")
        p.dma("scalar", lambda e: e.dma_start(out=BFM[:], in_=d_bc[:, 0:128], transpose=True),
              waits=[ev_bcw])
        p.dma("scalar", lambda e: e.dma_start(out=CFM[:], in_=d_bc[:, 128:256], transpose=True),
              waits=[ev_bcw])
        ev_bcr = p.ev("qa")

        prev_cbt = None
        for z in range(NCH):
            par = z % 2
            evm = p.mmg([lambda e, z=z, par=par: e.matmul(
                PS[0][par * 64:(par + 1) * 64, 0:64], BFM[:, z * 64:(z + 1) * 64],
                CFM[:, z * 64:(z + 1) * 64], start=True, stop=True,
                tile_position=(0, par * 64))], waits=[ev_bcr, prev_cbt, p.ev("act")])
            prev_cbt = p.op("vector", lambda e, z=z, par=par: e.tensor_mul(
                CBT[par * 64:(par + 1) * 64, (z // 2) * 64:(z // 2 + 1) * 64],
                PS[0][par * 64:(par + 1) * 64, 0:64],
                TRILM[par * 64:(par + 1) * 64, :]), waits=[evm])
        for k in range(16):
            p.dma("scalar", lambda e, k=k: e.dma_start(
                out=A[:, k * 1024:(k + 1) * 1024], in_=d_xn[:, k * 128:(k + 1) * 128],
                transpose=True), waits=[p.ev("pe")])
        p.barrier()


        if PHASES <= 7:
            with nc.Block() as block:
                p.finalize(block)
            return nc
        XP = C[:, 0:4096]
        XPD = C[:, 4096:8192]
        SPZ = C[:, 8192:12288]
        NSB = C[:, 12288:16384]
        YG = W[:, 0:4096]
        SVZ = [E[:, 0:2048].bitcast(BF16), E[:, 2048:4096].bitcast(BF16)]
        ev_svld = [None, None]
        ev_svld[0] = p.dma("sync", lambda e: e.dma_start(out=SVZ[0][:], in_=d_svz[0:128, :]))
        ev_pair = [None, None, None]
        ev_sadd = None
        ev_smm = None
        ev_ymm = None
        ev_yev = [None, None]
        ev_ygdma = [None, None]
        for z in range(NCH):
            tb = z // 2
            par = z % 2
            if par == 0:
                if tb < 7:
                    ev_svld[(tb + 1) % 2] = p.dma("sync", lambda e, tb=tb: e.dma_start(
                        out=SVZ[(tb + 1) % 2][:], in_=d_svz[(tb + 1) * 128:(tb + 2) * 128, :]),
                        waits=[ev_pair[2]])
                exp_ = p.op("gpsimd", lambda e, tb=tb: e.tensor_tensor(
                    out=XP[:].rearrange("p (h q) -> p h q", h=NH),
                    in0=B[:, tb * 4096:(tb + 1) * 4096].rearrange("p (h q) -> p h q", h=NH),
                    in1=DSC[:, tb * 64:(tb + 1) * 64, None].broadcast_to([128, NH, HD]),
                    op=ALU.mult), waits=[ev_ymm, ev_smm])
                exd = p.op("gpsimd", lambda e: e.tensor_tensor(
                    out=XPD[:].rearrange("p (h q) -> p h q", h=NH),
                    in0=XP[:].rearrange("p (h q) -> p h q", h=NH),
                    in1=DD[:, :, None].broadcast_to([128, NH, HD]),
                    op=ALU.mult), waits=[exp_, ev_dd, ev_ymm])
                esp = p.op("gpsimd", lambda e, tb=tb: e.tensor_tensor(
                    out=SPZ[:].rearrange("p (h q) -> p h q", h=NH),
                    in0=SVZ[tb % 2][:].rearrange("p (h q) -> p h q", h=NH),
                    in1=PST[:, tb * 64:(tb + 1) * 64, None].broadcast_to([128, NH, HD]),
                    op=ALU.mult), waits=[ev_svld[tb % 2], ev_yev[0], ev_yev[1]])
                ev_pair = [exp_, exd, esp]
            emul = p.op("gpsimd", lambda e, z=z: e.tensor_tensor(
                out=NSS[:].rearrange("p (h q) -> p h q", h=NH),
                in0=NSS[:].rearrange("p (h q) -> p h q", h=NH),
                in1=TREP[:, z * 64:(z + 1) * 64, None].broadcast_to([128, NH, HD]),
                op=ALU.mult), waits=[ev_sadd, ev_ymm])
            ev_nsb = p.op("vector", lambda e: e.tensor_copy(NSB[:], NSS[:]), waits=[emul])
            for hh in range(2):
                b1 = []
                for q in range(4):
                    cs = hh * 2048 + q * 512
                    b1.append(lambda e, par=par, z=z, cs=cs, q=q: e.matmul(
                        PSB[par * 64:(par + 1) * 64, q * 512:(q + 1) * 512],
                        CBT[par * 64:(par + 1) * 64, (z // 2) * 64:(z // 2 + 1) * 64],
                        XP[par * 64:(par + 1) * 64, cs:cs + 512],
                        start=True, stop=False, tile_position=(par * 64, par * 64)))
                for q in range(4):
                    cs = hh * 2048 + q * 512
                    b1.append(lambda e, par=par, cs=cs, q=q: e.matmul(
                        PSB[par * 64:(par + 1) * 64, q * 512:(q + 1) * 512],
                        IDB[par * 64:(par + 1) * 64, par * 64:(par + 1) * 64],
                        XPD[par * 64:(par + 1) * 64, cs:cs + 512],
                        start=False, stop=False, tile_position=(par * 64, par * 64)))
                p.mmg(b1, waits=[ev_pair[0], ev_pair[1], ev_yev[0], ev_yev[1]])
                b2 = []
                for q in range(4):
                    cs = hh * 2048 + q * 512
                    b2.append(lambda e, par=par, z=z, cs=cs, q=q: e.matmul(
                        PSB[par * 64:(par + 1) * 64, q * 512:(q + 1) * 512],
                        CFM[:, z * 64:(z + 1) * 64], NSB[:, cs:cs + 512],
                        start=False, stop=True, tile_position=(0, par * 64)))
                ev_ymm = p.mmg(b2, waits=[ev_nsb])
                ev_yev[hh] = p.op("vector", lambda e, par=par, hh=hh: e.tensor_mul(
                    YG[par * 64:(par + 1) * 64, hh * 2048:(hh + 1) * 2048],
                    PSB[par * 64:(par + 1) * 64, :],
                    SPZ[par * 64:(par + 1) * 64, hh * 2048:(hh + 1) * 2048]),
                    waits=[ev_ymm, ev_pair[2], ev_ygdma[par]])
            ev_ygdma[par] = p.dma("sync", lambda e, z=z, par=par: e.dma_start(
                out=d_yg[z * 64:(z + 1) * 64, :], in_=YG[par * 64:(par + 1) * 64, :]),
                waits=[ev_yev[0], ev_yev[1]])
            for wave in range(2):
                smms = []
                for q in range(4):
                    cq = wave * 2048 + q * 512
                    smms.append(lambda e, par=par, tb=tb, cq=cq, q=q: e.matmul(
                        PS[q][:, :], Bn[par * 64:(par + 1) * 64, tb * 128:(tb + 1) * 128],
                        XP[par * 64:(par + 1) * 64, cq:cq + 512],
                        start=True, stop=True, tile_position=(par * 64, 0)))
                ev_smm = p.mmg(smms, waits=[ev_sadd, ev_pair[0]])
                for q in range(4):
                    cq = wave * 2048 + q * 512
                    ev_sadd = p.op("vector", lambda e, cq=cq, q=q: e.scalar_tensor_tensor(
                        out=NSS[:, cq:cq + 512], in0=PS[q][:, :], scalar=1.0,
                        in1=NSS[:, cq:cq + 512], op0=ALU.mult, op1=ALU.add),
                        waits=[ev_smm])

        p.barrier()


        if PHASES <= 8:
            with nc.Block() as block:
                p.finalize(block)
            return nc
        W6 = E[:, 0:512].bitcast(BF16)
        HE = E[:, 512:1024].bitcast(BF16)
        HW = E[:, 1024:1536].bitcast(BF16)
        H1 = E[:, 1536:2048].bitcast(BF16)
        BS6 = E[:, 2048:3072].bitcast(BF16)
        BR6s = E[:, 3072:4096].bitcast(BF16)
        G6 = TMP[:, 0:48]
        GX = W[:, 4096:4608].bitcast(F32)
        WRT = W[:, 4608:4704]
        AST = W[:, 4704:5216]
        ART = W[:, 5216:6752]
        ev_wr = p.dma("gpsimd", lambda e: e.dma_start(
            out=WRT[:].rearrange("p (k n) -> p k n", k=16),
            in_=wr.rearrange("(k p) n -> p k n", k=16)))
        ev_as = p.dma("gpsimd", lambda e: e.dma_start(
            out=AST[:].rearrange("p (k n) -> p k n", k=16),
            in_=as_.rearrange("(k p) n -> p k n", k=16)))
        ev_ar = p.dma("gpsimd", lambda e: e.dma_start(
            out=ART[:].rearrange("p (k n) -> p k n", k=16),
            in_=arT.rearrange("(k p) n -> p k n", k=16)))
        ev_bs6 = p.dma("gpsimd", lambda e: e.dma_start(out=BS6[0:16, :], in_=bs))
        ev_br6 = p.dma("gpsimd", lambda e: e.dma_start(out=BR6s[0:96, :], in_=br6))
        ev_l1 = ldrow("brt", 6)
        evb1 = p.mmg([lambda e: e.matmul(PS[3][:, 0:6], ONES[0:1, :], ROWB[0:1, 0:6],
                                         start=True, stop=True)], waits=[ev_l1])
        rowst["busy"] = evb1
        ev_brtb = p.op("vector", lambda e: e.tensor_copy(TMP[:, 48:54], PS[3][:, 0:6]),
                       waits=[evb1])
        ev_l2 = ldrow("tie", 36)
        evb2 = p.mmg([lambda e: e.matmul(PS[3][:, 64:100], ONES[0:1, :], ROWB[0:1, 0:36],
                                         start=True, stop=True)], waits=[ev_brtb, ev_l2])
        rowst["busy"] = evb2
        ev_tieb = p.op("vector", lambda e: e.tensor_copy(TMP[:, 64:100], PS[3][:, 64:100]),
                       waits=[evb2])
        if G_PARTS <= 1:
            with nc.Block() as block:
                p.finalize(block)
            return nc
        ev_w6w = [None, None]
        ev_psfree = [None, None]
        for tb in range(NTB):
            pr = tb % 2
            ps6 = PS[pr]
            g6 = TMP[:, pr * 24:pr * 24 + 24]
            gx = GX[:, pr * 100:pr * 100 + 100]
            stc = STT[:, pr * 4:pr * 4 + 4]
            w6s = W6[0:128, pr * 6:pr * 6 + 6]
            builds = [lambda e, tb=tb, k=k, ps6=ps6: e.matmul(
                ps6[:, 0:6], A[:, k * 1024 + tb * 128:k * 1024 + (tb + 1) * 128],
                WRT[:, k * 6:(k + 1) * 6], start=(k == 0), stop=(k == 15))
                for k in range(16)]
            evm = p.mmg(builds, waits=[ev_wr, ev_psfree[pr]])
            el = p.op("vector", lambda e, g6=g6, ps6=ps6: e.tensor_add(
                g6[:, 0:6], ps6[:, 0:6], TMP[:, 48:54]), waits=[evm, ev_brtb])
            ev_psfree[pr] = el
            em = p.op("vector", lambda e, g6=g6, stc=stc: e.reduce_max(
                stc[:, 0:1], g6[:, 0:6], axis=AX.X), waits=[el])
            em2 = p.op("vector", lambda e, stc=stc: e.tensor_scalar_mul(
                stc[:, 1:2], stc[:, 0:1], -1.0), waits=[em])
            ee = p.op("scalar", lambda e, g6=g6, stc=stc: e.activation(
                out=g6[:, 8:14], in_=g6[:, 0:6], func=AF.Exp, bias=stc[:, 1:2]),
                waits=[em2])
            es = p.op("vector", lambda e, g6=g6, stc=stc: e.reduce_sum(
                stc[:, 2:3], g6[:, 8:14], axis=AX.X), waits=[ee])
            er = p.op("vector", lambda e, stc=stc: e.reciprocal(stc[:, 3:4], stc[:, 2:3]),
                      waits=[es])
            eg = p.op("vector", lambda e, g6=g6, stc=stc: e.tensor_scalar(
                out=g6[:, 16:22], in0=g6[:, 8:14], scalar1=stc[:, 3:4], scalar2=None,
                op0=ALU.mult), waits=[er])
            e1 = p.op("vector", lambda e, g6=g6, gx=gx: e.tensor_tensor(
                out=gx[:, 0:36].rearrange("p (a b) -> p a b", a=6),
                in0=g6[:, None, 16:22].broadcast_to([128, 6, 6]),
                in1=g6[:, 16:22, None].broadcast_to([128, 6, 6]),
                op=ALU.is_gt), waits=[eg])
            e2 = p.op("vector", lambda e, g6=g6, gx=gx: e.tensor_tensor(
                out=gx[:, 36:72].rearrange("p (a b) -> p a b", a=6),
                in0=g6[:, None, 16:22].broadcast_to([128, 6, 6]),
                in1=g6[:, 16:22, None].broadcast_to([128, 6, 6]),
                op=ALU.is_equal), waits=[e1])
            e3 = p.op("vector", lambda e, gx=gx: e.tensor_mul(
                gx[:, 36:72], gx[:, 36:72], TMP[:, 64:100]), waits=[e2, ev_tieb])
            e4 = p.op("vector", lambda e, gx=gx: e.tensor_add(
                gx[:, 0:36], gx[:, 0:36], gx[:, 36:72]), waits=[e3])
            e5 = p.op("vector", lambda e, gx=gx: e.tensor_reduce(
                out=gx[:, 72:78, None], in_=gx[:, 0:36].rearrange("p (a b) -> p a b", a=6),
                axis=AX.X, op=ALU.add), waits=[e4])
            e6 = p.op("vector", lambda e, gx=gx: e.tensor_single_scalar(
                gx[:, 78:84], gx[:, 72:78], 2.0, ALU.is_lt), waits=[e5])
            e7 = p.op("vector", lambda e, g6=g6, gx=gx: e.tensor_mul(
                gx[:, 84:90], g6[:, 16:22], gx[:, 78:84]), waits=[e6])
            e8 = p.op("vector", lambda e, gx=gx, w6s=w6s: e.tensor_copy(
                w6s, gx[:, 84:90]), waits=[e7, ev_w6w[pr]])
            ev_w6w[pr] = p.dma("sync", lambda e, tb=tb, w6s=w6s: e.dma_start(
                out=d_w6[tb * 128:(tb + 1) * 128, 0:6], in_=w6s), waits=[e8])
        p.barrier()
        if G_PARTS <= 2:
            with nc.Block() as block:
                p.finalize(block)
            return nc
        ev_w6t = p.dma("scalar", lambda e: e.dma_start(out=W6[:], in_=d_w6, transpose=True))
        if G_PARTS <= 3:
            with nc.Block() as block:
                p.finalize(block)
            return nc
        for h2 in range(2):
            builds = [lambda e, k=k, h2=h2: e.matmul(
                PSB[0:96, h2 * 512:(h2 + 1) * 512], ART[:, k * 96:(k + 1) * 96],
                A[:, k * 1024 + h2 * 512:k * 1024 + (h2 + 1) * 512],
                start=(k == 0), stop=(k == 15)) for k in range(16)]
            evh = p.mmg(builds, waits=[ev_ar])
            p.op("vector", lambda e, h2=h2: e.tensor_copy(
                HE[0:96, h2 * 512:(h2 + 1) * 512], PSB[0:96, h2 * 512:(h2 + 1) * 512]),
                waits=[evh])
        if G_PARTS <= 4:
            with nc.Block() as block:
                p.finalize(block)
            return nc
        prev_h1 = None
        for h2 in range(2):
            builds = [lambda e, k=k, h2=h2: e.matmul(
                PS[1][0:32, 0:512], AST[:, k * 32:(k + 1) * 32],
                A[:, k * 1024 + h2 * 512:k * 1024 + (h2 + 1) * 512],
                start=(k == 0), stop=(k == 15)) for k in range(16)]
            evh = p.mmg(builds, waits=[ev_as, prev_h1])
            prev_h1 = p.op("vector", lambda e, h2=h2: e.tensor_copy(
                H1[0:16, h2 * 512:(h2 + 1) * 512], PS[1][0:16, 0:512]), waits=[evh])
        if G_PARTS <= 5:
            with nc.Block() as block:
                p.finalize(block)
            return nc
        prev_hw = None
        for h2 in range(2):
            evw = p.mmg([lambda e, h2=h2: e.matmul(
                PS[2][0:96, 0:512], EXP6[0:6, :], W6[0:6, h2 * 512:(h2 + 1) * 512],
                start=True, stop=True)], waits=[ev_w6t, prev_hw])
            prev_hw = p.op("vector", lambda e, h2=h2: e.tensor_mul(
                HW[0:96, h2 * 512:(h2 + 1) * 512], HE[0:96, h2 * 512:(h2 + 1) * 512],
                PS[2][0:96, 0:512]), waits=[evw, p.ev("dve")])
        p.barrier()


        if PHASES <= 9:
            with nc.Block() as block:
                p.finalize(block)
            return nc
        for ec in range(32):
            p.dma("scalar", lambda e, ec=ec: e.dma_start(
                out=B[:, ec * 1024:(ec + 1) * 1024], in_=d_yg[:, ec * 128:(ec + 1) * 128],
                transpose=True))
        ev_bbT = []
        prevb = None
        for f in range(16):
            evl = ldrow("bb", 128, off2=f * 128)
            evm = p.mmg([lambda e, f=f: e.matmul(
                PS[3][:, 160 + f:161 + f], ROWB[0:1, 0:128],
                ONES[0:1, 0:1], start=True, stop=True)], waits=[prevb, evl])
            rowst["busy"] = evm
            prevb = p.op("vector", lambda e, f=f: e.tensor_copy(
                TMP[:, 100 + f:101 + f], PS[3][:, 160 + f:161 + f]), waits=[evm])
            ev_bbT.append(prevb)
        ev_wt = [None, None]
        ev_wfree = [None, None]

        def load_wb(j, buf):
            evs = []
            for k in range(16):
                evs.append(p.dma("gpsimd", lambda e, k=k, j=j, buf=buf: e.dma_start(
                    out=WT[buf][:, k * WCOL:(k + 1) * WCOL],
                    in_=wb[k * 128:(k + 1) * 128, j * WCOL:(j + 1) * WCOL]),
                    waits=[ev_wfree[buf]]))
            return evs[-1]

        ev_wt[0] = load_wb(0, 0)
        ev_gev = [None, None, None]
        for j in range(8):
            buf = j % 2
            if j < 7:
                ev_wt[(j + 1) % 2] = load_wb(j + 1, (j + 1) % 2)
            last = None
            for fc2 in range(2):
                f = j * 2 + fc2
                for tt in range(2):
                    ps = PS[(f * 2 + tt) % 3]
                    builds = [lambda e, k=k, fc2=fc2, tt=tt, ps=ps, buf=buf: e.matmul(
                        ps[:, :], WT[buf][:, k * WCOL + fc2 * 128:k * WCOL + (fc2 + 1) * 128],
                        A[:, k * 1024 + tt * 512:k * 1024 + (tt + 1) * 512],
                        start=(k == 0), stop=False) for k in range(16)]
                    builds.append(lambda e, f=f, tt=tt, ps=ps: e.matmul(
                        ps[:, :], BS6[0:16, f * 128:(f + 1) * 128],
                        H1[0:16, tt * 512:(tt + 1) * 512], start=False, stop=False))
                    builds.append(lambda e, f=f, tt=tt, ps=ps: e.matmul(
                        ps[:, :], BR6s[0:96, f * 128:(f + 1) * 128],
                        HW[0:96, tt * 512:(tt + 1) * 512], start=False, stop=True))
                    evm = p.mmg(builds, waits=[ev_wt[buf], ev_bs6, ev_br6,
                                               ev_gev[(f * 2 + tt) % 3]])
                    last = evm
                    ev_gev[(f * 2 + tt) % 3] = p.op(
                        "scalar", lambda e, f=f, tt=tt, ps=ps: e.activation(
                            out=C[:, f * 1024 + tt * 512:f * 1024 + (tt + 1) * 512],
                            in_=ps[:, :], func=AF.Silu, bias=TMP[:, 100 + f:101 + f]),
                        waits=[evm, ev_bbT[f]])
            ev_wfree[buf] = last
        p.barrier()


        if PHASES <= 10:
            with nc.Block() as block:
                p.finalize(block)
            return nc
        UTS = [E[:, 0:1024], E[:, 1024:2048]]
        OUS = [E[:, 2048:2560], E[:, 2560:3072]]
        ev_wt = [None, None]
        ev_wfree = [None, None]

        def load_wo(dchunk, buf):
            evs = []
            for k in range(32):
                evs.append(p.dma("gpsimd", lambda e, k=k, dchunk=dchunk, buf=buf: e.dma_start(
                    out=WT[buf][:, k * 128:(k + 1) * 128],
                    in_=woT[k * 128:(k + 1) * 128, dchunk * 128:(dchunk + 1) * 128]),
                    waits=[ev_wfree[buf]]))
            return evs[-1]

        ev_wt[0] = load_wo(0, 0)
        ev_ut = [None, None]
        ev_ut_rd = [None, None]
        ev_ut[0] = p.dma("sync", lambda e: e.dma_start(out=UTS[0][:], in_=uT[0:128, :]))
        ev_ep = [None, None, None]
        ev_ost = [None, None]
        for dc in range(16):
            buf = dc % 2
            if dc < 15:
                ev_wt[(dc + 1) % 2] = load_wo(dc + 1, (dc + 1) % 2)
                ev_ut[(dc + 1) % 2] = p.dma("sync", lambda e, dc=dc: e.dma_start(
                    out=UTS[(dc + 1) % 2][:], in_=uT[(dc + 1) * 128:(dc + 2) * 128, :]),
                    waits=[ev_ut_rd[(dc + 1) % 2]])
            for tt in range(2):
                ps = PS[(dc * 2 + tt) % 3]
                builds = [lambda e, k=k, tt=tt, ps=ps, buf=buf: e.matmul(
                    ps[:, :], WT[buf][:, k * 128:(k + 1) * 128],
                    B[:, k * 1024 + tt * 512:k * 1024 + (tt + 1) * 512],
                    start=(k == 0), stop=(k == 31)) for k in range(32)]
                evm = p.mmg(builds, waits=[ev_wt[buf], ev_ep[(dc * 2 + tt) % 3]])
                ob = OUS[tt]
                e1 = p.op("vector", lambda e, dc=dc, tt=tt, ps=ps, ob=ob: e.tensor_add(
                    ob, ps[:, :], A[:, dc * 1024 + tt * 512:dc * 1024 + (tt + 1) * 512]),
                    waits=[evm, ev_ost[tt]])
                e2 = p.op("vector", lambda e, dc=dc, tt=tt, ob=ob: e.tensor_mul(
                    ob, ob, C[:, dc * 1024 + tt * 512:dc * 1024 + (tt + 1) * 512]),
                    waits=[e1])
                e3 = p.op("vector", lambda e, dc=dc, tt=tt, ob=ob, buf=buf: e.tensor_add(
                    ob, ob, UTS[buf][:, tt * 512:(tt + 1) * 512]),
                    waits=[e2, ev_ut[buf]])
                ev_ut_rd[buf] = e3
                ev_ep[(dc * 2 + tt) % 3] = e3
                ev_ost[tt] = p.dma("sync", lambda e, dc=dc, tt=tt, ob=ob: e.dma_start(
                    out=outT[dc * 128:(dc + 1) * 128, tt * 512:(tt + 1) * 512], in_=ob),
                    waits=[e3])
        p.barrier()

        with nc.Block() as block:
            p.finalize(block)

    return nc


def _pack_rows(I):
    rows = np.zeros((1, 8192), np.float32)

    def put(off, v):
        v = np.asarray(v, np.float32).reshape(-1)
        rows[0, off:off + v.size] = v

    put(0, I["L0_attn_ln_w"]); put(2048, I["L0_attn_ln_b"]); put(4096, I["L0_ffn_b_base"])
    put(6144, I["L0_attn_bnorm_w"]); put(6272, I["L0_attn_cnorm_w"])
    put(6400, I["L0_attn_D"]); put(6464, I["L0_attn_dt_bias"]); put(6528, I["L0_attn_A_log"])
    put(6592, I["L0_ffn_b_router"])
    tie = (np.arange(6)[None, :] < np.arange(6)[:, None]).astype(np.float32)
    put(6598, tie.reshape(-1))
    return rows


def _consts():
    ones = np.ones((1, 128), np.float32)
    idb = np.eye(128, dtype=np.float32)
    jj, ii = np.meshgrid(np.arange(128), np.arange(128), indexing="ij")
    ut2 = ((jj <= ii) & ((jj // 64) == (ii // 64))).astype(np.float32)
    ut1 = (jj <= ii).astype(np.float32)
    pick = np.zeros((128, 128), np.float32)
    pick[63, 0:64] = 1.0
    pick[127, 64:128] = 1.0
    picke = np.zeros((128, 128), np.float32)
    picke[63, :] = 1.0
    picko = np.zeros((128, 128), np.float32)
    picko[127, :] = 1.0
    m = (np.arange(64)[None, :] >= np.arange(64)[:, None]).astype(np.float32)
    tril = np.concatenate([m, m], axis=0)
    exp6 = np.zeros((6, 96), np.float32)
    for e6 in range(6):
        exp6[e6, e6 * 16:(e6 + 1) * 16] = 1.0
    return ones, idb, ut2, ut1, pick, picke, picko, tril, exp6


def kernel(**inputs):
    I = inputs
    u = np.ascontiguousarray(np.asarray(I["L0_attn_u"], np.float32)[0])
    wiT = np.ascontiguousarray(np.asarray(I["L0_attn_in_proj_w"], np.float32).T)
    woT = np.ascontiguousarray(np.asarray(I["L0_attn_out_proj_w"], np.float32).T)
    wb = np.ascontiguousarray(np.asarray(I["L0_ffn_W_base"], np.float32))
    as_ = np.ascontiguousarray(np.concatenate([
        np.asarray(I["L0_ffn_A_s"], np.float32),
        np.zeros((D, 16), np.float32)], axis=1))
    bs = np.ascontiguousarray(np.asarray(I["L0_ffn_B_s"], np.float32))
    arT = np.ascontiguousarray(
        np.asarray(I["L0_ffn_A_r"], np.float32).transpose(1, 0, 2).reshape(D, 96))
    br6 = np.ascontiguousarray(np.asarray(I["L0_ffn_B_r"], np.float32).reshape(96, D))
    wr = np.ascontiguousarray(np.asarray(I["L0_ffn_W_router"], np.float32))
    rows = _pack_rows(I)
    ones, idb, ut2, ut1, pick, picke, picko, tril, exp6 = _consts()

    nc = build_nc()
    in_maps = []
    for c in range(NCORE):
        sl = slice(c * T, (c + 1) * T)
        u2 = np.zeros((2 * T, D), np.float32)
        if c > 0:
            u2[0:T] = u[(c - 1) * T:c * T]
        u2[T:2 * T] = u[sl]
        r = rows.copy()
        r[0, 6634] = 0.0 if c == 0 else 1.0
        in_maps.append(dict(
            u2=u2, uT=np.ascontiguousarray(u[sl].T), wiT=wiT, woT=woT, wb=wb,
            as_=as_, bs=bs, arT=arT, br6=br6, wr=wr, vrows=r, c_ones=ones,
            c_idb=idb, c_ut2=ut2, c_ut1=ut1, c_pick=pick, c_picke=picke,
            c_picko=picko, c_tril=tril, c_exp6=exp6,
        ))
    res = run_bass_kernel_spmd(nc, in_maps, core_ids=list(range(NCORE)))
    global _LAST_RES
    _LAST_RES = res
    out = np.empty((1, 8192, D), np.float32)
    for c in range(NCORE):
        out[0, c * T:(c + 1) * T, :] = np.asarray(res.results[c]["outT"]).T
    return out
